# version 24
# baseline (speedup 1.0000x reference)
import numpy as np
import ml_dtypes
from contextlib import ExitStack

import concourse.bass as bass
import concourse.mybir as mybir
from concourse.bass_utils import run_bass_kernel_spmd

F32 = mybir.dt.float32
BF16 = mybir.dt.bfloat16
AF = mybir.ActivationFunctionType
ALU = mybir.AluOpType
AX = mybir.AxisListType
BF = ml_dtypes.bfloat16

S = 2048
HD = 2048
NTS = 16
KH = 16
NBLK = 4
EPS_LN = 1e-5
EPS_RMS = 1e-6


class Plan:
    ENGINES = ("sync", "gpsimd", "tensor", "vector", "scalar")

    def __init__(self):
        self.items = {e: [] for e in self.ENGINES}

    def add(self, eng, fn):
        self.items[eng].append(fn)

    def wait(self, eng, sem, val):
        if val > 0:
            self.items[eng].append(lambda e, s=sem, v=val: e.wait_ge(s, v))

    def ins(self, eng, fn, inc=None):
        def g(e, fn=fn, inc=inc):
            i = fn(e)
            if inc is not None:
                i.then_inc(inc[0], inc[1])
        self.items[eng].append(g)

    def splice(self, other, spread=(), start_frac=0.0):
        for eng in self.ENGINES:
            oi = other.items[eng]
            if not oi:
                continue
            if eng in spread:
                mine = self.items[eng]
                k0 = int(len(mine) * start_frac)
                tail = mine[k0:]
                merged = []
                n_t, n_o = len(tail), len(oi)
                ti = oi_i = 0
                for step in range(n_t + n_o):
                    if ti * n_o <= oi_i * n_t and ti < n_t:
                        merged.append(tail[ti]); ti += 1
                    elif oi_i < n_o:
                        merged.append(oi[oi_i]); oi_i += 1
                    else:
                        merged.append(tail[ti]); ti += 1
                self.items[eng] = mine[:k0] + merged
            else:
                self.items[eng].extend(oi)

    def emit(self, nc):
        with nc.Block() as block:
            for eng in self.ENGINES:
                fns = self.items[eng]
                if not fns:
                    continue
                def body(e, fns=fns):
                    for f in fns:
                        f(e)
                getattr(block, eng)(body)



class LNEmitter:

    def __init__(self, nc, P, st, sp, d, X_ap, xnT, pref):
        sb = lambda n, sh, dt=F32: st.enter_context(nc.sbuf_tensor(pref + n, sh, dt))
        ps = lambda n, sh, dt=F32: st.enter_context(nc.psum_tensor(pref + n, sh, dt))
        sem = lambda n: sp.enter_context(nc.semaphore(pref + n))
        self.nc, self.P, self.d = nc, P, d
        self.X_ap, self.xnT = X_ap, xnT
        self.xo = [sb(f"xo{i}", [128, 2048]) for i in range(2)]
        self.stats = sb("stats", [128, 24])
        self.mv = sb("mv", [128, 2])
        self.lnt = sb("lnt", [128, 2])
        self.xnb = [sb("xnb0", [128, 2048], BF16)] * 2
        self.pst = [ps(f"pst{i}", [128, 512], BF16) for i in range(2)]
        self.s_x = [sem("sx0"), sem("sx1")]
        self.s_v, self.s_s, self.s_t, self.s_e = (
            sem("sv"), sem("ss"), sem("st"), sem("se"))
        self.cx = [0, 0]
        self.cv = self.cs = self.ct = self.ce = 0
        self.v_ts_done = {}
        self.grp = 0

    def e_count(self, ts):
        return 4 * (ts + 1)

    def emit(self, ts):
        P, d = self.P, self.d
        xo, stats, mv, lnt, xnb, pst = (self.xo, self.stats, self.mv,
                                        self.lnt, self.xnb, self.pst)
        s_x, s_v, s_s, s_t, s_e = self.s_x, self.s_v, self.s_s, self.s_t, self.s_e
        X_ap, xnT = self.X_ap, self.xnT
        identb = d["identb_sb"]
        sl = ts % 2
        if ts >= 2:
            P.wait("sync", s_v, self.v_ts_done[ts - 2])
        P.ins("sync", lambda e, sl=sl, ts=ts: e.dma_start(
            out=xo[sl][:, :], in_=X_ap[ts * 128:(ts + 1) * 128, :]), (s_x[sl], 16))
        self.cx[sl] += 16
        P.wait("vector", s_x[sl], self.cx[sl])
        for j in range(4):
            P.ins("vector", lambda e, sl=sl, j=j: e.bn_stats(
                stats[:, 6 * j:6 * j + 6], xo[sl][:, 512 * j:512 * (j + 1)]), (s_v, 1))
            self.cv += 1
        P.wait("vector", s_v, self.cv)
        P.ins("vector", lambda e: e.bn_aggr(mv[:, :], stats[:, :]), (s_v, 1))
        self.cv += 1
        P.wait("scalar", s_v, self.cv)
        P.ins("scalar", lambda e: e.activation(
            lnt[:, 0:1], mv[:, 1:2], AF.Ln, bias=d["eps_ln_col"]), (s_s, 1))
        self.cs += 1
        P.wait("scalar", s_s, self.cs)
        P.ins("scalar", lambda e: e.activation(
            lnt[:, 1:2], lnt[:, 0:1], AF.Exp, scale=-0.5), (s_s, 1))
        self.cs += 1
        P.wait("vector", s_s, self.cs)
        if ts >= 1:
            P.wait("vector", s_t, 4 * ts)
        P.ins("vector", lambda e, sl=sl: e.tensor_scalar(
            xnb[sl][:, :], xo[sl][:, :], mv[:, 0:1], lnt[:, 1:2],
            op0=ALU.subtract, op1=ALU.mult), (s_v, 1))
        self.cv += 1
        self.v_ts_done[ts] = self.cv
        P.wait("tensor", s_v, self.cv)
        for g in range(4):
            psl = self.grp % 2
            if self.grp >= 2:
                P.wait("tensor", s_e, self.grp - 1)
            for j in range(4):
                dt = 4 * g + j
                P.ins("tensor", lambda e, sl=sl, psl=psl, dt=dt, j=j: e.transpose(
                    pst[psl][:, j * 128:(j + 1) * 128],
                    xnb[sl][:, dt * 128:(dt + 1) * 128], identb[:, :]),
                    (s_t, 1) if j == 3 else None)
            self.ct += 1
            P.wait("scalar", s_t, self.ct)
            dst = bass.AP(xnT.tensor if hasattr(xnT, "tensor") else xnT,
                          (4 * g) * 2048 + ts * 128,
                          [[16 * 2048, 128], [2048, 4], [1, 128]])
            P.ins("scalar", lambda e, psl=psl, dst=dst: e.copy(dst, pst[psl][:, :]), (s_e, 1))
            self.ce += 1
            self.grp += 1



def plan_ln(nc, P, st, sp, d, X_ap, xnT, pref):
    sb = lambda n, sh, dt=F32: st.enter_context(nc.sbuf_tensor(pref + n, sh, dt))
    ps = lambda n, sh, dt=F32: st.enter_context(nc.psum_tensor(pref + n, sh, dt))
    sem = lambda n: sp.enter_context(nc.semaphore(pref + n))

    xo = [sb(f"xo{i}", [128, 2048]) for i in range(2)]
    stats = sb("stats", [128, 24])
    mv = sb("mv", [128, 2])
    lnt = sb("lnt", [128, 2])
    xnb = [sb(f"xnb{i}", [128, 2048], BF16) for i in range(2)]
    pst = [ps(f"pst{i}", [128, 512], BF16) for i in range(2)]
    identb = d["identb_sb"]

    s_x = [sem("sx0"), sem("sx1")]
    s_v, s_s, s_t, s_e = sem("sv"), sem("ss"), sem("st"), sem("se")
    cx = [0, 0]
    cv = cs = ct = ce = 0
    v_ts_done = {}
    grp = 0

    for ts in range(NTS):
        sl = ts % 2
        if ts >= 2:
            P.wait("sync", s_v, v_ts_done[ts - 2])
        P.ins("sync", lambda e, sl=sl, ts=ts: e.dma_start(
            out=xo[sl][:, :], in_=X_ap[ts * 128:(ts + 1) * 128, :]), (s_x[sl], 16))
        cx[sl] += 16
        P.wait("vector", s_x[sl], cx[sl])
        for j in range(4):
            P.ins("vector", lambda e, sl=sl, j=j: e.bn_stats(
                stats[:, 6 * j:6 * j + 6], xo[sl][:, 512 * j:512 * (j + 1)]), (s_v, 1))
            cv += 1
        P.wait("vector", s_v, cv)
        P.ins("vector", lambda e: e.bn_aggr(mv[:, :], stats[:, :]), (s_v, 1))
        cv += 1
        P.wait("scalar", s_v, cv)
        P.ins("scalar", lambda e: e.activation(
            lnt[:, 0:1], mv[:, 1:2], AF.Ln, bias=d["eps_ln_col"]), (s_s, 1))
        cs += 1
        P.wait("scalar", s_s, cs)
        P.ins("scalar", lambda e: e.activation(
            lnt[:, 1:2], lnt[:, 0:1], AF.Exp, scale=-0.5), (s_s, 1))
        cs += 1
        P.wait("vector", s_s, cs)
        if ts >= 2:
            P.wait("vector", s_t, 4 * (ts - 1))
        P.ins("vector", lambda e, sl=sl: e.tensor_scalar(
            xnb[sl][:, :], xo[sl][:, :], mv[:, 0:1], lnt[:, 1:2],
            op0=ALU.subtract, op1=ALU.mult), (s_v, 1))
        cv += 1
        v_ts_done[ts] = cv
        P.wait("tensor", s_v, cv)
        for g in range(4):
            psl = grp % 2
            if grp >= 2:
                P.wait("tensor", s_e, grp - 1)
            for j in range(4):
                dt = 4 * g + j
                last = (j == 3)
                P.ins("tensor", lambda e, sl=sl, psl=psl, dt=dt, j=j: e.transpose(
                    pst[psl][:, j * 128:(j + 1) * 128],
                    xnb[sl][:, dt * 128:(dt + 1) * 128], identb[:, :]),
                    (s_t, 1) if last else None)
            ct += 1
            P.wait("scalar", s_t, ct)
            dst = bass.AP(xnT.tensor if hasattr(xnT, "tensor") else xnT,
                          (4 * g) * 2048 + ts * 128,
                          [[16 * 2048, 128], [2048, 4], [1, 128]])
            P.ins("scalar", lambda e, psl=psl, dst=dst: e.copy(dst, pst[psl][:, :]), (s_e, 1))
            ce += 1
            grp += 1



def build_ffn(nc, d, sp, layer, app_idx, X_in, X_out, have_cache, extra=None, x_eng="gpsimd"):
    pref = f"f{layer}_{app_idx}_"
    NI = 44 if layer == 0 else 40
    with ExitStack() as st:
        xnT = st.enter_context(nc.sbuf_tensor(pref + "xnT", [128, 16 * 2048], BF16))

        with ExitStack() as st2:
            sb2 = lambda n, sh, dt=F32: st2.enter_context(nc.sbuf_tensor(pref + n, sh, dt))
            ps2 = lambda n, sh, dt=F32: st2.enter_context(nc.psum_tensor(pref + n, sh, dt))
            sem2 = lambda n: sp.enter_context(nc.semaphore(pref + n))
            P = Plan()
            ln = LNEmitter(nc, P, st2, sp, d, X_in, xnT[:, :], pref + "ln_")

            gwb = [sb2(f"gwb{i}", [128, 16 * 128], BF16) for i in range(2)]
            if layer == 0:
                uwb = [sb2(f"uwb{i}", [128, 16 * 128], BF16) for i in range(2)]
                gbuf = [sb2(f"gbuf{i}", [128, 512]) for i in range(2)]
                gsil = gbuf
                ubuf = [sb2(f"ubuf{i}", [128, 512]) for i in range(2)]
                gs_sb = sb2("gs", [128, NI])
                us_sb = sb2("us", [128, NI])
                ds_sb = sb2("ds", [128, 2048], BF16)
                t1b = [sb2(f"t1b{i}", [128, 256]) for i in range(2)]
            else:
                rbuf = [sb2(f"rbuf{i}", [128, 512]) for i in range(2)]
                cc_sb = sb2("cc", [128, 1])
            hdn = sb2("hdn", [128, NI * 512], BF16)
            dwb = [sb2(f"dwb{i}", [128, NI * 256], BF16) for i in range(2)]
            xo2 = [sb2(f"xo2{i}", [128, 256]) for i in range(2)]
            xnew = [sb2(f"xnew{i}", [128, 256]) for i in range(2)]

            pg = [ps2(f"pg{i}", [128, 512]) for i in range(2)]
            pu = [ps2(f"pu{i}", [128, 512]) for i in range(2)]
            s_wg = [sem2(f"swg{i}") for i in range(2)]
            s_wu = [sem2(f"swu{i}") for i in range(2)]
            s_wd = [sem2(f"swd{i}") for i in range(2)]
            s_mm = sem2("smm")
            s_ev = sem2("sev")
            s_hd = sem2("shd")
            s_sc = sem2("ssc")
            s_v2 = sem2("sv2")
            s_md = sem2("smd")
            s_x2 = sem2("sx2")
            s_xo2 = [sem2("sxo2a"), sem2("sxo2b")]
            s_xn = sem2("sxn")
            s_st = [sem2("ssta"), sem2("sstb")]
            s_cw = [sem2("scwa"), sem2("scwb")]
            s_cwd = [sem2("scwda"), sem2("scwdb")]

            c_wg = [0, 0]; c_wu = [0, 0]; c_wd = [0, 0]
            c_xo2 = [0, 0]; c_st = [0, 0]
            c_cw = [0, 0]; c_cwd = [0, 0]
            c_mm = c_ev = c_hd = c_sc = c_v2 = c_md = c_x2 = c_xn = 0
            mm_done_at_it = {}
            ev_done_at_it = {}
            md_total = 0
            xn_total = 0

            if layer == 0:
                P.ins("sync", lambda e: e.dma_start(out=gs_sb[:, :], in_=d["gs128"][:, :]), (s_x2, 16)); c_x2 += 16
                P.ins("sync", lambda e: e.dma_start(out=us_sb[:, :], in_=d["us128"][:, :]), (s_x2, 16)); c_x2 += 16
                P.ins("sync", lambda e: e.dma_start(out=ds_sb[:, :], in_=d["dsT"][:, :]), (s_x2, 16)); c_x2 += 16
                P.wait("vector", s_x2, c_x2)
                gW, uW, dW = d["gw_r"], d["uw_r"], d["dw_r"]
                gC, uC, dC = d["gw_c"], d["uw_c"], d["dw_c"]
            else:
                P.ins("sync", lambda e: e.dma_start(out=cc_sb[:, :], in_=d["c_col"][:, :]), (s_x2, 16)); c_x2 += 16
                P.wait("vector", s_x2, c_x2)
                gC, dC = d["qfc_c"], d["qproj_c"]

            for blk in range(NBLK):
                use_f32 = (layer == 0) and (not have_cache) and (blk == 0)
                for ts_ln in range(4 * blk, 4 * blk + 4):
                    ln.emit(ts_ln)
                for it in range(NI):
                    sl = it % 2
                    git = blk * NI + it
                    wload = "gpsimd" if use_f32 else "sync"
                    if git >= 2:
                        P.wait(wload, s_mm, mm_done_at_it[git - 2])
                    if use_f32:
                        if git >= 2:
                            P.wait(wload, s_cw[sl], c_cw[sl])
                        P.ins(wload, lambda e, sl=sl, it=it: e.dma_start(
                            out=gwb[sl][:, :], in_=gW[it, :, :]), (s_wg[sl], 16))
                        c_wg[sl] += 16
                        P.ins(wload, lambda e, sl=sl, it=it: e.dma_start(
                            out=uwb[sl][:, :], in_=uW[it, :, :]), (s_wu[sl], 16))
                        c_wu[sl] += 16
                        P.wait("sync", s_wg[sl], c_wg[sl])
                        P.ins("sync", lambda e, sl=sl, it=it: e.dma_start(
                            out=gC[it, :, :], in_=gwb[sl][:, :]), (s_cw[sl], 16))
                        c_cw[sl] += 16
                        P.wait("sync", s_wu[sl], c_wu[sl])
                        P.ins("sync", lambda e, sl=sl, it=it: e.dma_start(
                            out=uC[it, :, :], in_=uwb[sl][:, :]), (s_cw[sl], 16))
                        c_cw[sl] += 16
                    else:
                        if (layer == 0) and (not have_cache) and it == 0:
                            P.wait(wload, s_cw[0], c_cw[0])
                            P.wait(wload, s_cw[1], c_cw[1])
                        P.ins(wload, lambda e, sl=sl, it=it: e.dma_start(
                            out=gwb[sl][:, :], in_=gC[it, :, :]), (s_wg[sl], 16))
                        c_wg[sl] += 16
                        if layer == 0:
                            P.ins(wload, lambda e, sl=sl, it=it: e.dma_start(
                                out=uwb[sl][:, :], in_=uC[it, :, :]), (s_wu[sl], 16))
                            c_wu[sl] += 16
                    P.wait("tensor", s_wg[sl], c_wg[sl])
                    if layer == 0:
                        P.wait("tensor", s_wu[sl], c_wu[sl])
                    if it == 0:
                        P.wait("tensor", ln.s_e, ln.e_count(4 * blk + 3))
                    if git >= 2:
                        P.wait("tensor", s_ev, ev_done_at_it[git - 2])
                    if blk > 0 and it < 2:
                        P.wait("tensor", s_xn, xn_total)
                    for kh in range(KH):
                        P.ins("tensor", lambda e, sl=sl, kh=kh, blk=blk: e.matmul(
                            pg[sl][:, :], gwb[sl][:, kh * 128:(kh + 1) * 128],
                            xnT[:, kh * 2048 + blk * 512: kh * 2048 + (blk + 1) * 512],
                            start=(kh == 0), stop=(kh == KH - 1)),
                            (s_mm, 1) if (kh == KH - 1 and layer != 0) else None)
                    if layer == 0:
                        for kh in range(KH):
                            P.ins("tensor", lambda e, sl=sl, kh=kh, blk=blk: e.matmul(
                                pu[sl][:, :], uwb[sl][:, kh * 128:(kh + 1) * 128],
                                xnT[:, kh * 2048 + blk * 512: kh * 2048 + (blk + 1) * 512],
                                start=(kh == 0), stop=(kh == KH - 1)),
                                (s_mm, 1) if kh == KH - 1 else None)
                    c_mm += 1
                    mm_done_at_it[git] = c_mm
                    if layer == 0:
                        P.wait("vector", s_mm, c_mm)
                        if git >= 2:
                            P.wait("vector", s_sc, git - 1)
                        P.ins("vector", lambda e, sl=sl, it=it: e.tensor_scalar(
                            gbuf[sl][:, :], pg[sl][:, :], gs_sb[:, it:it + 1], None,
                            op0=ALU.mult), (s_ev, 1))
                        c_ev += 1; ev_gbuf = c_ev
                        P.ins("vector", lambda e, sl=sl, it=it: e.tensor_scalar(
                            ubuf[sl][:, :], pu[sl][:, :], us_sb[:, it:it + 1], None,
                            op0=ALU.mult), (s_ev, 1))
                        c_ev += 1
                        ev_done_at_it[git] = c_ev
                        P.wait("scalar", s_ev, ev_gbuf)
                        if git >= 2:
                            P.wait("scalar", s_hd, git - 1)
                        P.ins("scalar", lambda e, sl=sl: e.activation(
                            gsil[sl][:, :], gbuf[sl][:, :], AF.Silu), (s_sc, 1))
                        c_sc += 1
                        P.wait("vector", s_sc, c_sc)
                        P.wait("vector", s_ev, c_ev)
                        if blk > 0 and it == 0:
                            P.wait("vector", s_md, md_total)
                        P.ins("vector", lambda e, sl=sl, it=it: e.tensor_tensor(
                            hdn[:, it * 512:(it + 1) * 512], gsil[sl][:, :],
                            ubuf[sl][:, :], op=ALU.mult), (s_hd, 1))
                        c_hd += 1
                    else:
                        P.wait("scalar", s_mm, c_mm)
                        if git >= 2:
                            P.wait("scalar", s_hd, git - 1)
                        P.ins("scalar", lambda e, sl=sl: e.activation(
                            rbuf[sl][:, :], pg[sl][:, :], AF.Relu), (s_ev, 1))
                        c_ev += 1
                        ev_done_at_it[git] = c_ev
                        P.wait("vector", s_ev, c_ev)
                        if blk > 0 and it == 0:
                            P.wait("vector", s_md, md_total)
                        P.ins("vector", lambda e, sl=sl, it=it: e.tensor_tensor(
                            hdn[:, it * 512:(it + 1) * 512], rbuf[sl][:, :],
                            rbuf[sl][:, :], op=ALU.mult), (s_hd, 1))
                        c_hd += 1

                use_f32_d = (layer == 0) and (not have_cache) and (blk == 0)
                for hc2 in range(8):
                    dload = "gpsimd" if use_f32_d else "sync"
                    sld = hc2 % 2
                    ghc = blk * 8 + hc2
                    if ghc >= 2:
                        P.wait(dload, s_md, (ghc - 1) * 4)
                    if use_f32_d:
                        if ghc >= 2:
                            P.wait(dload, s_cwd[sld], c_cwd[sld])
                        P.ins(dload, lambda e, sld=sld, hc2=hc2: e.dma_start(
                            out=dwb[sld][:, :], in_=dW[hc2, :, :]), (s_wd[sld], 16))
                        c_wd[sld] += 16
                        P.wait("sync", s_wd[sld], c_wd[sld])
                        P.ins("sync", lambda e, sld=sld, hc2=hc2: e.dma_start(
                            out=dC[hc2, :, :], in_=dwb[sld][:, :]), (s_cwd[sld], 16))
                        c_cwd[sld] += 16
                    else:
                        if (layer == 0) and (not have_cache) and hc2 == 0:
                            P.wait(dload, s_cwd[0], c_cwd[0])
                            P.wait(dload, s_cwd[1], c_cwd[1])
                        P.ins(dload, lambda e, sld=sld, hc2=hc2: e.dma_start(
                            out=dwb[sld][:, :], in_=dC[hc2, :, :]), (s_wd[sld], 16))
                        c_wd[sld] += 16
                    for ts in range(4):
                        gmd = blk * 32 + hc2 * 4 + ts
                        slp = gmd % 2
                        slx = gmd % 2
                        row0 = blk * 512 + ts * 128
                        col0 = hc2 * 256
                        if gmd >= 2:
                            P.wait(x_eng, s_xn, gmd - 1)
                        P.ins(x_eng, lambda e, slx=slx, row0=row0, col0=col0: e.dma_start(
                            out=xo2[slx][:, :], in_=X_in[row0:row0 + 128, col0:col0 + 256]),
                            (s_xo2[slx], 16))
                        c_xo2[slx] += 16
                        P.wait("tensor", s_wd[sld], c_wd[sld])
                        if hc2 == 0 and ts == 0:
                            P.wait("tensor", s_hd, c_hd)
                            P.wait("tensor", s_ev, c_ev)
                        if gmd >= 2:
                            P.wait("tensor", s_xn, gmd - 1)
                        for it in range(NI):
                            P.ins("tensor", lambda e, slp=slp, sld=sld, it=it, ts=ts: e.matmul(
                                pg[slp][:, 0:256],
                                hdn[:, it * 512 + ts * 128: it * 512 + (ts + 1) * 128],
                                dwb[sld][:, it * 256:(it + 1) * 256],
                                start=(it == 0), stop=(it == NI - 1)),
                                (s_md, 1) if it == NI - 1 else None)
                        c_md += 1
                        P.wait("vector", s_md, c_md)
                        P.wait("vector", s_xo2[slx], c_xo2[slx])
                        if gmd >= 2:
                            P.wait("vector", s_st[slx], c_st[slx])
                        if layer == 0:
                            P.ins("vector", lambda e, slp=slp, slx=slx, col0=col0: e.tensor_tensor(
                                t1b[slx][:, :], pg[slp][:, 0:256],
                                ds_sb[:, col0:col0 + 256], op=ALU.mult), (s_v2, 1))
                            c_v2 += 1
                            P.wait("vector", s_v2, c_v2)
                            P.ins("vector", lambda e, slx=slx: e.scalar_tensor_tensor(
                                xnew[slx][:, :], t1b[slx][:, :], 0.5, xo2[slx][:, :],
                                op0=ALU.mult, op1=ALU.add), (s_xn, 1))
                        else:
                            P.ins("vector", lambda e, slp=slp, slx=slx: e.scalar_tensor_tensor(
                                xnew[slx][:, :], pg[slp][:, 0:256], cc_sb[:, 0:1],
                                xo2[slx][:, :], op0=ALU.mult, op1=ALU.add), (s_xn, 1))
                        c_xn += 1
                        P.wait(x_eng, s_xn, c_xn)
                        P.ins(x_eng, lambda e, slx=slx, row0=row0, col0=col0: e.dma_start(
                            out=X_out[row0:row0 + 128, col0:col0 + 256], in_=xnew[slx][:, :]),
                            (s_st[slx], 16))
                        c_st[slx] += 16
                md_total = c_md
                xn_total = c_xn

            if extra is not None:
                extra(nc, P, st2, sp)
            P.wait(x_eng, s_st[0], c_st[0])
            P.wait(x_eng, s_st[1], c_st[1])
            for i in range(2):
                if c_cw[i]:
                    P.wait("sync", s_cw[i], c_cw[i])
                if c_cwd[i]:
                    P.wait("sync", s_cwd[i], c_cwd[i])
            P.emit(nc)



SC0 = 0.08838834764831845

def build_attn0(nc, d, sp, X_in, X_out, extra=None):
    pref = "a0_"
    sem = lambda n: sp.enter_context(nc.semaphore(pref + n))
    with ExitStack() as stA:
        sbA = lambda n, sh, dt=F32: stA.enter_context(nc.sbuf_tensor(pref + n, sh, dt, side="right"))
        KT = sbA("KT", [128, 4 * 2048], BF16)
        V = sbA("V", [128, 16 * 512], BF16)
        cosb = sbA("cosb", [128, 2048], BF16)
        sinb = sbA("sinb", [128, 2048], BF16)
        mk0 = sbA("mk0", [128, 4 * 512], BF16)

        with ExitStack() as stB:
            sbB = lambda n, sh, dt=F32: stB.enter_context(nc.sbuf_tensor(pref + n, sh, dt))
            xnT = sbB("xnT", [128, 16 * 2048], BF16)

            with ExitStack() as st1:
                P = Plan()
                plan_ln(nc, P, st1, sp, d, X_in, xnT[:, :], pref + "ln_")
                P.emit(nc)

            with ExitStack() as st2:
                sb2 = lambda n, sh, dt=F32: st2.enter_context(nc.sbuf_tensor(pref + n, sh, dt))
                ps2 = lambda n, sh, dt=F32: st2.enter_context(nc.psum_tensor(pref + n, sh, dt))
                P = Plan()
                wkd = sb2("wkd", [128, 16 * 256], BF16)
                wku = sb2("wku", [128, 2 * 512], BF16)
                wvd = sb2("wvd", [128, 16 * 384], BF16)
                wvu = sb2("wvu", [128, 3 * 512], BF16)
                kdT = sb2("kdT", [128, 2 * 2048], BF16)
                vdT = sb2("vdT", [128, 3 * 2048], BF16)
                kanc = sb2("kanc", [128, 4])
                vanc = sb2("vanc", [128, 512])
                kf = [sb2(f"kf{i}", [128, 512]) for i in range(2)]
                rt1 = sb2("rt1", [128, 512])
                rt2 = sb2("rt2", [128, 512])
                pk = [ps2(f"pk{i}", [128, 512]) for i in range(2)]

                s_wk, s_ck = sem("swk"), sem("sck")
                s_t1, s_e1, s_v1 = sem("st1"), sem("se1"), sem("sv1")
                c_wk = c_ck = c_t1 = c_e1 = c_v1 = 0
                pk_free = [None, None]

                P.ins("gpsimd", lambda e: e.dma_start(out=wkd[:, :], in_=d["wkd_r"][:, :]), (s_wk, 16)); c_wk += 16
                P.ins("gpsimd", lambda e: e.dma_start(out=wku[:, :], in_=d["wku_r"][:, :]), (s_wk, 16)); c_wk += 16
                P.ins("gpsimd", lambda e: e.dma_start(out=wvd[:, :], in_=d["wvd_r"][:, :]), (s_wk, 16)); c_wk += 16
                P.ins("gpsimd", lambda e: e.dma_start(out=wvu[:, :], in_=d["wvu_r"][:, :]), (s_wk, 16)); c_wk += 16
                P.ins("sync", lambda e: e.dma_start(out=kanc[:, :], in_=d["kanc"][:, :]), (s_ck, 16)); c_ck += 16
                P.ins("sync", lambda e: e.dma_start(out=vanc[:, :], in_=d["vanc"][:, :]), (s_ck, 16)); c_ck += 16
                P.ins("sync", lambda e: e.dma_start(out=cosb[:, :], in_=d["cosT"][:, :]), (s_ck, 16)); c_ck += 16
                P.ins("sync", lambda e: e.dma_start(out=sinb[:, :], in_=d["sinT"][:, :]), (s_ck, 16)); c_ck += 16
                for j in range(4):
                    P.ins("sync", lambda e, j=j: e.dma_start(
                        out=mk0[:, j * 512:(j + 1) * 512], in_=d["masks0"][j, :, :]), (s_ck, 16)); c_ck += 16

                g = 0
                P.wait("tensor", s_wk, c_wk)
                for rt in range(2):
                    for tc in range(4):
                        psl = g % 2
                        if pk_free[psl] is not None:
                            P.wait("tensor", *pk_free[psl])
                        for kh in range(KH):
                            P.ins("tensor", lambda e, psl=psl, kh=kh, rt=rt, tc=tc: e.matmul(
                                pk[psl][:, :], wkd[:, kh * 256 + rt * 128: kh * 256 + rt * 128 + 128],
                                xnT[:, kh * 2048 + tc * 512: kh * 2048 + (tc + 1) * 512],
                                start=(kh == 0), stop=(kh == KH - 1)),
                                (s_t1, 1) if kh == KH - 1 else None)
                        c_t1 += 1
                        P.wait("scalar", s_t1, c_t1)
                        P.ins("scalar", lambda e, psl=psl, rt=rt, tc=tc: e.copy(
                            kdT[:, rt * 2048 + tc * 512: rt * 2048 + (tc + 1) * 512],
                            pk[psl][:, :]), (s_e1, 1))
                        c_e1 += 1
                        pk_free[psl] = (s_e1, c_e1)
                        g += 1
                P.wait("tensor", s_wk, c_wk)
                for rt in range(3):
                    for tc in range(4):
                        psl = g % 2
                        P.wait("tensor", *pk_free[psl])
                        for kh in range(KH):
                            P.ins("tensor", lambda e, psl=psl, kh=kh, rt=rt, tc=tc: e.matmul(
                                pk[psl][:, :], wvd[:, kh * 384 + rt * 128: kh * 384 + rt * 128 + 128],
                                xnT[:, kh * 2048 + tc * 512: kh * 2048 + (tc + 1) * 512],
                                start=(kh == 0), stop=(kh == KH - 1)),
                                (s_t1, 1) if kh == KH - 1 else None)
                        c_t1 += 1
                        P.wait("scalar", s_t1, c_t1)
                        P.ins("scalar", lambda e, psl=psl, rt=rt, tc=tc: e.copy(
                            vdT[:, rt * 2048 + tc * 512: rt * 2048 + (tc + 1) * 512],
                            pk[psl][:, :]), (s_e1, 1))
                        c_e1 += 1
                        pk_free[psl] = (s_e1, c_e1)
                        g += 1
                P.wait("tensor", s_wk, c_wk)
                P.wait("vector", s_ck, c_ck)
                for dk in range(4):
                    for tc in range(4):
                        psl = g % 2
                        kfs = g % 2
                        P.wait("tensor", *pk_free[psl])
                        for rt in range(2):
                            P.ins("tensor", lambda e, psl=psl, rt=rt, dk=dk, tc=tc: e.matmul(
                                pk[psl][:, :], wku[:, rt * 512 + dk * 128: rt * 512 + dk * 128 + 128],
                                kdT[:, rt * 2048 + tc * 512: rt * 2048 + (tc + 1) * 512],
                                start=(rt == 0), stop=(rt == 1)),
                                (s_t1, 1) if rt == 1 else None)
                        c_t1 += 1
                        P.wait("vector", s_t1, c_t1)
                        P.ins("vector", lambda e, psl=psl, kfs=kfs, dk=dk: e.tensor_scalar(
                            kf[kfs][:, :], pk[psl][:, :], kanc[:, dk:dk + 1], None,
                            op0=ALU.add), (s_v1, 1))
                        c_v1 += 1
                        pk_free[psl] = (s_v1, c_v1)
                        P.wait("vector", s_v1, c_v1)
                        cs_ = lambda t: slice(t * 512, (t + 1) * 512)
                        P.ins("vector", lambda e, kfs=kfs, tc=tc: e.tensor_tensor(
                            rt1[:, :], kf[kfs][:, :], cosb[:, tc * 512:(tc + 1) * 512],
                            op=ALU.mult), (s_v1, 1)); c_v1 += 1
                        P.ins("vector", lambda e, kfs=kfs, tc=tc: e.tensor_tensor(
                            rt2[0:64, :], kf[kfs][64:128, :], sinb[64:128, tc * 512:(tc + 1) * 512],
                            op=ALU.mult), (s_v1, 1)); c_v1 += 1
                        P.ins("vector", lambda e, kfs=kfs, tc=tc: e.tensor_tensor(
                            rt2[64:128, :], kf[kfs][0:64, :], sinb[0:64, tc * 512:(tc + 1) * 512],
                            op=ALU.mult), (s_v1, 1)); c_v1 += 1
                        P.wait("vector", s_v1, c_v1)
                        P.ins("vector", lambda e, dk=dk, tc=tc: e.tensor_tensor(
                            KT[:, dk * 2048 + tc * 512: dk * 2048 + (tc + 1) * 512],
                            rt1[:, :], rt2[:, :], op=ALU.add), (s_v1, 1)); c_v1 += 1
                        g += 1
                for ts in range(NTS):
                    psl = g % 2
                    P.wait("tensor", *pk_free[psl])
                    for rt in range(3):
                        P.ins("tensor", lambda e, psl=psl, rt=rt, ts=ts: e.matmul(
                            pk[psl][:, :], vdT[:, rt * 2048 + ts * 128: rt * 2048 + ts * 128 + 128],
                            wvu[:, rt * 512:(rt + 1) * 512],
                            start=(rt == 0), stop=(rt == 2)),
                            (s_t1, 1) if rt == 2 else None)
                    c_t1 += 1
                    P.wait("vector", s_t1, c_t1)
                    P.ins("vector", lambda e, psl=psl, ts=ts: e.tensor_tensor(
                        V[:, ts * 512:(ts + 1) * 512], pk[psl][:, :], vanc[:, :],
                        op=ALU.add), (s_v1, 1))
                    c_v1 += 1
                    pk_free[psl] = (s_v1, c_v1)
                    g += 1
                P.emit(nc)

            QO = stA.enter_context(nc.sbuf_tensor(pref + "QO", [128, 16 * 2048], BF16, side="right"))
            with ExitStack() as st3:
                sb3 = lambda n, sh, dt=F32: st3.enter_context(nc.sbuf_tensor(pref + n, sh, dt))
                ps3 = lambda n, sh, dt=F32: st3.enter_context(nc.psum_tensor(pref + n, sh, dt))
                P = Plan()
                wq = [sb3(f"wq{i}", [128, 2048], BF16) for i in range(2)]
                expT = sb3("expT", [128, 4 * 512], BF16)
                rt1 = sb3("rt1b", [128, 512])
                rt2 = sb3("rt2b", [128, 512])
                rs = sb3("rs", [1, 512])
                bc = sb3("bc", [128, 512])
                pq = ps3("pq", [128, 512])
                pss = [ps3(f"pss{i}", [128, 512]) for i in range(2)]
                po = [ps3(f"po{i}", [128, 512]) for i in range(2)]
                psum_s = [ps3(f"psum_s{i}", [1, 512]) for i in range(2)]
                pbc = ps3("pbc", [128, 512])

                s_wq = [sem("swq0"), sem("swq1")]
                s_tq, s_vq = sem("stq"), sem("svq")
                s_ts, s_se, s_vm, s_tp = sem("sts"), sem("sse"), sem("svm"), sem("stp")
                s_vr, s_tb, s_vb, s_qo = sem("svr"), sem("stb"), sem("svb"), sem("sqo")
                c_wq = [0, 0]
                c_tq = c_vq = c_ts = c_se = c_vm = c_tp = c_vr = c_tb = c_vb = c_qo = 0
                tq_done = {}
                vq_done = {}
                pq_free = None
                pss_free = [None, None]
                expT_free = [None] * 4
                po_free = [None, None]
                pbc_free = None
                pending_norm = None
                nslot = 0

                def flush_norm():
                    nonlocal c_vr, c_tb, c_vb, c_qo, pbc_free, pending_norm
                    if pending_norm is None:
                        return
                    ph, pqc, pslot, ptp = pending_norm
                    pending_norm = None
                    P.wait("vector", s_tp, ptp)
                    P.ins("vector", lambda e, pslot=pslot: e.reciprocal(
                        rs[:, :], psum_s[pslot][:, :]), (s_vr, 1))
                    c_vr += 1
                    P.wait("tensor", s_vr, c_vr)
                    if pbc_free is not None:
                        P.wait("tensor", *pbc_free)
                    P.ins("tensor", lambda e: e.matmul(
                        pbc[:, :], d["ones_row_sb"], rs[:, :], start=True, stop=True), (s_tb, 1))
                    c_tb += 1
                    P.wait("vector", s_tb, c_tb)
                    P.ins("vector", lambda e: e.tensor_copy(bc[:, :], pbc[:, :]), (s_vb, 1))
                    c_vb += 1
                    pbc_free = (s_vb, c_vb)
                    P.wait("vector", s_vb, c_vb)
                    P.ins("vector", lambda e, ph=ph, pqc=pqc, pslot=pslot: e.tensor_tensor(
                        QO[:, ph * 2048 + pqc * 512: ph * 2048 + (pqc + 1) * 512],
                        po[pslot][:, :], bc[:, :], op=ALU.mult), (s_qo, 1))
                    c_qo += 1
                    po_free[pslot] = (s_qo, c_qo)

                for h in range(17):
                    if h < 16:
                        slw = h % 2
                        if h >= 2:
                            P.wait("gpsimd", s_tq, tq_done[h - 2])
                        P.ins("gpsimd", lambda e, slw=slw, h=h: e.dma_start(
                            out=wq[slw][:, :], in_=d["wq_r"][h, :, :]), (s_wq[slw], 16))
                        c_wq[slw] += 16
                    for step in range(4):
                        if h < 16:
                            slw = h % 2
                            tc = step
                            if step == 0:
                                P.wait("tensor", s_wq[slw], c_wq[slw])
                            if pq_free is not None:
                                P.wait("tensor", *pq_free)
                            for kh in range(KH):
                                P.ins("tensor", lambda e, kh=kh, slw=slw, tc=tc: e.matmul(
                                    pq[:, :], wq[slw][:, kh * 128:(kh + 1) * 128],
                                    xnT[:, kh * 2048 + tc * 512: kh * 2048 + (tc + 1) * 512],
                                    start=(kh == 0), stop=(kh == KH - 1)),
                                    (s_tq, 1) if kh == KH - 1 else None)
                            c_tq += 1
                            P.wait("vector", s_tq, c_tq)
                            P.ins("vector", lambda e, tc=tc: e.tensor_tensor(
                                rt1[:, :], pq[:, :], cosb[:, tc * 512:(tc + 1) * 512],
                                op=ALU.mult), (s_vq, 1)); c_vq += 1
                            P.ins("vector", lambda e, tc=tc: e.tensor_tensor(
                                rt2[0:64, :], pq[64:128, :], sinb[64:128, tc * 512:(tc + 1) * 512],
                                op=ALU.mult), (s_vq, 1)); c_vq += 1
                            P.ins("vector", lambda e, tc=tc: e.tensor_tensor(
                                rt2[64:128, :], pq[0:64, :], sinb[0:64, tc * 512:(tc + 1) * 512],
                                op=ALU.mult), (s_vq, 1)); c_vq += 1
                            P.wait("vector", s_vq, c_vq)
                            P.ins("vector", lambda e, h=h, tc=tc: e.tensor_tensor(
                                QO[:, h * 2048 + tc * 512: h * 2048 + (tc + 1) * 512],
                                rt1[:, :], rt2[:, :], op=ALU.add), (s_vq, 1)); c_vq += 1
                            pq_free = (s_vq, c_vq)
                            vq_done[(h, tc)] = c_vq
                            if step == 3:
                                tq_done[h] = c_tq
                        if h >= 1:
                            ah = h - 1
                            kv = ah // 4
                            qc = step
                            nkt = 4 * qc + 4
                            rdy = {}
                            def emit_pv(k, ah=ah, qc=qc, nkt=nkt, nslot=None):
                                nonlocal c_tp
                                kv_ = ah // 4
                                P.wait("tensor", *rdy[k])
                                if k == 0 and po_free[nslot] is not None:
                                    P.wait("tensor", *po_free[nslot])
                                P.ins("tensor", lambda e, k=k, kv_=kv_, nslot=nslot, nkt=nkt: e.matmul(
                                    po[nslot][:, :], V[:, k * 512 + kv_ * 128: k * 512 + kv_ * 128 + 128],
                                    expT[:, (k % 4) * 512:(k % 4 + 1) * 512],
                                    start=(k == 0), stop=(k == nkt - 1)))
                                P.ins("tensor", lambda e, k=k, nslot=nslot, nkt=nkt: e.matmul(
                                    psum_s[nslot][:, :], d["ones_col_sb"],
                                    expT[:, (k % 4) * 512:(k % 4 + 1) * 512],
                                    start=(k == 0), stop=(k == nkt - 1)), (s_tp, 1))
                                c_tp += 1
                                expT_free[k % 4] = (s_tp, c_tp)
                            for kt in range(nkt):
                                psl = kt % 2
                                if kt == 0:
                                    P.wait("tensor", s_vq, vq_done[(ah, qc)])
                                if kt == 2:
                                    flush_norm()
                                if pss_free[psl] is not None:
                                    P.wait("tensor", *pss_free[psl])
                                P.ins("tensor", lambda e, psl=psl, kv=kv, kt=kt, ah=ah, qc=qc: e.matmul(
                                    pss[psl][:, :], KT[:, kv * 2048 + kt * 128: kv * 2048 + kt * 128 + 128],
                                    QO[:, ah * 2048 + qc * 512: ah * 2048 + (qc + 1) * 512],
                                    start=True, stop=True), (s_ts, 1))
                                c_ts += 1
                                P.wait("scalar", s_ts, c_ts)
                                if expT_free[kt % 4] is not None:
                                    P.wait("scalar", *expT_free[kt % 4])
                                P.ins("scalar", lambda e, psl=psl, kt=kt: e.activation(
                                    expT[:, (kt % 4) * 512:(kt % 4 + 1) * 512], pss[psl][:, :],
                                    AF.Exp, scale=SC0), (s_se, 1))
                                c_se += 1
                                pss_free[psl] = (s_se, c_se)
                                rdy[kt] = (s_se, c_se)
                                if kt >= 4 * qc:
                                    j = kt - 4 * qc
                                    P.wait("vector", s_se, c_se)
                                    P.ins("vector", lambda e, kt=kt, j=j: e.tensor_tensor(
                                        expT[:, (kt % 4) * 512:(kt % 4 + 1) * 512],
                                        expT[:, (kt % 4) * 512:(kt % 4 + 1) * 512],
                                        mk0[:, j * 512:(j + 1) * 512], op=ALU.mult), (s_vm, 1))
                                    c_vm += 1
                                    rdy[kt] = (s_vm, c_vm)
                                if kt >= 1:
                                    emit_pv(kt - 1, nslot=nslot)
                            emit_pv(nkt - 1, nslot=nslot)
                            pending_norm = (ah, qc, nslot, c_tp)
                            nslot = 1 - nslot
                flush_norm()
                if extra is not None:
                    extra(nc, P, st3, sp)
                P.emit(nc)

        with ExitStack() as st4:
            sb4 = lambda n, sh, dt=F32: st4.enter_context(nc.sbuf_tensor(pref + n, sh, dt))
            ps4 = lambda n, sh, dt=F32: st4.enter_context(nc.psum_tensor(pref + n, sh, dt))
            P = Plan()
            wo = [sb4(f"wo{i}", [128, 16 * 512], BF16) for i in range(2)]
            xo3 = [sb4(f"xo3{i}", [128, 512]) for i in range(2)]
            xn3 = [sb4(f"xn3{i}", [128, 512]) for i in range(2)]
            py = [ps4(f"py{i}", [128, 512]) for i in range(2)]
            s_wo = [sem("swo0"), sem("swo1")]
            s_xo = [sem("sxoa"), sem("sxob")]
            s_sy = [sem("ssya"), sem("ssyb")]
            s_ty, s_vy = sem("sty"), sem("svy")
            c_wo = [0, 0]
            c_xo = [0, 0]
            c_sy = [0, 0]
            c_ty = c_vy = 0
            for hc in range(4):
                slw = hc % 2
                if hc >= 2:
                    P.wait("gpsimd", s_ty, (hc - 1) * 16)
                P.ins("gpsimd", lambda e, slw=slw, hc=hc: e.dma_start(
                    out=wo[slw][:, :], in_=d["wo_r"][hc, :, :]), (s_wo[slw], 16))
                c_wo[slw] += 16
                for ts in range(NTS):
                    g = hc * 16 + ts
                    psl = g % 2
                    slx = g % 2
                    if g >= 2:
                        P.wait("sync", s_vy, g - 1)
                    P.ins("sync", lambda e, slx=slx, ts=ts, hc=hc: e.dma_start(
                        out=xo3[slx][:, :], in_=X_in[ts * 128:(ts + 1) * 128, hc * 512:(hc + 1) * 512]),
                        (s_xo[slx], 16))
                    c_xo[slx] += 16
                    P.wait("tensor", s_wo[slw], c_wo[slw])
                    if g >= 2:
                        P.wait("tensor", s_vy, g - 1)
                    for dt in range(16):
                        P.ins("tensor", lambda e, psl=psl, slw=slw, dt=dt, ts=ts: e.matmul(
                            py[psl][:, :], QO[:, dt * 2048 + ts * 128: dt * 2048 + ts * 128 + 128],
                            wo[slw][:, dt * 512:(dt + 1) * 512],
                            start=(dt == 0), stop=(dt == 15)),
                            (s_ty, 1) if dt == 15 else None)
                    c_ty += 1
                    P.wait("vector", s_ty, c_ty)
                    P.wait("vector", s_xo[slx], c_xo[slx])
                    if g >= 2:
                        P.wait("vector", s_sy[slx], c_sy[slx])
                    P.ins("vector", lambda e, psl=psl, slx=slx: e.tensor_tensor(
                        xn3[slx][:, :], py[psl][:, :], xo3[slx][:, :], op=ALU.add), (s_vy, 1))
                    c_vy += 1
                    P.wait("sync", s_vy, c_vy)
                    P.ins("sync", lambda e, slx=slx, ts=ts, hc=hc: e.dma_start(
                        out=X_out[ts * 128:(ts + 1) * 128, hc * 512:(hc + 1) * 512], in_=xn3[slx][:, :]),
                        (s_sy[slx], 16))
                    c_sy[slx] += 16
            P.wait("sync", s_sy[0], c_sy[0])
            P.wait("sync", s_sy[1], c_sy[1])
            P.emit(nc)



def build_attn1(nc, d, sp, X_in, X_out, qkv_s, qT_s, kT_s):
    pref = "a1_"
    sem = lambda n: sp.enter_context(nc.semaphore(pref + n))

    with ExitStack() as stB:
        xnT = stB.enter_context(nc.sbuf_tensor(pref + "xnT", [128, 16 * 2048], BF16))
        with ExitStack() as st1:
            P = Plan()
            plan_ln(nc, P, st1, sp, d, X_in, xnT[:, :], pref + "ln_")
            P.emit(nc)
        with ExitStack() as st2:
            sb2 = lambda n, sh, dt=F32: st2.enter_context(nc.sbuf_tensor(pref + n, sh, dt))
            ps2 = lambda n, sh, dt=F32: st2.enter_context(nc.psum_tensor(pref + n, sh, dt))
            P = Plan()
            wqv = [sb2(f"wqv{i}", [128, 16 * 512], BF16) for i in range(2)]
            qb = [sb2(f"qb{i}", [128, 512], BF16) for i in range(2)]
            pqk = [ps2(f"pqk{i}", [128, 512]) for i in range(2)]
            s_w = [sem("sw0"), sem("sw1")]
            s_t, s_e = sem("sqt"), sem("sqe")
            s_st = [sem("sqsa"), sem("sqsb")]
            c_w = [0, 0]; c_st = [0, 0]; c_t = c_e = 0
            for jc in range(12):
                sl = jc % 2
                if jc >= 2:
                    P.wait("gpsimd", s_t, (jc - 1) * 16)
                P.ins("gpsimd", lambda e, sl=sl, jc=jc: e.dma_start(
                    out=wqv[sl][:, :], in_=d["wqkv_r"][jc, :, :]), (s_w[sl], 16))
                c_w[sl] += 16
                P.wait("tensor", s_w[sl], c_w[sl])
                for ts in range(NTS):
                    g = jc * 16 + ts
                    psl = g % 2
                    if g >= 2:
                        P.wait("tensor", s_e, g - 1)
                    for kh in range(KH):
                        P.ins("tensor", lambda e, psl=psl, kh=kh, sl=sl, ts=ts: e.matmul(
                            pqk[psl][:, :], xnT[:, kh * 2048 + ts * 128: kh * 2048 + ts * 128 + 128],
                            wqv[sl][:, kh * 512:(kh + 1) * 512],
                            start=(kh == 0), stop=(kh == KH - 1)),
                            (s_t, 1) if kh == KH - 1 else None)
                    c_t += 1
                    P.wait("scalar", s_t, c_t)
                    if g >= 2:
                        P.wait("scalar", s_st[g % 2], c_st[g % 2])
                    P.ins("scalar", lambda e, psl=psl, g=g: e.copy(
                        qb[g % 2][:, :], pqk[psl][:, :]), (s_e, 1))
                    c_e += 1
                    P.wait("sync", s_e, c_e)
                    P.ins("sync", lambda e, g=g, ts=ts, jc=jc: e.dma_start(
                        out=qkv_s[ts * 128:(ts + 1) * 128, jc * 512:(jc + 1) * 512],
                        in_=qb[g % 2][:, :]), (s_st[g % 2], 16))
                    c_st[g % 2] += 16
            P.wait("sync", s_st[0], c_st[0])
            P.wait("sync", s_st[1], c_st[1])
            P.emit(nc)

    with ExitStack() as st3:
        sb3 = lambda n, sh, dt=F32: st3.enter_context(nc.sbuf_tensor(pref + n, sh, dt))
        ps3 = lambda n, sh, dt=F32: st3.enter_context(nc.psum_tensor(pref + n, sh, dt))
        P = Plan()
        qkt = [sb3(f"qkt{i}", [128, 4096], BF16) for i in range(2)]
        qnw = sb3("qnw", [128, 2048], BF16)
        knw = sb3("knw", [128, 2048], BF16)
        ctm = sb3("ctm", [128, 16 * 128], BF16)
        stm = sb3("stm", [128, 16 * 128], BF16)
        sqd = sb3("q2sqd", [128, 2048])
        lnt = sb3("q2lnt", [128, 8])
        qn = [sb3(f"qn{i}", [128, 2048], BF16) for i in range(2)]
        rt1 = [sb3(f"q2rt1{i}", [128, 2048], BF16) for i in range(2)]
        rth = [sb3(f"q2rth{i}", [128, 2048], BF16) for i in range(2)]
        qrb = [sb3(f"qrb{i}", [128, 2048], BF16) for i in range(2)]
        qTs = [sb3(f"qTs{i}", [128, 2048], BF16) for i in range(2)]
        pst = [ps3(f"q2pst{i}", [128, 512], BF16) for i in range(8)]
        identb = d["identb_sb"]
        pending_ev = None
        s_l, s_v, s_s, s_t, s_e = (sem("r_l"), sem("r_v"), sem("r_s"),
                                   sem("r_t"), sem("r_e"))
        s_lq = [sem("r_lqa"), sem("r_lqb")]
        s_st = [sem("r_sta"), sem("r_stb")]
        c_lq = [0, 0]; c_stk = [0, 0]
        c_l = c_v = c_s = c_t = c_e = c_g = 0
        stt_v = {}
        s_done = {}
        P.ins("sync", lambda e: e.dma_start(out=qnw[:, :], in_=d["qnw"][:, :]), (s_l, 16)); c_l += 16
        P.ins("sync", lambda e: e.dma_start(out=knw[:, :], in_=d["knw"][:, :]), (s_l, 16)); c_l += 16
        for ts in range(16):
            P.ins("sync", lambda e, ts=ts: e.dma_start(
                out=ctm[:, ts * 128:(ts + 1) * 128], in_=d["cosTM_r"][ts, :, :]), (s_l, 16)); c_l += 16
            P.ins("sync", lambda e, ts=ts: e.dma_start(
                out=stm[:, ts * 128:(ts + 1) * 128], in_=d["sinTM_r"][ts, :, :]), (s_l, 16)); c_l += 16
        grp = 0
        done_l = {}
        for ts in range(NTS):
            sl = ts % 2
            if ts >= 2:
                P.wait("sync", s_v, done_l[ts - 2])
                P.wait("sync", s_s, s_done[ts - 2])
            P.ins("sync", lambda e, sl=sl, ts=ts: e.dma_start(
                out=qkt[sl][:, :], in_=qkv_s[ts * 128:(ts + 1) * 128, 0:4096]), (s_lq[sl], 16))
            c_lq[sl] += 16
            for kind in range(2):
                off = kind * 2048
                nw = qnw if kind == 0 else knw
                dstT = qT_s if kind == 0 else kT_s
                lcol = kind * 4
                P.wait("scalar", s_l, c_l)
                P.wait("scalar", s_lq[sl], c_lq[sl])
                if ts >= 1:
                    P.wait("scalar", s_v, stt_v[(ts - 1, kind)])
                P.ins("scalar", lambda e, sl=sl, off=off, lcol=lcol: e.activation(
                    sqd[:, :], qkt[sl][:, off:off + 2048], AF.Square,
                    accum_out=lnt[:, lcol:lcol + 1]), (s_s, 1)); c_s += 1
                P.wait("scalar", s_s, c_s)
                P.ins("scalar", lambda e, lcol=lcol: e.activation(
                    lnt[:, lcol + 2:lcol + 3], lnt[:, lcol:lcol + 1], AF.Ln,
                    bias=d["eps_rms_col"], scale=1.0 / 2048.0), (s_s, 1)); c_s += 1
                P.wait("scalar", s_s, c_s)
                P.ins("scalar", lambda e, lcol=lcol: e.activation(
                    lnt[:, lcol + 3:lcol + 4], lnt[:, lcol + 2:lcol + 3], AF.Exp,
                    scale=-0.5), (s_s, 1)); c_s += 1
                cosAP = bass.AP(ctm[:, :].tensor, ts * 128, [[16 * 128, 128], [0, 16], [1, 128]])
                sinHI = bass.AP(stm[:, :].tensor, ts * 128 + 64, [[16 * 128, 128], [0, 16], [1, 64]])
                sinLO = bass.AP(stm[:, :].tensor, ts * 128, [[16 * 128, 128], [0, 16], [1, 64]])
                qnLO = bass.AP(qn[kind][:, :].tensor, 0, [[2048, 128], [128, 16], [1, 64]])
                qnHI = bass.AP(qn[kind][:, :].tensor, 64, [[2048, 128], [128, 16], [1, 64]])
                rthLO = bass.AP(rth[kind][:, :].tensor, 0, [[2048, 128], [128, 16], [1, 64]])
                rthHI = bass.AP(rth[kind][:, :].tensor, 64, [[2048, 128], [128, 16], [1, 64]])
                P.wait("vector", s_s, c_s)
                P.ins("vector", lambda e, sl=sl, off=off, nw=nw, lcol=lcol, kind=kind: e.scalar_tensor_tensor(
                    qn[kind][:, :], qkt[sl][:, off:off + 2048], lnt[:, lcol + 3:lcol + 4], nw[:, :],
                    op0=ALU.mult, op1=ALU.mult), (s_v, 1)); c_v += 1
                stt_v[(ts, kind)] = c_v
                P.wait("vector", s_v, c_v)
                P.ins("vector", lambda e, cosAP=cosAP, kind=kind: e.tensor_tensor(
                    rt1[kind][:, :], qn[kind][:, :], cosAP, op=ALU.mult), (s_v, 1)); c_v += 1
                P.ins("vector", lambda e, rthLO=rthLO, qnHI=qnHI, sinHI=sinHI: e.tensor_tensor(
                    rthLO, qnHI, sinHI, op=ALU.mult), (s_v, 1)); c_v += 1
                P.ins("vector", lambda e, rthHI=rthHI, qnLO=qnLO, sinLO=sinLO: e.tensor_tensor(
                    rthHI, qnLO, sinLO, op=ALU.mult), (s_v, 1)); c_v += 1
                P.wait("vector", s_v, c_v)
                if grp >= 2:
                    P.wait("vector", s_t, 4 * (grp - 1))
                P.ins("vector", lambda e, kind=kind: e.tensor_tensor(
                    qrb[kind][:, :], rt1[kind][:, :], rth[kind][:, :], op=ALU.add), (s_v, 1)); c_v += 1
                P.wait("tensor", s_v, c_v)
                tsl = kind
                if grp >= 2:
                    P.wait("tensor", s_st[tsl], c_stk[tsl])
                for g4 in range(4):
                    psl = (grp * 4 + g4) % 8
                    if grp * 4 + g4 >= 8:
                        P.wait("tensor", s_e, grp * 4 + g4 - 7)
                    for j in range(4):
                        dt = 4 * g4 + j
                        P.ins("tensor", lambda e, psl=psl, dt=dt, j=j, kind=kind: e.transpose(
                            pst[psl][:, j * 128:(j + 1) * 128],
                            qrb[kind][:, dt * 128:(dt + 1) * 128], identb[:, :]),
                            (s_t, 1) if j == 3 else None)
                    c_t += 1
                this_ev = []
                for g4 in range(4):
                    psl = (grp * 4 + g4) % 8
                    tcnt = c_t - 3 + g4
                    this_ev.append((psl, g4, tsl, tcnt))
                this_store = (ts, tsl, dstT, grp)
                if pending_ev is not None:
                    pev, pstore = pending_ev
                    pgrp = pstore[3]
                    ptsl0 = pstore[1]
                    if pgrp >= 2:
                        P.wait("scalar", s_st[ptsl0], c_stk[ptsl0])
                    for (ppsl, pg4, ptsl, ptcnt) in pev:
                        P.wait("scalar", s_t, ptcnt)
                        P.ins("scalar", lambda e, ppsl=ppsl, pg4=pg4, ptsl=ptsl: e.copy(
                            qTs[ptsl][:, pg4 * 512:(pg4 + 1) * 512], pst[ppsl][:, :]), (s_e, 1))
                        c_e += 1
                    P.wait("sync", s_e, c_e)
                    pts, ptsl, pdstT = pstore[:3]
                    P.ins("sync", lambda e, pts=pts, ptsl=ptsl, pdstT=pdstT: e.dma_start(
                        out=pdstT[pts, :, :], in_=qTs[ptsl][:, :]), (s_st[ptsl], 16))
                    c_stk[ptsl] += 16
                pending_ev = (this_ev, this_store)
                grp += 1
            done_l[ts] = c_v
            s_done[ts] = c_s
        if pending_ev is not None:
            pev, pstore = pending_ev
            pgrp = pstore[3]
            ptsl0 = pstore[1]
            if pgrp >= 2:
                P.wait("scalar", s_st[ptsl0], c_stk[ptsl0])
            for (ppsl, pg4, ptsl, ptcnt) in pev:
                P.wait("scalar", s_t, ptcnt)
                P.ins("scalar", lambda e, ppsl=ppsl, pg4=pg4, ptsl=ptsl: e.copy(
                    qTs[ptsl][:, pg4 * 512:(pg4 + 1) * 512], pst[ppsl][:, :]), (s_e, 1))
                c_e += 1
            P.wait("sync", s_e, c_e)
            pts, ptsl, pdstT = pstore[:3]
            P.ins("sync", lambda e, pts=pts, ptsl=ptsl, pdstT=pdstT: e.dma_start(
                out=pdstT[pts, :, :], in_=qTs[ptsl][:, :]), (s_st[ptsl], 16))
            c_stk[ptsl] += 16
        P.wait("sync", s_st[0], c_stk[0])
        P.wait("sync", s_st[1], c_stk[1])
        P.emit(nc)

    with ExitStack() as stO:
        outQ = stO.enter_context(nc.sbuf_tensor(pref + "outQ", [128, 16 * 2048], BF16, side="right"))
        with ExitStack() as st4:
            sb4 = lambda n, sh, dt=F32: st4.enter_context(nc.sbuf_tensor(pref + n, sh, dt))
            ps4 = lambda n, sh, dt=F32: st4.enter_context(nc.psum_tensor(pref + n, sh, dt))
            P = Plan()
            KTr = sb4("KTr", [128, 12 * 2048], BF16)
            Vr = sb4("Vr", [128, 12 * 2048], BF16)
            qTq = sb4("qTq", [128, 4 * 2048], BF16)
            mk1 = sb4("mk1", [128, 12 * 512], BF16)
            expT = sb4("a1expT", [128, 3 * 512], BF16)
            eacc = sb4("a1eacc", [128, 512])
            rs = sb4("a1rs", [1, 512])
            bc = sb4("a1bc", [128, 512])
            pss = [ps4(f"a1pss{i}", [128, 512]) for i in range(3)]
            po = [ps4(f"a1po{i}", [128, 512]) for i in range(2)]
            psum_s = [ps4(f"a1ps{i}", [1, 512]) for i in range(2)]
            pbc = ps4("a1pbc", [128, 512])
            s_kl = [sem("w_kl0"), sem("w_kl1")]
            s_ql = [sem("w_ql0"), sem("w_ql1")]
            s_ts, s_se, s_vm, s_tp = sem("w_ts"), sem("w_se"), sem("w_vm"), sem("w_tp")
            s_va, s_sm = sem("w_va"), sem("w_sm")
            s_vr, s_tb, s_vb, s_qo = sem("w_vr"), sem("w_tb"), sem("w_vb"), sem("w_qo")
            c_kl = [0, 0]; c_ql = [0, 0]
            c_ts = c_se = c_vm = c_tp = c_vr = c_tb = c_vb = c_qo = 0
            c_va = c_sm = 0
            psum_s_free = None
            acc_read = [None] * 3
            for j in range(12):
                P.ins("sync", lambda e, j=j: e.dma_start(
                    out=mk1[:, j * 512:(j + 1) * 512], in_=d["masks1"][j, :, :]), (s_kl[0], 16))
                c_kl[0] += 16
            pss_free = [None, None, None]
            expT_free = [None, None, None]
            po_free = [None, None]
            pbc_free = None
            pending_norm = None
            nslot = 0
            tp_at_qc_end = {}

            def flush_norm():
                nonlocal c_vr, c_tb, c_vb, c_qo, pbc_free, pending_norm
                if pending_norm is None:
                    return
                ph, pqc, pslot, ptp = pending_norm
                pending_norm = None
                P.wait("vector", s_tp, ptp)
                P.ins("vector", lambda e, pslot=pslot: e.reciprocal(
                    rs[:, :], psum_s[pslot][:, :]), (s_vr, 1))
                c_vr += 1
                P.wait("tensor", s_vr, c_vr)
                if pbc_free is not None:
                    P.wait("tensor", *pbc_free)
                P.ins("tensor", lambda e: e.matmul(
                    pbc[:, :], d["ones_row_sb"], rs[:, :], start=True, stop=True), (s_tb, 1))
                c_tb += 1
                P.wait("vector", s_tb, c_tb)
                P.ins("vector", lambda e: e.tensor_copy(bc[:, :], pbc[:, :]), (s_vb, 1))
                c_vb += 1
                pbc_free = (s_vb, c_vb)
                P.wait("vector", s_vb, c_vb)
                P.ins("vector", lambda e, ph=ph, pqc=pqc, pslot=pslot: e.tensor_tensor(
                    outQ[:, ph * 2048 + pqc * 512: ph * 2048 + (pqc + 1) * 512],
                    po[pslot][:, :], bc[:, :], op=ALU.mult), (s_qo, 1))
                c_qo += 1
                po_free[pslot] = (s_qo, c_qo)
            for qc in range(4):
                qp = qc % 2
                for kt in range(4 * qc, 4 * qc + 4):
                    slot = kt % 12
                    if qc >= 3:
                        P.wait("sync", s_tp, tp_at_qc_end[qc - 1])
                    P.ins("sync", lambda e, kt=kt, slot=slot: e.dma_start(
                        out=KTr[:, slot * 2048:(slot + 1) * 2048], in_=kT_s[kt, :, :]), (s_kl[qp], 16))
                    c_kl[qp] += 16
                    P.ins("sync", lambda e, kt=kt, slot=slot: e.dma_start(
                        out=Vr[:, slot * 2048:(slot + 1) * 2048],
                        in_=qkv_s[kt * 128:(kt + 1) * 128, 4096:6144]), (s_kl[qp], 16))
                    c_kl[qp] += 16
                if qc >= 1:
                    P.wait("sync", s_ts, c_ts)
                for j in range(4):
                    P.ins("sync", lambda e, j=j, qc=qc: e.dma_start(
                        out=qTq[:, j * 2048:(j + 1) * 2048], in_=qT_s[4 * qc + j, :, :]), (s_ql[qp], 16))
                    c_ql[qp] += 16
                kt_lo = max(0, 4 * qc - 8)
                kts = list(range(kt_lo, 4 * qc + 4))
                for h in range(16):
                    rdy = {}
                    def emit_pv(k, h=h, qc=qc, kts=kts, nslot=None):
                        nonlocal c_tp
                        P.wait("tensor", *rdy[k])
                        if k == kts[0] and po_free[nslot] is not None:
                            P.wait("tensor", *po_free[nslot])
                        P.ins("tensor", lambda e, k=k, h=h, nslot=nslot: e.matmul(
                            po[nslot][:, :], Vr[:, (k % 12) * 2048 + h * 128: (k % 12) * 2048 + h * 128 + 128],
                            expT[:, (k % 3) * 512:(k % 3 + 1) * 512],
                            start=(k == kts[0]), stop=(k == kts[-1])))
                        P.ins("tensor", lambda e, k=k, nslot=nslot: e.matmul(
                            psum_s[nslot][:, :], d["ones_col_sb"],
                            expT[:, (k % 3) * 512:(k % 3 + 1) * 512],
                            start=(k == kts[0]), stop=(k == kts[-1])), (s_tp, 1))
                        c_tp += 1
                        expT_free[k % 3] = (s_tp, c_tp)
                    for ki, kt in enumerate(kts):
                        ktr = kt + 8 - 4 * qc
                        psl = kt % 3
                        if h == 0 and ki == 0:
                            P.wait("tensor", s_kl[0], c_kl[0])
                            P.wait("tensor", s_kl[1], c_kl[1])
                            P.wait("tensor", s_ql[qc % 2], c_ql[qc % 2])
                        if ki == 2:
                            flush_norm()
                        if pss_free[psl] is not None:
                            P.wait("tensor", *pss_free[psl])
                        rhsq = bass.AP(qTq[:, :].tensor, h * 128, [[4 * 2048, 128], [2048, 4], [1, 128]])
                        P.ins("tensor", lambda e, psl=psl, kt=kt, h=h, rhsq=rhsq: e.matmul(
                            pss[psl][:, :],
                            KTr[:, (kt % 12) * 2048 + h * 128: (kt % 12) * 2048 + h * 128 + 128],
                            rhsq, start=True, stop=True), (s_ts, 1))
                        c_ts += 1
                        P.wait("scalar", s_ts, c_ts)
                        if expT_free[kt % 3] is not None:
                            P.wait("scalar", *expT_free[kt % 3])
                        P.ins("scalar", lambda e, psl=psl, kt=kt: e.activation(
                            expT[:, (kt % 3) * 512:(kt % 3 + 1) * 512], pss[psl][:, :],
                            AF.Exp, scale=SC0), (s_se, 1))
                        c_se += 1
                        pss_free[psl] = (s_se, c_se)
                        rdy[kt] = (s_se, c_se)
                        if not (4 <= ktr <= 7):
                            P.wait("vector", s_se, c_se)
                            P.ins("vector", lambda e, kt=kt, ktr=ktr: e.tensor_tensor(
                                expT[:, (kt % 3) * 512:(kt % 3 + 1) * 512],
                                expT[:, (kt % 3) * 512:(kt % 3 + 1) * 512],
                                mk1[:, ktr * 512:(ktr + 1) * 512], op=ALU.mult), (s_vm, 1))
                            c_vm += 1
                            rdy[kt] = (s_vm, c_vm)
                        if ki >= 1:
                            emit_pv(kts[ki - 1], nslot=nslot)
                    emit_pv(kts[-1], nslot=nslot)
                    pending_norm = (h, qc, nslot, c_tp)
                    nslot = 1 - nslot
                tp_at_qc_end[qc] = c_tp
            flush_norm()
            P.emit(nc)

        with ExitStack() as st5:
            sb5 = lambda n, sh, dt=F32: st5.enter_context(nc.sbuf_tensor(pref + n, sh, dt))
            ps5 = lambda n, sh, dt=F32: st5.enter_context(nc.psum_tensor(pref + n, sh, dt))
            P = Plan()
            wo = [sb5(f"a1wo{i}", [128, 16 * 512], BF16) for i in range(2)]
            xo3 = [sb5(f"a1xo{i}", [128, 512]) for i in range(2)]
            xn3 = [sb5(f"a1xn{i}", [128, 512]) for i in range(2)]
            py = [ps5(f"a1py{i}", [128, 512]) for i in range(2)]
            s_wo = [sem("o_w0"), sem("o_w1")]
            s_xo = [sem("o_xoa"), sem("o_xob")]
            s_sy = [sem("o_sya"), sem("o_syb")]
            s_ty, s_vy = sem("o_ty"), sem("o_vy")
            c_wo = [0, 0]
            c_xo = [0, 0]
            c_sy = [0, 0]
            c_ty = c_vy = 0
            for hc in range(4):
                slw = hc % 2
                if hc >= 2:
                    P.wait("gpsimd", s_ty, (hc - 1) * 16)
                P.ins("gpsimd", lambda e, slw=slw, hc=hc: e.dma_start(
                    out=wo[slw][:, :], in_=d["wo1_r"][hc, :, :]), (s_wo[slw], 16))
                c_wo[slw] += 16
                for ts in range(NTS):
                    g = hc * 16 + ts
                    psl = g % 2
                    slx = g % 2
                    if g >= 2:
                        P.wait("sync", s_vy, g - 1)
                    P.ins("sync", lambda e, slx=slx, ts=ts, hc=hc: e.dma_start(
                        out=xo3[slx][:, :], in_=X_in[ts * 128:(ts + 1) * 128, hc * 512:(hc + 1) * 512]),
                        (s_xo[slx], 16))
                    c_xo[slx] += 16
                    P.wait("tensor", s_wo[slw], c_wo[slw])
                    if g >= 2:
                        P.wait("tensor", s_vy, g - 1)
                    for dt in range(16):
                        P.ins("tensor", lambda e, psl=psl, slw=slw, dt=dt, ts=ts: e.matmul(
                            py[psl][:, :], outQ[:, dt * 2048 + ts * 128: dt * 2048 + ts * 128 + 128],
                            wo[slw][:, dt * 512:(dt + 1) * 512],
                            start=(dt == 0), stop=(dt == 15)),
                            (s_ty, 1) if dt == 15 else None)
                    c_ty += 1
                    P.wait("vector", s_ty, c_ty)
                    P.wait("vector", s_xo[slx], c_xo[slx])
                    if g >= 2:
                        P.wait("vector", s_sy[slx], c_sy[slx])
                    P.ins("vector", lambda e, psl=psl, slx=slx: e.tensor_tensor(
                        xn3[slx][:, :], py[psl][:, :], xo3[slx][:, :], op=ALU.add), (s_vy, 1))
                    c_vy += 1
                    P.wait("sync", s_vy, c_vy)
                    P.ins("sync", lambda e, slx=slx, ts=ts, hc=hc: e.dma_start(
                        out=X_out[ts * 128:(ts + 1) * 128, hc * 512:(hc + 1) * 512], in_=xn3[slx][:, :]),
                        (s_sy[slx], 16))
                    c_sy[slx] += 16
            P.wait("sync", s_sy[0], c_sy[0])
            P.wait("sync", s_sy[1], c_sy[1])
            P.emit(nc)



def _tern_chunks(d):
    out = []
    for i in range(40):
        for c in range(4):
            out.append((0, i, c * 512, 512))
    for i in range(8):
        for c in range(20):
            out.append((1, i, c * 512, 512))
    return out


def plan_tern_absmean(nc, P, Pt, st, sp, d, thr_sb, c_col):
    sb = lambda n, sh, dt=F32: st.enter_context(nc.sbuf_tensor("tm_" + n, sh, dt))
    ps = lambda n, sh, dt=F32: st.enter_context(nc.psum_tensor("tm_" + n, sh, dt))
    sem = lambda n: sp.enter_context(nc.semaphore("tm_" + n))
    wt = [sb(f"wt{i}", [128, 512]) for i in range(2)]
    acc = sb("acc", [128, 320])
    red = sb("red", [128, 2])
    sml = sb("sml", [1, 8])
    pam = ps("pam", [1, 2])
    pbc2 = ps("pbc2", [128, 8])
    s_l, s_v, s_t, s_st = sem("sl"), sem("sv"), sem("st"), sem("sst")
    c_l = c_v = c_t = c_st = 0
    chunks = _tern_chunks(d)
    srcs = [d["wfc_r"], d["wpj_r"]]
    for ci, (m, i, c0, w) in enumerate(chunks):
        sl = ci % 2
        if ci >= 2:
            P.wait("gpsimd", s_v, ci - 1)
        P.ins("gpsimd", lambda e, sl=sl, m=m, i=i, c0=c0, w=w: e.dma_start(
            out=wt[sl][:, 0:w], in_=srcs[m][i, :, c0:c0 + w]), (s_l, 16)); c_l += 16
        P.wait("vector", s_l, c_l)
        P.ins("vector", lambda e, sl=sl, ci=ci, w=w: e.reduce_sum(
            acc[:, ci:ci + 1], wt[sl][:, 0:w], axis=AX.X,
            apply_absolute_value=True), (s_v, 1)); c_v += 1
    nfc_ch = 160
    P = Pt
    P.wait("vector", s_v, c_v)
    P.ins("vector", lambda e: e.reduce_sum(red[:, 0:1], acc[:, 0:nfc_ch], axis=AX.X), (s_v, 1)); c_v += 1
    P.ins("vector", lambda e: e.reduce_sum(red[:, 1:2], acc[:, nfc_ch:320], axis=AX.X), (s_v, 1)); c_v += 1
    P.wait("tensor", s_v, c_v)
    P.ins("tensor", lambda e: e.matmul(pam[:, 0:1], d["ones_col_f_sb"], red[:, 0:1],
                                       start=True, stop=True), (s_t, 1)); c_t += 1
    P.ins("tensor", lambda e: e.matmul(pam[:, 1:2], d["ones_col_f_sb"], red[:, 1:2],
                                       start=True, stop=True), (s_t, 1)); c_t += 1
    NEL = 5120 * 2048
    P.wait("vector", s_t, c_t)
    P.ins("vector", lambda e: e.tensor_scalar(
        sml[:, 0:1], pam[:, 0:1], 1.0 / NEL, 1e-5, op0=ALU.mult, op1=ALU.max), (s_v, 1)); c_v += 1
    P.ins("vector", lambda e: e.tensor_scalar(
        sml[:, 1:2], pam[:, 1:2], 1.0 / NEL, 1e-5, op0=ALU.mult, op1=ALU.max), (s_v, 1)); c_v += 1
    P.wait("vector", s_v, c_v)
    P.ins("vector", lambda e: e.tensor_scalar(
        sml[:, 2:3], sml[:, 0:1], 0.7, None, op0=ALU.mult), (s_v, 1)); c_v += 1
    P.ins("vector", lambda e: e.tensor_scalar(
        sml[:, 3:4], sml[:, 0:1], -0.7, None, op0=ALU.mult), (s_v, 1)); c_v += 1
    P.ins("vector", lambda e: e.tensor_scalar(
        sml[:, 4:5], sml[:, 1:2], 0.7, None, op0=ALU.mult), (s_v, 1)); c_v += 1
    P.ins("vector", lambda e: e.tensor_scalar(
        sml[:, 5:6], sml[:, 1:2], -0.7, None, op0=ALU.mult), (s_v, 1)); c_v += 1
    P.ins("vector", lambda e: e.tensor_tensor(
        sml[:, 6:7], sml[:, 0:1], sml[:, 0:1], op=ALU.mult), (s_v, 1)); c_v += 1
    P.wait("vector", s_v, c_v)
    P.ins("vector", lambda e: e.tensor_tensor(
        sml[:, 6:7], sml[:, 6:7], sml[:, 1:2], op=ALU.mult), (s_v, 1)); c_v += 1
    P.wait("vector", s_v, c_v)
    P.ins("vector", lambda e: e.tensor_scalar(
        sml[:, 6:7], sml[:, 6:7], 0.5, None, op0=ALU.mult), (s_v, 1)); c_v += 1
    P.wait("tensor", s_v, c_v)
    P.ins("tensor", lambda e: e.matmul(pbc2[:, :], d["ones_row_sb"], sml[:, :],
                                       start=True, stop=True), (s_t, 1)); c_t += 1
    P.wait("vector", s_t, c_t)
    P.ins("vector", lambda e: e.tensor_copy(thr_sb[:, :], pbc2[:, :]), (s_v, 1)); c_v += 1
    P.wait("sync", s_v, c_v)
    P.ins("sync", lambda e: e.dma_start(out=c_col[:, :], in_=thr_sb[:, 6:7],
                                        allow_slow_non_contiguous=True), (s_st, 16)); c_st += 16
    P.wait("sync", s_st, c_st)


def plan_tern_quant(nc, P, st, sp, d, thr_sb, qfc_c, qproj_c):
    sb = lambda n, sh, dt=F32: st.enter_context(nc.sbuf_tensor("tq_" + n, sh, dt))
    sem = lambda n: sp.enter_context(nc.semaphore("tq_" + n))
    wt = [sb(f"wt{i}", [128, 512]) for i in range(2)]
    tb = [sb(f"tb{i}", [128, 512]) for i in range(2)]
    qt = [sb(f"qt{i}", [128, 512], BF16) for i in range(2)]
    s_l, s_g, s_v, s_st = sem("sl"), sem("sg"), sem("sv"), sem("sst")
    c_l = c_g = c_v = c_st = 0
    chunks = _tern_chunks(d)
    srcs = [d["wfc_r"], d["wpj_r"]]
    dsts = [qfc_c, qproj_c]
    for ci, (m, i, c0, w) in enumerate(chunks):
        sl = ci % 2
        tcol, ncol = (2, 3) if m == 0 else (4, 5)
        if ci >= 2:
            P.wait("sync", s_v, ci - 1)
        P.ins("sync", lambda e, sl=sl, m=m, i=i, c0=c0, w=w: e.dma_start(
            out=wt[sl][:, :], in_=srcs[m][i, :, c0:c0 + w]), (s_l, 16)); c_l += 16
        P.wait("vector", s_l, c_l)
        if ci >= 2:
            P.wait("vector", s_st, 16 * (ci - 1))
        P.ins("vector", lambda e, sl=sl, ncol=ncol: e.tensor_scalar(
            tb[sl][:, :], wt[sl][:, :], thr_sb[:, ncol:ncol + 1], None,
            op0=ALU.is_lt), (s_g, 1)); c_g += 1
        P.wait("vector", s_g, c_g)
        P.ins("vector", lambda e, sl=sl, tcol=tcol: e.scalar_tensor_tensor(
            qt[sl][:, :], wt[sl][:, :], thr_sb[:, tcol:tcol + 1], tb[sl][:, :],
            op0=ALU.is_gt, op1=ALU.subtract), (s_v, 1)); c_v += 1
        P.wait("sync", s_v, c_v)
        P.ins("sync", lambda e, sl=sl, m=m, i=i, c0=c0, w=w: e.dma_start(
            out=dsts[m][i, :, c0:c0 + w], in_=qt[sl][:, :]), (s_st, 16)); c_st += 16
    P.wait("sync", s_st, c_st)



def build_tern(nc, d, sp, qfc_c, qproj_c, c_col):
    pref = "tn_"
    sem = lambda n: sp.enter_context(nc.semaphore(pref + n))
    NEL = 5120 * 2048
    fc_ch = [(0, i, 0) for i in range(40)]
    pj_ch = [(1, i, c * 2048) for i in range(8) for c in range(5)]
    zipped = []
    for i in range(40):
        zipped.append(("f", i))
        zipped.append(("p", i))

    with ExitStack() as stA:
        thr = stA.enter_context(nc.sbuf_tensor(pref + "thr", [128, 8], F32, side="right"))
        with ExitStack() as st1:
            sb1 = lambda n, sh, dt=F32: st1.enter_context(nc.sbuf_tensor(pref + n, sh, dt))
            ps1 = lambda n, sh, dt=F32: st1.enter_context(nc.psum_tensor(pref + n, sh, dt))
            P = Plan()
            NB = 4
            wtf = [sb1(f"wtf{i}", [128, 2048]) for i in range(NB)]
            wtp = [sb1(f"wtp{i}", [128, 2048]) for i in range(NB)]
            accf = sb1("accf", [128, 40])
            accp = sb1("accp", [128, 40])
            red = sb1("red", [128, 2])
            sml = sb1("sml", [1, 8])
            tb = sb1("tb", [128, 2048])
            qtf = [sb1(f"qtf{i}", [128, 2048], BF16) for i in range(NB)]
            qtp = [sb1(f"qtp{i}", [128, 2048], BF16) for i in range(NB)]
            pam = ps1("pam", [1, 2])
            pbc2 = ps1("pbc2", [128, 8])
            s_lf = [sem(f"slf{i}") for i in range(NB)]
            s_lp = [sem(f"slp{i}") for i in range(NB)]
            s_vf, s_vp, s_t, s_st = sem("svf"), sem("svp"), sem("stt"), sem("sst")
            s_sf = [sem(f"ssf{i}") for i in range(NB)]
            s_sp = [sem(f"ssp{i}") for i in range(NB)]
            s_v = sem("sv")
            c_lf = [0] * NB; c_lp = [0] * NB; c_sf = [0] * NB; c_sp = [0] * NB
            c_vf = c_vp = c_t = c_st = c_v = 0
            vf_done = {}
            vp_done = {}
            for kind, i in zipped:
                if kind == "f":
                    sl = i % NB
                    m, ti, c0 = fc_ch[i]
                    if i >= NB:
                        P.wait("sync", s_vf, vf_done[i - NB])
                    P.ins("sync", lambda e, sl=sl, ti=ti, c0=c0: e.dma_start(
                        out=wtf[sl][:, :], in_=d["wfc_r"][ti, :, c0:c0 + 2048]), (s_lf[sl], 16)); c_lf[sl] += 16
                    P.wait("vector", s_lf[sl], c_lf[sl])
                    P.ins("vector", lambda e, sl=sl, i=i: e.reduce_sum(
                        accf[:, i:i + 1], wtf[sl][:, :], axis=AX.X,
                        apply_absolute_value=True), (s_vf, 1)); c_vf += 1
                    vf_done[i] = c_vf
                else:
                    sl = i % NB
                    m, ti, c0 = pj_ch[i]
                    if i >= NB:
                        P.wait("gpsimd", s_vp, vp_done[i - NB])
                    P.ins("gpsimd", lambda e, sl=sl, ti=ti, c0=c0: e.dma_start(
                        out=wtp[sl][:, :], in_=d["wpj_r"][ti, :, c0:c0 + 2048]), (s_lp[sl], 16)); c_lp[sl] += 16
                    P.wait("vector", s_lp[sl], c_lp[sl])
                    P.ins("vector", lambda e, sl=sl, i=i: e.reduce_sum(
                        accp[:, i:i + 1], wtp[sl][:, :], axis=AX.X,
                        apply_absolute_value=True), (s_vp, 1)); c_vp += 1
                    vp_done[i] = c_vp
            P.wait("vector", s_vf, c_vf)
            P.wait("vector", s_vp, c_vp)
            P.ins("vector", lambda e: e.reduce_sum(red[:, 0:1], accf[:, :], axis=AX.X), (s_v, 1)); c_v += 1
            P.ins("vector", lambda e: e.reduce_sum(red[:, 1:2], accp[:, :], axis=AX.X), (s_v, 1)); c_v += 1
            P.wait("tensor", s_v, c_v)
            P.ins("tensor", lambda e: e.matmul(pam[:, 0:1], d["ones_col_f_sb"], red[:, 0:1],
                                               start=True, stop=True), (s_t, 1)); c_t += 1
            P.ins("tensor", lambda e: e.matmul(pam[:, 1:2], d["ones_col_f_sb"], red[:, 1:2],
                                               start=True, stop=True), (s_t, 1)); c_t += 1
            P.wait("vector", s_t, c_t)
            P.ins("vector", lambda e: e.tensor_scalar(
                sml[:, 0:1], pam[:, 0:1], 1.0 / NEL, 1e-5, op0=ALU.mult, op1=ALU.max), (s_v, 1)); c_v += 1
            P.ins("vector", lambda e: e.tensor_scalar(
                sml[:, 1:2], pam[:, 1:2], 1.0 / NEL, 1e-5, op0=ALU.mult, op1=ALU.max), (s_v, 1)); c_v += 1
            P.wait("vector", s_v, c_v)
            P.ins("vector", lambda e: e.tensor_scalar(
                sml[:, 2:3], sml[:, 0:1], 0.7, None, op0=ALU.mult), (s_v, 1)); c_v += 1
            P.ins("vector", lambda e: e.tensor_scalar(
                sml[:, 3:4], sml[:, 0:1], -0.7, None, op0=ALU.mult), (s_v, 1)); c_v += 1
            P.ins("vector", lambda e: e.tensor_scalar(
                sml[:, 4:5], sml[:, 1:2], 0.7, None, op0=ALU.mult), (s_v, 1)); c_v += 1
            P.ins("vector", lambda e: e.tensor_scalar(
                sml[:, 5:6], sml[:, 1:2], -0.7, None, op0=ALU.mult), (s_v, 1)); c_v += 1
            P.ins("vector", lambda e: e.tensor_tensor(
                sml[:, 6:7], sml[:, 0:1], sml[:, 0:1], op=ALU.mult), (s_v, 1)); c_v += 1
            P.wait("vector", s_v, c_v)
            P.ins("vector", lambda e: e.tensor_tensor(
                sml[:, 6:7], sml[:, 6:7], sml[:, 1:2], op=ALU.mult), (s_v, 1)); c_v += 1
            P.wait("vector", s_v, c_v)
            P.ins("vector", lambda e: e.tensor_scalar(
                sml[:, 6:7], sml[:, 6:7], 0.5, None, op0=ALU.mult), (s_v, 1)); c_v += 1
            P.wait("tensor", s_v, c_v)
            P.ins("tensor", lambda e: e.matmul(pbc2[:, :], d["ones_row_sb"], sml[:, :],
                                               start=True, stop=True), (s_t, 1)); c_t += 1
            P.wait("vector", s_t, c_t)
            P.ins("vector", lambda e: e.tensor_copy(thr[:, :], pbc2[:, :]), (s_v, 1)); c_v += 1
            P.wait("sync", s_v, c_v)
            P.ins("sync", lambda e: e.dma_start(out=c_col[:, :], in_=thr[:, 6:7],
                                                allow_slow_non_contiguous=True), (s_st, 16)); c_st += 16
            s_lf2 = [sem(f"lf2{i}") for i in range(NB)]
            s_lp2 = [sem(f"lp2{i}") for i in range(NB)]
            c_lf2 = [0] * NB; c_lp2 = [0] * NB
            s_v2f, s_v2p = sem("v2f"), sem("v2p")
            c_v2f = c_v2p = 0
            v2f_done = {}
            v2p_done = {}
            first_q = True
            for kind, i in zipped:
                if kind == "f":
                    sl = i % NB
                    m, ti, c0 = fc_ch[i]
                    if i >= NB:
                        P.wait("sync", s_v2f, v2f_done[i - NB])
                        P.wait("sync", s_sf[sl], c_sf[sl])
                    P.ins("sync", lambda e, sl=sl, ti=ti, c0=c0: e.dma_start(
                        out=wtf[sl][:, :], in_=d["wfc_r"][ti, :, c0:c0 + 2048]), (s_lf2[sl], 16)); c_lf2[sl] += 16
                    P.wait("vector", s_lf2[sl], c_lf2[sl])
                    if first_q:
                        P.wait("vector", s_v, c_v)
                        first_q = False
                    P.ins("vector", lambda e, sl=sl: e.tensor_scalar(
                        tb[:, :], wtf[sl][:, :], thr[:, 3:4], None, op0=ALU.is_lt), (s_v2f, 1)); c_v2f += 1
                    P.wait("vector", s_v2f, c_v2f)
                    P.ins("vector", lambda e, sl=sl: e.scalar_tensor_tensor(
                        qtf[sl][:, :], wtf[sl][:, :], thr[:, 2:3], tb[:, :],
                        op0=ALU.is_gt, op1=ALU.subtract), (s_v2f, 1)); c_v2f += 1
                    v2f_done[i] = c_v2f
                    P.wait("sync", s_v2f, c_v2f)
                    P.ins("sync", lambda e, sl=sl, ti=ti, c0=c0: e.dma_start(
                        out=qfc_c[ti, :, c0:c0 + 2048], in_=qtf[sl][:, :]), (s_sf[sl], 16)); c_sf[sl] += 16
                else:
                    sl = i % NB
                    m, ti, c0 = pj_ch[i]
                    if i >= NB:
                        P.wait("gpsimd", s_v2p, v2p_done[i - NB])
                        P.wait("gpsimd", s_sp[sl], c_sp[sl])
                    P.ins("gpsimd", lambda e, sl=sl, ti=ti, c0=c0: e.dma_start(
                        out=wtp[sl][:, :], in_=d["wpj_r"][ti, :, c0:c0 + 2048]), (s_lp2[sl], 16)); c_lp2[sl] += 16
                    P.wait("vector", s_lp2[sl], c_lp2[sl])
                    if first_q:
                        P.wait("vector", s_v, c_v)
                        first_q = False
                    P.ins("vector", lambda e, sl=sl: e.tensor_scalar(
                        tb[:, :], wtp[sl][:, :], thr[:, 5:6], None, op0=ALU.is_lt), (s_v2p, 1)); c_v2p += 1
                    P.wait("vector", s_v2p, c_v2p)
                    P.ins("vector", lambda e, sl=sl: e.scalar_tensor_tensor(
                        qtp[sl][:, :], wtp[sl][:, :], thr[:, 4:5], tb[:, :],
                        op0=ALU.is_gt, op1=ALU.subtract), (s_v2p, 1)); c_v2p += 1
                    v2p_done[i] = c_v2p
                    P.wait("gpsimd", s_v2p, c_v2p)
                    P.ins("gpsimd", lambda e, sl=sl, ti=ti, c0=c0: e.dma_start(
                        out=qproj_c[ti, :, c0:c0 + 2048], in_=qtp[sl][:, :]), (s_sp[sl], 16)); c_sp[sl] += 16
            for i in range(NB):
                P.wait("sync", s_sf[i], c_sf[i])
                P.wait("gpsimd", s_sp[i], c_sp[i])
            P.wait("sync", s_st, c_st)
            P.emit(nc)



def _bf(x):
    return np.ascontiguousarray(x.astype(BF))


def prep_host(inputs):
    d = {}
    f32 = lambda x: np.ascontiguousarray(np.asarray(x, dtype=np.float32))

    gw = f32(inputs["L0_ffn_gate_w"])
    uw = f32(inputs["L0_ffn_up_w"])
    dw = f32(inputs["L0_ffn_down_w"])
    d["gw_r"] = np.ascontiguousarray(gw.reshape(44, 128, 16, 128).transpose(0, 3, 2, 1).reshape(44, 128, 2048))
    d["uw_r"] = np.ascontiguousarray(uw.reshape(44, 128, 16, 128).transpose(0, 3, 2, 1).reshape(44, 128, 2048))
    d["dw_r"] = np.ascontiguousarray(dw.T.reshape(44, 128, 8, 256).transpose(2, 1, 0, 3).reshape(8, 128, 44 * 256))
    d["gs128"] = np.ascontiguousarray(f32(inputs["L0_ffn_gate_s"]).reshape(44, 128).T)
    d["us128"] = np.ascontiguousarray(f32(inputs["L0_ffn_up_s"]).reshape(44, 128).T)
    d["dsT"] = _bf(np.tile(f32(inputs["L0_ffn_down_s"])[None, :], (128, 1)))

    wq = f32(inputs["L0_attn_Wq"])
    d["wq_r"] = np.ascontiguousarray(wq.reshape(16, 128, 16, 128).transpose(0, 3, 2, 1).reshape(16, 128, 2048))
    wkd = f32(inputs["L0_attn_Wk_down"])
    d["wkd_r"] = np.ascontiguousarray(wkd.T.reshape(16, 128, 256).transpose(1, 0, 2).reshape(128, 16 * 256))
    wku = f32(inputs["L0_attn_Wk_up"])
    d["wku_r"] = np.ascontiguousarray(wku.T.reshape(2, 128, 512).transpose(1, 0, 2).reshape(128, 1024))
    wvd = f32(inputs["L0_attn_Wv_down"])
    d["wvd_r"] = np.ascontiguousarray(wvd.T.reshape(16, 128, 384).transpose(1, 0, 2).reshape(128, 16 * 384))
    wvu = f32(inputs["L0_attn_Wv_up"])
    d["wvu_r"] = np.ascontiguousarray(wvu.T.reshape(3, 128, 512).transpose(1, 0, 2).reshape(128, 3 * 512))
    wo = f32(inputs["L0_attn_Wo"])
    d["wo_r"] = np.ascontiguousarray(wo.T.reshape(16, 128, 4, 512).transpose(2, 1, 0, 3).reshape(4, 128, 16 * 512))
    d["kanc"] = np.ascontiguousarray(f32(inputs["L0_attn_k_anchor"]).reshape(4, 128).T)
    d["vanc"] = np.ascontiguousarray(np.tile(f32(inputs["L0_attn_v_anchor"])[None, :], (128, 1)))

    inv = 1.0 / (10000.0 ** (np.arange(0, 64, dtype=np.float64) * 2.0 / 128.0))
    pos = np.arange(S, dtype=np.float64)
    ang = pos[None, :] * inv[:, None]
    cosT = np.concatenate([np.cos(ang), np.cos(ang)], axis=0).astype(np.float32)
    sinT = np.concatenate([np.sin(ang), -np.sin(ang)], axis=0).astype(np.float32)
    d["cosT"] = _bf(cosT)
    d["sinT"] = _bf(sinT)
    d["cosTM_r"] = np.ascontiguousarray(_bf(cosT.T).reshape(16, 128, 128))
    d["sinTM_r"] = np.ascontiguousarray(_bf(sinT.T).reshape(16, 128, 128))

    tri = (np.arange(128)[None, :] >= np.arange(128)[:, None])
    masks = np.zeros((4, 128, 512), dtype=np.float32)
    for j in range(4):
        masks[j, :, 128 * j:128 * (j + 1)] = tri
        masks[j, :, 128 * (j + 1):] = 1.0
    d["masks0"] = _bf(masks.reshape(4, 128, 512))

    m1 = np.zeros((12, 128, 512), dtype=np.float32)
    edge = (np.arange(128)[None, :] < np.arange(128)[:, None])
    for ktr in range(12):
        for jp in range(4):
            m = jp - ktr + 8
            if m == 0:
                m1[ktr, :, 128 * jp:128 * (jp + 1)] = tri
            elif m == 8:
                m1[ktr, :, 128 * jp:128 * (jp + 1)] = edge
            elif 0 < m < 8:
                m1[ktr, :, 128 * jp:128 * (jp + 1)] = 1.0
    d["masks1"] = _bf(m1)

    d["wqkv_r"] = np.ascontiguousarray(
        f32(inputs["L1_attn_Wqkv"]).reshape(16, 128, 12, 512).transpose(2, 1, 0, 3).reshape(12, 128, 16 * 512))
    d["wo1_r"] = np.ascontiguousarray(
        f32(inputs["L1_attn_Wo"]).reshape(16, 128, 4, 512).transpose(2, 1, 0, 3).reshape(4, 128, 16 * 512))
    d["qnw"] = _bf(np.tile(f32(inputs["L1_attn_q_norm_w"])[None, :], (128, 1)))
    d["knw"] = _bf(np.tile(f32(inputs["L1_attn_k_norm_w"])[None, :], (128, 1)))

    wfc = f32(inputs["L1_ffn_w_fc"])
    wpj = f32(inputs["L1_ffn_w_proj"])
    d["wfc_r"] = np.ascontiguousarray(wfc.reshape(40, 128, 16, 128).transpose(0, 3, 2, 1).reshape(40, 128, 2048))
    d["wpj_r"] = np.ascontiguousarray(wpj.T.reshape(40, 128, 8, 256).transpose(2, 1, 0, 3).reshape(8, 128, 40 * 256))

    d["identb"] = np.eye(128, dtype=BF)
    d["onescol"] = np.ones((128, 1), dtype=BF)
    d["onesrow"] = np.ones((1, 128), dtype=np.float32)
    return d



def _declare_common(nc, d, names):
    aps = {}
    for n in names:
        arr = d[n]
        dt = BF16 if arr.dtype == BF else F32
        aps[n] = nc.dram_tensor(n, list(arr.shape), dt, kind="ExternalInput").ap()
    return aps


_FUSED_CACHE = {}

ALL_W_NAMES = ["gw_r", "uw_r", "dw_r", "gs128", "us128", "dsT",
               "wq_r", "wkd_r", "wku_r", "wvd_r", "wvu_r", "wo_r",
               "kanc", "vanc", "cosT", "sinT", "masks0",
               "wfc_r", "wpj_r",
               "wqkv_r", "wo1_r", "qnw", "knw", "cosTM_r", "sinTM_r", "masks1",
               "identb"]


def _build_fused(d):
    nc = _mk_nc()
    aps = _declare_common(nc, d, ALL_W_NAMES)
    x_in = nc.dram_tensor("Xin", [S, HD], F32, kind="ExternalInput").ap()
    x_out = nc.dram_tensor("Xout", [S, HD], F32, kind="ExternalOutput").ap()
    XA = nc.dram_tensor("XA", [S, HD], F32).ap()
    XB = nc.dram_tensor("XB", [S, HD], F32).ap()
    dd = dict(aps)
    dd["gw_c"] = nc.dram_tensor("gw_c", [44, 128, 2048], BF16).ap()
    dd["uw_c"] = nc.dram_tensor("uw_c", [44, 128, 2048], BF16).ap()
    dd["dw_c"] = nc.dram_tensor("dw_c", [8, 128, 44 * 256], BF16).ap()
    dd["qfc_c"] = nc.dram_tensor("qfc_c", [40, 128, 2048], BF16).ap()
    dd["qproj_c"] = nc.dram_tensor("qproj_c", [8, 128, 40 * 256], BF16).ap()
    dd["c_col"] = nc.dram_tensor("c_col", [128, 1], F32).ap()
    qkv_s = nc.dram_tensor("qkv_s", [S, 6144], BF16).ap()
    qT_s = nc.dram_tensor("qT_s", [16, 128, 2048], BF16).ap()
    kT_s = nc.dram_tensor("kT_s", [16, 128, 2048], BF16).ap()
    _const_block(nc, d, aps, dd)

    def rst():
        nc.reset(
            previous_sbuf_base=nc.sbuf_base, previous_sbuf_top=nc.sbuf_top,
            previous_psum_base=nc.psum_base, previous_psum_top=nc.psum_top,
            previous_local_dram_base=nc.local_dram_base,
            previous_shared_dram_base=nc.shared_dram_base)

    build_ffn(nc, dd, ExitStack(), 0, 0, x_in, XA, have_cache=False)
    rst()
    build_attn0(nc, dd, ExitStack(), XA, XB)
    rst()
    build_ffn(nc, dd, ExitStack(), 0, 1, XB, XA, have_cache=True)
    rst()
    build_tern(nc, dd, ExitStack(), dd["qfc_c"], dd["qproj_c"], dd["c_col"])
    rst()
    build_ffn(nc, dd, ExitStack(), 1, 0, XA, XB, have_cache=True)
    rst()
    build_attn1(nc, dd, ExitStack(), XB, XA, qkv_s, qT_s, kT_s)
    rst()
    build_ffn(nc, dd, ExitStack(), 1, 1, XA, x_out, have_cache=True)
    return nc


def kernel(**inputs):
    d = prep_host(inputs)
    x_full = np.ascontiguousarray(np.asarray(inputs["L0_attn_hidden_states"], dtype=np.float32))
    B = x_full.shape[0]

    if "nc" not in _FUSED_CACHE:
        _FUSED_CACHE["nc"] = _build_fused(d)
    nc = _FUSED_CACHE["nc"]
    shared = {n: d[n] for n in ALL_W_NAMES}
    in_maps = []
    for b in range(B):
        m = dict(shared)
        m["Xin"] = np.ascontiguousarray(x_full[b])
        in_maps.append(m)
    res = run_bass_kernel_spmd(nc, in_maps, core_ids=list(range(B)))
    return np.stack([res.results[b]["Xout"] for b in range(B)], axis=0)


def _mk_nc():
    return bass.Bass("TRN2", target_bir_lowering=False)


def _const_block(nc, d, aps, dd):
    h = nc.sbuf_tensor("identb_sb", [128, 128], BF16).__enter__()
    cst = nc.sbuf_tensor("cst_sb", [128, 4], F32).__enter__()
    orow = nc.sbuf_tensor("onesrow_sb", [1, 128], F32).__enter__()
    ocol = nc.sbuf_tensor("onescol_sb", [128, 1], BF16).__enter__()
    ocolf = nc.sbuf_tensor("onescolf_sb", [128, 1], F32).__enter__()
    with nc.semaphore("cst_s") as s0, nc.Block() as block:
        @block.sync
        def _(e):
            e.dma_start(out=h[:, :], in_=aps["identb"][:, :]).then_inc(s0, 16)
        @block.vector
        def _(e):
            e.memset(cst[:, 0:1], EPS_LN)
            e.memset(cst[:, 1:2], EPS_RMS)
            e.memset(orow[:, :], 1.0)
            e.memset(ocol[:, :], 1.0)
            e.memset(ocolf[:, :], 1.0).then_inc(s0, 1)
            e.wait_ge(s0, 17)
    dd["identb_sb"] = h
    dd["eps_ln_col"] = cst[:, 0:1]
    dd["eps_rms_col"] = cst[:, 1:2]
    dd["ones_row_sb"] = orow[:, :]
    dd["ones_col_sb"] = ocol[:, :]
    dd["ones_col_f_sb"] = ocolf[:, :]


def _run_chain(d, xs, apps=("f0", "a0", "f0", "tern", "f1", "a1", "f1")):
    B = len(xs)
    cores = list(range(B))
    progs = {}

    def make_ffn0():
        nc = _mk_nc()
        names = ["gw_r", "uw_r", "dw_r", "gs128", "us128", "dsT", "identb"]
        aps = _declare_common(nc, d, names)
        X_in = nc.dram_tensor("Xin", [S, HD], F32, kind="ExternalInput").ap()
        X_out = nc.dram_tensor("Xout", [S, HD], F32, kind="ExternalOutput").ap()
        dd = dict(aps)
        dd["gw_c"] = nc.dram_tensor("gw_c", [44, 128, 2048], BF16).ap()
        dd["uw_c"] = nc.dram_tensor("uw_c", [44, 128, 2048], BF16).ap()
        dd["dw_c"] = nc.dram_tensor("dw_c", [8, 128, 44 * 256], BF16).ap()
        _const_block(nc, d, aps, dd)
        sp = ExitStack()
        build_ffn(nc, dd, sp, layer=0, app_idx=0, X_in=X_in, X_out=X_out, have_cache=False)
        return nc, names

    def make_attn0():
        nc = _mk_nc()
        names = ["wq_r", "wkd_r", "wku_r", "wvd_r", "wvu_r", "wo_r",
                 "kanc", "vanc", "cosT", "sinT", "masks0", "identb"]
        aps = _declare_common(nc, d, names)
        X_in = nc.dram_tensor("Xin", [S, HD], F32, kind="ExternalInput").ap()
        X_out = nc.dram_tensor("Xout", [S, HD], F32, kind="ExternalOutput").ap()
        dd = dict(aps)
        _const_block(nc, d, aps, dd)
        sp = ExitStack()
        build_attn0(nc, dd, sp, X_in, X_out)
        return nc, names

    def make_tern():
        nc = _mk_nc()
        names = ["wfc_r", "wpj_r", "identb"]
        aps = _declare_common(nc, d, names)
        qfc_c = nc.dram_tensor("qfc_c", [40, 128, 2048], BF16, kind="ExternalOutput").ap()
        qproj_c = nc.dram_tensor("qproj_c", [8, 128, 40 * 256], BF16, kind="ExternalOutput").ap()
        c_col = nc.dram_tensor("c_col", [128, 1], F32, kind="ExternalOutput").ap()
        dd = dict(aps)
        _const_block(nc, d, aps, dd)
        sp = ExitStack()
        build_tern(nc, dd, sp, qfc_c, qproj_c, c_col)
        return nc, names

    def make_ffn1():
        nc = _mk_nc()
        names = ["identb"]
        aps = _declare_common(nc, d, names)
        X_in = nc.dram_tensor("Xin", [S, HD], F32, kind="ExternalInput").ap()
        X_out = nc.dram_tensor("Xout", [S, HD], F32, kind="ExternalOutput").ap()
        dd = dict(aps)
        dd["qfc_c"] = nc.dram_tensor("qfc_c", [40, 128, 2048], BF16, kind="ExternalInput").ap()
        dd["qproj_c"] = nc.dram_tensor("qproj_c", [8, 128, 40 * 256], BF16, kind="ExternalInput").ap()
        dd["c_col"] = nc.dram_tensor("c_col", [128, 1], F32, kind="ExternalInput").ap()
        _const_block(nc, d, aps, dd)
        sp = ExitStack()
        build_ffn(nc, dd, sp, layer=1, app_idx=0, X_in=X_in, X_out=X_out, have_cache=True)
        return nc, names

    def make_attn1():
        nc = _mk_nc()
        names = ["wqkv_r", "wo1_r", "qnw", "knw", "cosTM_r", "sinTM_r", "masks1", "identb"]
        aps = _declare_common(nc, d, names)
        X_in = nc.dram_tensor("Xin", [S, HD], F32, kind="ExternalInput").ap()
        X_out = nc.dram_tensor("Xout", [S, HD], F32, kind="ExternalOutput").ap()
        dd = dict(aps)
        qkv_s = nc.dram_tensor("qkv_s", [S, 6144], BF16).ap()
        qT_s = nc.dram_tensor("qT_s", [16, 128, 2048], BF16).ap()
        kT_s = nc.dram_tensor("kT_s", [16, 128, 2048], BF16).ap()
        _const_block(nc, d, aps, dd)
        sp = ExitStack()
        build_attn1(nc, dd, sp, X_in, X_out, qkv_s, qT_s, kT_s)
        return nc, names

    MAKERS = {"f0": make_ffn0, "a0": make_attn0, "tern": make_tern,
              "f1": make_ffn1, "a1": make_attn1}
    tern_out = {}

    def run_app(key):
        nonlocal xs
        if key not in progs:
            progs[key] = MAKERS[key]()
        nc, names = progs[key]
        shared = {n: d[n] for n in names}
        if key == "tern":
            in_maps = [dict(shared) for _ in range(B)]
            res = run_bass_kernel_spmd(nc, in_maps, core_ids=cores)
            for k in ("qfc_c", "qproj_c", "c_col"):
                tern_out[k] = res.results[0][k]
            return
        if key == "f1":
            shared.update(tern_out)
        in_maps = []
        for b in range(B):
            m = dict(shared)
            m["Xin"] = np.ascontiguousarray(xs[b])
            in_maps.append(m)
        res = run_bass_kernel_spmd(nc, in_maps, core_ids=cores)
        xs = [res.results[b]["Xout"] for b in range(B)]

    for a in apps:
        if a in MAKERS:
            run_app(a)
    return xs


# version 25
# speedup vs baseline: 1.1143x; 1.1143x over previous
import numpy as np
import ml_dtypes
from contextlib import ExitStack

import concourse.bass as bass
import concourse.mybir as mybir
from concourse.bass_utils import run_bass_kernel_spmd

F32 = mybir.dt.float32
BF16 = mybir.dt.bfloat16
AF = mybir.ActivationFunctionType
ALU = mybir.AluOpType
AX = mybir.AxisListType
BF = ml_dtypes.bfloat16

S = 2048
HD = 2048
NTS = 16
KH = 16
NBLK = 4
EPS_LN = 1e-5
EPS_RMS = 1e-6


class Plan:
    ENGINES = ("sync", "gpsimd", "tensor", "vector", "scalar")

    def __init__(self):
        self.items = {e: [] for e in self.ENGINES}

    def add(self, eng, fn):
        self.items[eng].append(fn)

    def wait(self, eng, sem, val):
        if val > 0:
            self.items[eng].append(lambda e, s=sem, v=val: e.wait_ge(s, v))

    def ins(self, eng, fn, inc=None):
        def g(e, fn=fn, inc=inc):
            i = fn(e)
            if inc is not None:
                i.then_inc(inc[0], inc[1])
        self.items[eng].append(g)

    def splice(self, other, spread=(), start_frac=0.0):
        for eng in self.ENGINES:
            oi = other.items[eng]
            if not oi:
                continue
            if eng in spread:
                mine = self.items[eng]
                k0 = int(len(mine) * start_frac)
                tail = mine[k0:]
                merged = []
                n_t, n_o = len(tail), len(oi)
                ti = oi_i = 0
                for step in range(n_t + n_o):
                    if ti * n_o <= oi_i * n_t and ti < n_t:
                        merged.append(tail[ti]); ti += 1
                    elif oi_i < n_o:
                        merged.append(oi[oi_i]); oi_i += 1
                    else:
                        merged.append(tail[ti]); ti += 1
                self.items[eng] = mine[:k0] + merged
            else:
                self.items[eng].extend(oi)

    def emit(self, nc):
        with nc.Block() as block:
            for eng in self.ENGINES:
                fns = self.items[eng]
                if not fns:
                    continue
                def body(e, fns=fns):
                    for f in fns:
                        f(e)
                getattr(block, eng)(body)



class LNEmitter:

    def __init__(self, nc, P, st, sp, d, X_ap, xnT, pref):
        sb = lambda n, sh, dt=F32: st.enter_context(nc.sbuf_tensor(pref + n, sh, dt))
        ps = lambda n, sh, dt=F32: st.enter_context(nc.psum_tensor(pref + n, sh, dt))
        sem = lambda n: sp.enter_context(nc.semaphore(pref + n))
        self.nc, self.P, self.d = nc, P, d
        self.X_ap, self.xnT = X_ap, xnT
        self.xo = [sb(f"xo{i}", [128, 2048]) for i in range(2)]
        self.stats = sb("stats", [128, 24])
        self.mv = sb("mv", [128, 2])
        self.lnt = sb("lnt", [128, 2])
        self.xnb = [sb("xnb0", [128, 2048], BF16)] * 2
        self.pst = [ps(f"pst{i}", [128, 512], BF16) for i in range(2)]
        self.s_x = [sem("sx0"), sem("sx1")]
        self.s_v, self.s_s, self.s_t, self.s_e = (
            sem("sv"), sem("ss"), sem("st"), sem("se"))
        self.cx = [0, 0]
        self.cv = self.cs = self.ct = self.ce = 0
        self.v_ts_done = {}
        self.grp = 0

    def e_count(self, ts):
        return 4 * (ts + 1)

    def emit(self, ts):
        P, d = self.P, self.d
        xo, stats, mv, lnt, xnb, pst = (self.xo, self.stats, self.mv,
                                        self.lnt, self.xnb, self.pst)
        s_x, s_v, s_s, s_t, s_e = self.s_x, self.s_v, self.s_s, self.s_t, self.s_e
        X_ap, xnT = self.X_ap, self.xnT
        identb = d["identb_sb"]
        sl = ts % 2
        if ts >= 2:
            P.wait("sync", s_v, self.v_ts_done[ts - 2])
        P.ins("sync", lambda e, sl=sl, ts=ts: e.dma_start(
            out=xo[sl][:, :], in_=X_ap[ts * 128:(ts + 1) * 128, :]), (s_x[sl], 16))
        self.cx[sl] += 16
        P.wait("vector", s_x[sl], self.cx[sl])
        for j in range(4):
            P.ins("vector", lambda e, sl=sl, j=j: e.bn_stats(
                stats[:, 6 * j:6 * j + 6], xo[sl][:, 512 * j:512 * (j + 1)]), (s_v, 1))
            self.cv += 1
        P.wait("vector", s_v, self.cv)
        P.ins("vector", lambda e: e.bn_aggr(mv[:, :], stats[:, :]), (s_v, 1))
        self.cv += 1
        P.wait("scalar", s_v, self.cv)
        P.ins("scalar", lambda e: e.activation(
            lnt[:, 0:1], mv[:, 1:2], AF.Ln, bias=d["eps_ln_col"]), (s_s, 1))
        self.cs += 1
        P.wait("scalar", s_s, self.cs)
        P.ins("scalar", lambda e: e.activation(
            lnt[:, 1:2], lnt[:, 0:1], AF.Exp, scale=-0.5), (s_s, 1))
        self.cs += 1
        P.wait("vector", s_s, self.cs)
        if ts >= 1:
            P.wait("vector", s_t, 4 * ts)
        P.ins("vector", lambda e, sl=sl: e.tensor_scalar(
            xnb[sl][:, :], xo[sl][:, :], mv[:, 0:1], lnt[:, 1:2],
            op0=ALU.subtract, op1=ALU.mult), (s_v, 1))
        self.cv += 1
        self.v_ts_done[ts] = self.cv
        P.wait("tensor", s_v, self.cv)
        for g in range(4):
            psl = self.grp % 2
            if self.grp >= 2:
                P.wait("tensor", s_e, self.grp - 1)
            for j in range(4):
                dt = 4 * g + j
                P.ins("tensor", lambda e, sl=sl, psl=psl, dt=dt, j=j: e.transpose(
                    pst[psl][:, j * 128:(j + 1) * 128],
                    xnb[sl][:, dt * 128:(dt + 1) * 128], identb[:, :]),
                    (s_t, 1) if j == 3 else None)
            self.ct += 1
            P.wait("scalar", s_t, self.ct)
            dst = bass.AP(xnT.tensor if hasattr(xnT, "tensor") else xnT,
                          (4 * g) * 2048 + ts * 128,
                          [[16 * 2048, 128], [2048, 4], [1, 128]])
            P.ins("scalar", lambda e, psl=psl, dst=dst: e.copy(dst, pst[psl][:, :]), (s_e, 1))
            self.ce += 1
            self.grp += 1



def plan_ln(nc, P, st, sp, d, X_ap, xnT, pref):
    sb = lambda n, sh, dt=F32: st.enter_context(nc.sbuf_tensor(pref + n, sh, dt))
    ps = lambda n, sh, dt=F32: st.enter_context(nc.psum_tensor(pref + n, sh, dt))
    sem = lambda n: sp.enter_context(nc.semaphore(pref + n))

    xo = [sb(f"xo{i}", [128, 2048]) for i in range(2)]
    stats = sb("stats", [128, 24])
    mv = sb("mv", [128, 2])
    lnt = sb("lnt", [128, 2])
    xnb = [sb(f"xnb{i}", [128, 2048], BF16) for i in range(2)]
    pst = [ps(f"pst{i}", [128, 512], BF16) for i in range(2)]
    identb = d["identb_sb"]

    s_x = [sem("sx0"), sem("sx1")]
    s_v, s_s, s_t, s_e = sem("sv"), sem("ss"), sem("st"), sem("se")
    cx = [0, 0]
    cv = cs = ct = ce = 0
    v_ts_done = {}
    grp = 0

    for ts in range(NTS):
        sl = ts % 2
        if ts >= 2:
            P.wait("sync", s_v, v_ts_done[ts - 2])
        P.ins("sync", lambda e, sl=sl, ts=ts: e.dma_start(
            out=xo[sl][:, :], in_=X_ap[ts * 128:(ts + 1) * 128, :]), (s_x[sl], 16))
        cx[sl] += 16
        P.wait("vector", s_x[sl], cx[sl])
        for j in range(4):
            P.ins("vector", lambda e, sl=sl, j=j: e.bn_stats(
                stats[:, 6 * j:6 * j + 6], xo[sl][:, 512 * j:512 * (j + 1)]), (s_v, 1))
            cv += 1
        P.wait("vector", s_v, cv)
        P.ins("vector", lambda e: e.bn_aggr(mv[:, :], stats[:, :]), (s_v, 1))
        cv += 1
        P.wait("scalar", s_v, cv)
        P.ins("scalar", lambda e: e.activation(
            lnt[:, 0:1], mv[:, 1:2], AF.Ln, bias=d["eps_ln_col"]), (s_s, 1))
        cs += 1
        P.wait("scalar", s_s, cs)
        P.ins("scalar", lambda e: e.activation(
            lnt[:, 1:2], lnt[:, 0:1], AF.Exp, scale=-0.5), (s_s, 1))
        cs += 1
        P.wait("vector", s_s, cs)
        if ts >= 2:
            P.wait("vector", s_t, 4 * (ts - 1))
        P.ins("vector", lambda e, sl=sl: e.tensor_scalar(
            xnb[sl][:, :], xo[sl][:, :], mv[:, 0:1], lnt[:, 1:2],
            op0=ALU.subtract, op1=ALU.mult), (s_v, 1))
        cv += 1
        v_ts_done[ts] = cv
        P.wait("tensor", s_v, cv)
        for g in range(4):
            psl = grp % 2
            if grp >= 2:
                P.wait("tensor", s_e, grp - 1)
            for j in range(4):
                dt = 4 * g + j
                last = (j == 3)
                P.ins("tensor", lambda e, sl=sl, psl=psl, dt=dt, j=j: e.transpose(
                    pst[psl][:, j * 128:(j + 1) * 128],
                    xnb[sl][:, dt * 128:(dt + 1) * 128], identb[:, :]),
                    (s_t, 1) if last else None)
            ct += 1
            P.wait("scalar", s_t, ct)
            dst = bass.AP(xnT.tensor if hasattr(xnT, "tensor") else xnT,
                          (4 * g) * 2048 + ts * 128,
                          [[16 * 2048, 128], [2048, 4], [1, 128]])
            P.ins("scalar", lambda e, psl=psl, dst=dst: e.copy(dst, pst[psl][:, :]), (s_e, 1))
            ce += 1
            grp += 1



def build_ffn(nc, d, sp, layer, app_idx, X_in, X_out, have_cache, extra=None, x_eng="gpsimd"):
    pref = f"f{layer}_{app_idx}_"
    NI = 44 if layer == 0 else 40
    with ExitStack() as st:
        xnT = st.enter_context(nc.sbuf_tensor(pref + "xnT", [128, 16 * 2048], BF16))

        with ExitStack() as st2:
            sb2 = lambda n, sh, dt=F32: st2.enter_context(nc.sbuf_tensor(pref + n, sh, dt))
            ps2 = lambda n, sh, dt=F32: st2.enter_context(nc.psum_tensor(pref + n, sh, dt))
            sem2 = lambda n: sp.enter_context(nc.semaphore(pref + n))
            P = Plan()
            ln = LNEmitter(nc, P, st2, sp, d, X_in, xnT[:, :], pref + "ln_")

            gwb = [sb2(f"gwb{i}", [128, 16 * 128], BF16) for i in range(2)]
            if layer == 0:
                uwb = [sb2(f"uwb{i}", [128, 16 * 128], BF16) for i in range(2)]
                gbuf = [sb2(f"gbuf{i}", [128, 512]) for i in range(2)]
                gsil = gbuf
                ubuf = [sb2(f"ubuf{i}", [128, 512]) for i in range(2)]
                gs_sb = sb2("gs", [128, NI])
                us_sb = sb2("us", [128, NI])
                ds_sb = sb2("ds", [128, 2048], BF16)
                t1b = [sb2(f"t1b{i}", [128, 256]) for i in range(2)]
            else:
                rbuf = [sb2(f"rbuf{i}", [128, 512]) for i in range(2)]
                cc_sb = sb2("cc", [128, 1])
            hdn = sb2("hdn", [128, NI * 512], BF16)
            dwb = [sb2(f"dwb{i}", [128, NI * 256], BF16) for i in range(2)]
            xo2 = [sb2(f"xo2{i}", [128, 256]) for i in range(2)]
            xnew = [sb2(f"xnew{i}", [128, 256]) for i in range(2)]

            pg = [ps2(f"pg{i}", [128, 512]) for i in range(2)]
            pu = [ps2(f"pu{i}", [128, 512]) for i in range(2)]
            s_wg = [sem2(f"swg{i}") for i in range(2)]
            s_wu = [sem2(f"swu{i}") for i in range(2)]
            s_wd = [sem2(f"swd{i}") for i in range(2)]
            s_wgf = [sem2(f"swgf{i}") for i in range(2)]
            s_wuf = [sem2(f"swuf{i}") for i in range(2)]
            s_wdf = [sem2(f"swdf{i}") for i in range(2)]
            c_wgf = [0, 0]; c_wuf = [0, 0]; c_wdf = [0, 0]
            wg_rdy = {}; wu_rdy = {}; wd_rdy = {}
            s_mm = sem2("smm")
            s_ev = sem2("sev")
            s_hd = sem2("shd")
            s_sc = sem2("ssc")
            s_v2 = sem2("sv2")
            s_md = sem2("smd")
            s_x2 = sem2("sx2")
            s_xo2 = [sem2("sxo2a"), sem2("sxo2b")]
            s_xn = sem2("sxn")
            s_st = [sem2("ssta"), sem2("sstb")]
            s_cw = [sem2("scwa"), sem2("scwb")]
            s_cwd = [sem2("scwda"), sem2("scwdb")]

            c_wg = [0, 0]; c_wu = [0, 0]; c_wd = [0, 0]
            c_xo2 = [0, 0]; c_st = [0, 0]
            c_cw = [0, 0]; c_cwd = [0, 0]
            c_mm = c_ev = c_hd = c_sc = c_v2 = c_md = c_x2 = c_xn = 0
            mm_done_at_it = {}
            ev_done_at_it = {}
            md_total = 0
            xn_total = 0

            if layer == 0:
                P.ins("sync", lambda e: e.dma_start(out=gs_sb[:, :], in_=d["gs128"][:, :]), (s_x2, 16)); c_x2 += 16
                P.ins("sync", lambda e: e.dma_start(out=us_sb[:, :], in_=d["us128"][:, :]), (s_x2, 16)); c_x2 += 16
                P.ins("sync", lambda e: e.dma_start(out=ds_sb[:, :], in_=d["dsT"][:, :]), (s_x2, 16)); c_x2 += 16
                P.wait("vector", s_x2, c_x2)
                gW, uW, dW = d["gw_r"], d["uw_r"], d["dw_r"]
                gC, uC, dC = d["gw_c"], d["uw_c"], d["dw_c"]
            else:
                P.ins("sync", lambda e: e.dma_start(out=cc_sb[:, :], in_=d["c_col"][:, :]), (s_x2, 16)); c_x2 += 16
                P.wait("vector", s_x2, c_x2)
                gC, dC = d["qfc_c"], d["qproj_c"]

            for blk in range(NBLK):
                use_f32 = (layer == 0) and (not have_cache) and (blk == 0)
                for ts_ln in range(4 * blk, 4 * blk + 4):
                    ln.emit(ts_ln)
                for it in range(NI):
                    sl = it % 2
                    git = blk * NI + it
                    wload = "gpsimd" if use_f32 else "sync"
                    if git >= 2:
                        P.wait(wload, s_mm, mm_done_at_it[git - 2])
                    if use_f32:
                        if git >= 2:
                            P.wait(wload, s_cw[sl], c_cw[sl])
                        P.ins(wload, lambda e, sl=sl, it=it: e.dma_start(
                            out=gwb[sl][:, :], in_=gW[it, :, :]), (s_wgf[sl], 16))
                        c_wgf[sl] += 16
                        wg_rdy[git] = (s_wgf[sl], c_wgf[sl])
                        P.ins(wload, lambda e, sl=sl, it=it: e.dma_start(
                            out=uwb[sl][:, :], in_=uW[it, :, :]), (s_wuf[sl], 16))
                        c_wuf[sl] += 16
                        wu_rdy[git] = (s_wuf[sl], c_wuf[sl])
                        P.wait("sync", s_wgf[sl], c_wgf[sl])
                        P.ins("sync", lambda e, sl=sl, it=it: e.dma_start(
                            out=gC[it, :, :], in_=gwb[sl][:, :]), (s_cw[sl], 16))
                        c_cw[sl] += 16
                        P.wait("sync", s_wuf[sl], c_wuf[sl])
                        P.ins("sync", lambda e, sl=sl, it=it: e.dma_start(
                            out=uC[it, :, :], in_=uwb[sl][:, :]), (s_cw[sl], 16))
                        c_cw[sl] += 16
                    else:
                        if (layer == 0) and (not have_cache) and it == 0:
                            P.wait(wload, s_cw[0], c_cw[0])
                            P.wait(wload, s_cw[1], c_cw[1])
                        P.ins(wload, lambda e, sl=sl, it=it: e.dma_start(
                            out=gwb[sl][:, :], in_=gC[it, :, :]), (s_wg[sl], 16))
                        c_wg[sl] += 16
                        wg_rdy[git] = (s_wg[sl], c_wg[sl])
                        if layer == 0:
                            P.ins(wload, lambda e, sl=sl, it=it: e.dma_start(
                                out=uwb[sl][:, :], in_=uC[it, :, :]), (s_wu[sl], 16))
                            c_wu[sl] += 16
                            wu_rdy[git] = (s_wu[sl], c_wu[sl])
                    P.wait("tensor", *wg_rdy[git])
                    if layer == 0:
                        P.wait("tensor", *wu_rdy[git])
                    if it == 0:
                        P.wait("tensor", ln.s_e, ln.e_count(4 * blk + 3))
                    if git >= 2:
                        P.wait("tensor", s_ev, ev_done_at_it[git - 2])
                    if blk > 0 and it < 2:
                        P.wait("tensor", s_xn, xn_total)
                    for kh in range(KH):
                        P.ins("tensor", lambda e, sl=sl, kh=kh, blk=blk: e.matmul(
                            pg[sl][:, :], gwb[sl][:, kh * 128:(kh + 1) * 128],
                            xnT[:, kh * 2048 + blk * 512: kh * 2048 + (blk + 1) * 512],
                            start=(kh == 0), stop=(kh == KH - 1)),
                            (s_mm, 1) if (kh == KH - 1 and layer != 0) else None)
                    if layer == 0:
                        for kh in range(KH):
                            P.ins("tensor", lambda e, sl=sl, kh=kh, blk=blk: e.matmul(
                                pu[sl][:, :], uwb[sl][:, kh * 128:(kh + 1) * 128],
                                xnT[:, kh * 2048 + blk * 512: kh * 2048 + (blk + 1) * 512],
                                start=(kh == 0), stop=(kh == KH - 1)),
                                (s_mm, 1) if kh == KH - 1 else None)
                    c_mm += 1
                    mm_done_at_it[git] = c_mm
                    if layer == 0:
                        P.wait("vector", s_mm, c_mm)
                        if git >= 2:
                            P.wait("vector", s_sc, git - 1)
                        P.ins("vector", lambda e, sl=sl, it=it: e.tensor_scalar(
                            gbuf[sl][:, :], pg[sl][:, :], gs_sb[:, it:it + 1], None,
                            op0=ALU.mult), (s_ev, 1))
                        c_ev += 1; ev_gbuf = c_ev
                        P.ins("vector", lambda e, sl=sl, it=it: e.tensor_scalar(
                            ubuf[sl][:, :], pu[sl][:, :], us_sb[:, it:it + 1], None,
                            op0=ALU.mult), (s_ev, 1))
                        c_ev += 1
                        ev_done_at_it[git] = c_ev
                        P.wait("scalar", s_ev, ev_gbuf)
                        if git >= 2:
                            P.wait("scalar", s_hd, git - 1)
                        P.ins("scalar", lambda e, sl=sl: e.activation(
                            gsil[sl][:, :], gbuf[sl][:, :], AF.Silu), (s_sc, 1))
                        c_sc += 1
                        P.wait("vector", s_sc, c_sc)
                        P.wait("vector", s_ev, c_ev)
                        if blk > 0 and it == 0:
                            P.wait("vector", s_md, md_total)
                        P.ins("vector", lambda e, sl=sl, it=it: e.tensor_tensor(
                            hdn[:, it * 512:(it + 1) * 512], gsil[sl][:, :],
                            ubuf[sl][:, :], op=ALU.mult), (s_hd, 1))
                        c_hd += 1
                    else:
                        P.wait("scalar", s_mm, c_mm)
                        if git >= 2:
                            P.wait("scalar", s_hd, git - 1)
                        P.ins("scalar", lambda e, sl=sl: e.activation(
                            rbuf[sl][:, :], pg[sl][:, :], AF.Relu), (s_ev, 1))
                        c_ev += 1
                        ev_done_at_it[git] = c_ev
                        P.wait("vector", s_ev, c_ev)
                        if blk > 0 and it == 0:
                            P.wait("vector", s_md, md_total)
                        P.ins("vector", lambda e, sl=sl, it=it: e.tensor_tensor(
                            hdn[:, it * 512:(it + 1) * 512], rbuf[sl][:, :],
                            rbuf[sl][:, :], op=ALU.mult), (s_hd, 1))
                        c_hd += 1

                use_f32_d = (layer == 0) and (not have_cache) and (blk == 0)
                for hc2 in range(8):
                    dload = "gpsimd" if use_f32_d else "sync"
                    sld = hc2 % 2
                    ghc = blk * 8 + hc2
                    if ghc >= 2:
                        P.wait(dload, s_md, (ghc - 1) * 4)
                    if use_f32_d:
                        if ghc >= 2:
                            P.wait(dload, s_cwd[sld], c_cwd[sld])
                        P.ins(dload, lambda e, sld=sld, hc2=hc2: e.dma_start(
                            out=dwb[sld][:, :], in_=dW[hc2, :, :]), (s_wdf[sld], 16))
                        c_wdf[sld] += 16
                        wd_rdy[ghc] = (s_wdf[sld], c_wdf[sld])
                        P.wait("sync", s_wdf[sld], c_wdf[sld])
                        P.ins("sync", lambda e, sld=sld, hc2=hc2: e.dma_start(
                            out=dC[hc2, :, :], in_=dwb[sld][:, :]), (s_cwd[sld], 16))
                        c_cwd[sld] += 16
                    else:
                        if (layer == 0) and (not have_cache) and hc2 == 0:
                            P.wait(dload, s_cwd[0], c_cwd[0])
                            P.wait(dload, s_cwd[1], c_cwd[1])
                        P.ins(dload, lambda e, sld=sld, hc2=hc2: e.dma_start(
                            out=dwb[sld][:, :], in_=dC[hc2, :, :]), (s_wd[sld], 16))
                        c_wd[sld] += 16
                        wd_rdy[ghc] = (s_wd[sld], c_wd[sld])
                    for ts in range(4):
                        gmd = blk * 32 + hc2 * 4 + ts
                        slp = gmd % 2
                        slx = gmd % 2
                        row0 = blk * 512 + ts * 128
                        col0 = hc2 * 256
                        if gmd >= 2:
                            P.wait(x_eng, s_xn, gmd - 1)
                        P.ins(x_eng, lambda e, slx=slx, row0=row0, col0=col0: e.dma_start(
                            out=xo2[slx][:, :], in_=X_in[row0:row0 + 128, col0:col0 + 256]),
                            (s_xo2[slx], 16))
                        c_xo2[slx] += 16
                        P.wait("tensor", *wd_rdy[ghc])
                        if hc2 == 0 and ts == 0:
                            P.wait("tensor", s_hd, c_hd)
                            P.wait("tensor", s_ev, c_ev)
                        if gmd >= 2:
                            P.wait("tensor", s_xn, gmd - 1)
                        for it in range(NI):
                            P.ins("tensor", lambda e, slp=slp, sld=sld, it=it, ts=ts: e.matmul(
                                pg[slp][:, 0:256],
                                hdn[:, it * 512 + ts * 128: it * 512 + (ts + 1) * 128],
                                dwb[sld][:, it * 256:(it + 1) * 256],
                                start=(it == 0), stop=(it == NI - 1)),
                                (s_md, 1) if it == NI - 1 else None)
                        c_md += 1
                        P.wait("vector", s_md, c_md)
                        P.wait("vector", s_xo2[slx], c_xo2[slx])
                        if gmd >= 2:
                            P.wait("vector", s_st[slx], c_st[slx])
                        if layer == 0:
                            P.ins("vector", lambda e, slp=slp, slx=slx, col0=col0: e.tensor_tensor(
                                t1b[slx][:, :], pg[slp][:, 0:256],
                                ds_sb[:, col0:col0 + 256], op=ALU.mult), (s_v2, 1))
                            c_v2 += 1
                            P.wait("vector", s_v2, c_v2)
                            P.ins("vector", lambda e, slx=slx: e.scalar_tensor_tensor(
                                xnew[slx][:, :], t1b[slx][:, :], 0.5, xo2[slx][:, :],
                                op0=ALU.mult, op1=ALU.add), (s_xn, 1))
                        else:
                            P.ins("vector", lambda e, slp=slp, slx=slx: e.scalar_tensor_tensor(
                                xnew[slx][:, :], pg[slp][:, 0:256], cc_sb[:, 0:1],
                                xo2[slx][:, :], op0=ALU.mult, op1=ALU.add), (s_xn, 1))
                        c_xn += 1
                        P.wait(x_eng, s_xn, c_xn)
                        P.ins(x_eng, lambda e, slx=slx, row0=row0, col0=col0: e.dma_start(
                            out=X_out[row0:row0 + 128, col0:col0 + 256], in_=xnew[slx][:, :]),
                            (s_st[slx], 16))
                        c_st[slx] += 16
                md_total = c_md
                xn_total = c_xn

            if extra is not None:
                extra(nc, P, st2, sp)
            P.wait(x_eng, s_st[0], c_st[0])
            P.wait(x_eng, s_st[1], c_st[1])
            for i in range(2):
                if c_cw[i]:
                    P.wait("sync", s_cw[i], c_cw[i])
                if c_cwd[i]:
                    P.wait("sync", s_cwd[i], c_cwd[i])
            P.emit(nc)



SC0 = 0.08838834764831845

def build_attn0(nc, d, sp, X_in, X_out, extra=None):
    pref = "a0_"
    sem = lambda n: sp.enter_context(nc.semaphore(pref + n))
    with ExitStack() as stA:
        sbA = lambda n, sh, dt=F32: stA.enter_context(nc.sbuf_tensor(pref + n, sh, dt, side="right"))
        KT = sbA("KT", [128, 4 * 2048], BF16)
        V = sbA("V", [128, 16 * 512], BF16)
        cosb = sbA("cosb", [128, 2048], BF16)
        sinb = sbA("sinb", [128, 2048], BF16)
        mk0 = sbA("mk0", [128, 4 * 512], BF16)

        with ExitStack() as stB:
            sbB = lambda n, sh, dt=F32: stB.enter_context(nc.sbuf_tensor(pref + n, sh, dt))
            xnT = sbB("xnT", [128, 16 * 2048], BF16)

            with ExitStack() as st1:
                P = Plan()
                plan_ln(nc, P, st1, sp, d, X_in, xnT[:, :], pref + "ln_")
                P.emit(nc)

            with ExitStack() as st2:
                sb2 = lambda n, sh, dt=F32: st2.enter_context(nc.sbuf_tensor(pref + n, sh, dt))
                ps2 = lambda n, sh, dt=F32: st2.enter_context(nc.psum_tensor(pref + n, sh, dt))
                P = Plan()
                wkd = sb2("wkd", [128, 16 * 256], BF16)
                wku = sb2("wku", [128, 2 * 512], BF16)
                wvd = sb2("wvd", [128, 16 * 384], BF16)
                wvu = sb2("wvu", [128, 3 * 512], BF16)
                kdT = sb2("kdT", [128, 2 * 2048], BF16)
                vdT = sb2("vdT", [128, 3 * 2048], BF16)
                kanc = sb2("kanc", [128, 4])
                vanc = sb2("vanc", [128, 512])
                kf = [sb2(f"kf{i}", [128, 512]) for i in range(2)]
                rt1 = sb2("rt1", [128, 512])
                rt2 = sb2("rt2", [128, 512])
                pk = [ps2(f"pk{i}", [128, 512]) for i in range(2)]

                s_wk, s_ck = sem("swk"), sem("sck")
                s_t1, s_e1, s_v1 = sem("st1"), sem("se1"), sem("sv1")
                c_wk = c_ck = c_t1 = c_e1 = c_v1 = 0
                pk_free = [None, None]

                P.ins("gpsimd", lambda e: e.dma_start(out=wkd[:, :], in_=d["wkd_r"][:, :]), (s_wk, 16)); c_wk += 16
                P.ins("gpsimd", lambda e: e.dma_start(out=wku[:, :], in_=d["wku_r"][:, :]), (s_wk, 16)); c_wk += 16
                P.ins("gpsimd", lambda e: e.dma_start(out=wvd[:, :], in_=d["wvd_r"][:, :]), (s_wk, 16)); c_wk += 16
                P.ins("gpsimd", lambda e: e.dma_start(out=wvu[:, :], in_=d["wvu_r"][:, :]), (s_wk, 16)); c_wk += 16
                P.ins("sync", lambda e: e.dma_start(out=kanc[:, :], in_=d["kanc"][:, :]), (s_ck, 16)); c_ck += 16
                P.ins("sync", lambda e: e.dma_start(out=vanc[:, :], in_=d["vanc"][:, :]), (s_ck, 16)); c_ck += 16
                P.ins("sync", lambda e: e.dma_start(out=cosb[:, :], in_=d["cosT"][:, :]), (s_ck, 16)); c_ck += 16
                P.ins("sync", lambda e: e.dma_start(out=sinb[:, :], in_=d["sinT"][:, :]), (s_ck, 16)); c_ck += 16
                for j in range(4):
                    P.ins("sync", lambda e, j=j: e.dma_start(
                        out=mk0[:, j * 512:(j + 1) * 512], in_=d["masks0"][j, :, :]), (s_ck, 16)); c_ck += 16

                g = 0
                P.wait("tensor", s_wk, c_wk)
                for rt in range(2):
                    for tc in range(4):
                        psl = g % 2
                        if pk_free[psl] is not None:
                            P.wait("tensor", *pk_free[psl])
                        for kh in range(KH):
                            P.ins("tensor", lambda e, psl=psl, kh=kh, rt=rt, tc=tc: e.matmul(
                                pk[psl][:, :], wkd[:, kh * 256 + rt * 128: kh * 256 + rt * 128 + 128],
                                xnT[:, kh * 2048 + tc * 512: kh * 2048 + (tc + 1) * 512],
                                start=(kh == 0), stop=(kh == KH - 1)),
                                (s_t1, 1) if kh == KH - 1 else None)
                        c_t1 += 1
                        P.wait("scalar", s_t1, c_t1)
                        P.ins("scalar", lambda e, psl=psl, rt=rt, tc=tc: e.copy(
                            kdT[:, rt * 2048 + tc * 512: rt * 2048 + (tc + 1) * 512],
                            pk[psl][:, :]), (s_e1, 1))
                        c_e1 += 1
                        pk_free[psl] = (s_e1, c_e1)
                        g += 1
                P.wait("tensor", s_wk, c_wk)
                for rt in range(3):
                    for tc in range(4):
                        psl = g % 2
                        P.wait("tensor", *pk_free[psl])
                        for kh in range(KH):
                            P.ins("tensor", lambda e, psl=psl, kh=kh, rt=rt, tc=tc: e.matmul(
                                pk[psl][:, :], wvd[:, kh * 384 + rt * 128: kh * 384 + rt * 128 + 128],
                                xnT[:, kh * 2048 + tc * 512: kh * 2048 + (tc + 1) * 512],
                                start=(kh == 0), stop=(kh == KH - 1)),
                                (s_t1, 1) if kh == KH - 1 else None)
                        c_t1 += 1
                        P.wait("scalar", s_t1, c_t1)
                        P.ins("scalar", lambda e, psl=psl, rt=rt, tc=tc: e.copy(
                            vdT[:, rt * 2048 + tc * 512: rt * 2048 + (tc + 1) * 512],
                            pk[psl][:, :]), (s_e1, 1))
                        c_e1 += 1
                        pk_free[psl] = (s_e1, c_e1)
                        g += 1
                P.wait("tensor", s_wk, c_wk)
                P.wait("vector", s_ck, c_ck)
                for dk in range(4):
                    for tc in range(4):
                        psl = g % 2
                        kfs = g % 2
                        P.wait("tensor", *pk_free[psl])
                        for rt in range(2):
                            P.ins("tensor", lambda e, psl=psl, rt=rt, dk=dk, tc=tc: e.matmul(
                                pk[psl][:, :], wku[:, rt * 512 + dk * 128: rt * 512 + dk * 128 + 128],
                                kdT[:, rt * 2048 + tc * 512: rt * 2048 + (tc + 1) * 512],
                                start=(rt == 0), stop=(rt == 1)),
                                (s_t1, 1) if rt == 1 else None)
                        c_t1 += 1
                        P.wait("vector", s_t1, c_t1)
                        P.ins("vector", lambda e, psl=psl, kfs=kfs, dk=dk: e.tensor_scalar(
                            kf[kfs][:, :], pk[psl][:, :], kanc[:, dk:dk + 1], None,
                            op0=ALU.add), (s_v1, 1))
                        c_v1 += 1
                        pk_free[psl] = (s_v1, c_v1)
                        P.wait("vector", s_v1, c_v1)
                        cs_ = lambda t: slice(t * 512, (t + 1) * 512)
                        P.ins("vector", lambda e, kfs=kfs, tc=tc: e.tensor_tensor(
                            rt1[:, :], kf[kfs][:, :], cosb[:, tc * 512:(tc + 1) * 512],
                            op=ALU.mult), (s_v1, 1)); c_v1 += 1
                        P.ins("vector", lambda e, kfs=kfs, tc=tc: e.tensor_tensor(
                            rt2[0:64, :], kf[kfs][64:128, :], sinb[64:128, tc * 512:(tc + 1) * 512],
                            op=ALU.mult), (s_v1, 1)); c_v1 += 1
                        P.ins("vector", lambda e, kfs=kfs, tc=tc: e.tensor_tensor(
                            rt2[64:128, :], kf[kfs][0:64, :], sinb[0:64, tc * 512:(tc + 1) * 512],
                            op=ALU.mult), (s_v1, 1)); c_v1 += 1
                        P.wait("vector", s_v1, c_v1)
                        P.ins("vector", lambda e, dk=dk, tc=tc: e.tensor_tensor(
                            KT[:, dk * 2048 + tc * 512: dk * 2048 + (tc + 1) * 512],
                            rt1[:, :], rt2[:, :], op=ALU.add), (s_v1, 1)); c_v1 += 1
                        g += 1
                for ts in range(NTS):
                    psl = g % 2
                    P.wait("tensor", *pk_free[psl])
                    for rt in range(3):
                        P.ins("tensor", lambda e, psl=psl, rt=rt, ts=ts: e.matmul(
                            pk[psl][:, :], vdT[:, rt * 2048 + ts * 128: rt * 2048 + ts * 128 + 128],
                            wvu[:, rt * 512:(rt + 1) * 512],
                            start=(rt == 0), stop=(rt == 2)),
                            (s_t1, 1) if rt == 2 else None)
                    c_t1 += 1
                    P.wait("vector", s_t1, c_t1)
                    P.ins("vector", lambda e, psl=psl, ts=ts: e.tensor_tensor(
                        V[:, ts * 512:(ts + 1) * 512], pk[psl][:, :], vanc[:, :],
                        op=ALU.add), (s_v1, 1))
                    c_v1 += 1
                    pk_free[psl] = (s_v1, c_v1)
                    g += 1
                P.emit(nc)

            QO = stA.enter_context(nc.sbuf_tensor(pref + "QO", [128, 16 * 2048], BF16, side="right"))
            with ExitStack() as st3:
                sb3 = lambda n, sh, dt=F32: st3.enter_context(nc.sbuf_tensor(pref + n, sh, dt))
                ps3 = lambda n, sh, dt=F32: st3.enter_context(nc.psum_tensor(pref + n, sh, dt))
                P = Plan()
                wq = [sb3(f"wq{i}", [128, 2048], BF16) for i in range(2)]
                expT = sb3("expT", [128, 4 * 512], BF16)
                rt1 = sb3("rt1b", [128, 512])
                rt2 = sb3("rt2b", [128, 512])
                rs = sb3("rs", [1, 512])
                bc = sb3("bc", [128, 512])
                pq = ps3("pq", [128, 512])
                pss = [ps3(f"pss{i}", [128, 512]) for i in range(2)]
                po = [ps3(f"po{i}", [128, 512]) for i in range(2)]
                psum_s = [ps3(f"psum_s{i}", [1, 512]) for i in range(2)]
                pbc = ps3("pbc", [128, 512])

                s_wq = [sem("swq0"), sem("swq1")]
                s_tq, s_vq = sem("stq"), sem("svq")
                s_ts, s_se, s_vm, s_tp = sem("sts"), sem("sse"), sem("svm"), sem("stp")
                s_vr, s_tb, s_vb, s_qo = sem("svr"), sem("stb"), sem("svb"), sem("sqo")
                c_wq = [0, 0]
                c_tq = c_vq = c_ts = c_se = c_vm = c_tp = c_vr = c_tb = c_vb = c_qo = 0
                tq_done = {}
                vq_done = {}
                pq_free = None
                pss_free = [None, None]
                expT_free = [None] * 4
                po_free = [None, None]
                pbc_free = None
                pending_norm = None
                nslot = 0

                def flush_norm():
                    nonlocal c_vr, c_tb, c_vb, c_qo, pbc_free, pending_norm
                    if pending_norm is None:
                        return
                    ph, pqc, pslot, ptp = pending_norm
                    pending_norm = None
                    P.wait("vector", s_tp, ptp)
                    P.ins("vector", lambda e, pslot=pslot: e.reciprocal(
                        rs[:, :], psum_s[pslot][:, :]), (s_vr, 1))
                    c_vr += 1
                    P.wait("tensor", s_vr, c_vr)
                    if pbc_free is not None:
                        P.wait("tensor", *pbc_free)
                    P.ins("tensor", lambda e: e.matmul(
                        pbc[:, :], d["ones_row_sb"], rs[:, :], start=True, stop=True), (s_tb, 1))
                    c_tb += 1
                    P.wait("vector", s_tb, c_tb)
                    P.ins("vector", lambda e: e.tensor_copy(bc[:, :], pbc[:, :]), (s_vb, 1))
                    c_vb += 1
                    pbc_free = (s_vb, c_vb)
                    P.wait("vector", s_vb, c_vb)
                    P.ins("vector", lambda e, ph=ph, pqc=pqc, pslot=pslot: e.tensor_tensor(
                        QO[:, ph * 2048 + pqc * 512: ph * 2048 + (pqc + 1) * 512],
                        po[pslot][:, :], bc[:, :], op=ALU.mult), (s_qo, 1))
                    c_qo += 1
                    po_free[pslot] = (s_qo, c_qo)

                for h in range(17):
                    if h < 16:
                        slw = h % 2
                        if h >= 2:
                            P.wait("gpsimd", s_tq, tq_done[h - 2])
                        P.ins("gpsimd", lambda e, slw=slw, h=h: e.dma_start(
                            out=wq[slw][:, :], in_=d["wq_r"][h, :, :]), (s_wq[slw], 16))
                        c_wq[slw] += 16
                    for step in range(4):
                        if h < 16:
                            slw = h % 2
                            tc = step
                            if step == 0:
                                P.wait("tensor", s_wq[slw], c_wq[slw])
                            if pq_free is not None:
                                P.wait("tensor", *pq_free)
                            for kh in range(KH):
                                P.ins("tensor", lambda e, kh=kh, slw=slw, tc=tc: e.matmul(
                                    pq[:, :], wq[slw][:, kh * 128:(kh + 1) * 128],
                                    xnT[:, kh * 2048 + tc * 512: kh * 2048 + (tc + 1) * 512],
                                    start=(kh == 0), stop=(kh == KH - 1)),
                                    (s_tq, 1) if kh == KH - 1 else None)
                            c_tq += 1
                            P.wait("vector", s_tq, c_tq)
                            P.ins("vector", lambda e, tc=tc: e.tensor_tensor(
                                rt1[:, :], pq[:, :], cosb[:, tc * 512:(tc + 1) * 512],
                                op=ALU.mult), (s_vq, 1)); c_vq += 1
                            P.ins("vector", lambda e, tc=tc: e.tensor_tensor(
                                rt2[0:64, :], pq[64:128, :], sinb[64:128, tc * 512:(tc + 1) * 512],
                                op=ALU.mult), (s_vq, 1)); c_vq += 1
                            P.ins("vector", lambda e, tc=tc: e.tensor_tensor(
                                rt2[64:128, :], pq[0:64, :], sinb[0:64, tc * 512:(tc + 1) * 512],
                                op=ALU.mult), (s_vq, 1)); c_vq += 1
                            P.wait("vector", s_vq, c_vq)
                            P.ins("vector", lambda e, h=h, tc=tc: e.tensor_tensor(
                                QO[:, h * 2048 + tc * 512: h * 2048 + (tc + 1) * 512],
                                rt1[:, :], rt2[:, :], op=ALU.add), (s_vq, 1)); c_vq += 1
                            pq_free = (s_vq, c_vq)
                            vq_done[(h, tc)] = c_vq
                            if step == 3:
                                tq_done[h] = c_tq
                        if h >= 1:
                            ah = h - 1
                            kv = ah // 4
                            qc = step
                            nkt = 4 * qc + 4
                            rdy = {}
                            def emit_pv(k, ah=ah, qc=qc, nkt=nkt, nslot=None):
                                nonlocal c_tp
                                kv_ = ah // 4
                                P.wait("tensor", *rdy[k])
                                if k == 0 and po_free[nslot] is not None:
                                    P.wait("tensor", *po_free[nslot])
                                P.ins("tensor", lambda e, k=k, kv_=kv_, nslot=nslot, nkt=nkt: e.matmul(
                                    po[nslot][:, :], V[:, k * 512 + kv_ * 128: k * 512 + kv_ * 128 + 128],
                                    expT[:, (k % 4) * 512:(k % 4 + 1) * 512],
                                    start=(k == 0), stop=(k == nkt - 1)))
                                P.ins("tensor", lambda e, k=k, nslot=nslot, nkt=nkt: e.matmul(
                                    psum_s[nslot][:, :], d["ones_col_sb"],
                                    expT[:, (k % 4) * 512:(k % 4 + 1) * 512],
                                    start=(k == 0), stop=(k == nkt - 1)), (s_tp, 1))
                                c_tp += 1
                                expT_free[k % 4] = (s_tp, c_tp)
                            for kt in range(nkt):
                                psl = kt % 2
                                if kt == 0:
                                    P.wait("tensor", s_vq, vq_done[(ah, qc)])
                                if kt == 2:
                                    flush_norm()
                                if pss_free[psl] is not None:
                                    P.wait("tensor", *pss_free[psl])
                                P.ins("tensor", lambda e, psl=psl, kv=kv, kt=kt, ah=ah, qc=qc: e.matmul(
                                    pss[psl][:, :], KT[:, kv * 2048 + kt * 128: kv * 2048 + kt * 128 + 128],
                                    QO[:, ah * 2048 + qc * 512: ah * 2048 + (qc + 1) * 512],
                                    start=True, stop=True), (s_ts, 1))
                                c_ts += 1
                                P.wait("scalar", s_ts, c_ts)
                                if expT_free[kt % 4] is not None:
                                    P.wait("scalar", *expT_free[kt % 4])
                                P.ins("scalar", lambda e, psl=psl, kt=kt: e.activation(
                                    expT[:, (kt % 4) * 512:(kt % 4 + 1) * 512], pss[psl][:, :],
                                    AF.Exp, scale=SC0), (s_se, 1))
                                c_se += 1
                                pss_free[psl] = (s_se, c_se)
                                rdy[kt] = (s_se, c_se)
                                if kt >= 4 * qc:
                                    j = kt - 4 * qc
                                    P.wait("vector", s_se, c_se)
                                    P.ins("vector", lambda e, kt=kt, j=j: e.tensor_tensor(
                                        expT[:, (kt % 4) * 512:(kt % 4 + 1) * 512],
                                        expT[:, (kt % 4) * 512:(kt % 4 + 1) * 512],
                                        mk0[:, j * 512:(j + 1) * 512], op=ALU.mult), (s_vm, 1))
                                    c_vm += 1
                                    rdy[kt] = (s_vm, c_vm)
                                if kt >= 1:
                                    emit_pv(kt - 1, nslot=nslot)
                            emit_pv(nkt - 1, nslot=nslot)
                            pending_norm = (ah, qc, nslot, c_tp)
                            nslot = 1 - nslot
                flush_norm()
                if extra is not None:
                    extra(nc, P, st3, sp)
                P.emit(nc)

        with ExitStack() as st4:
            sb4 = lambda n, sh, dt=F32: st4.enter_context(nc.sbuf_tensor(pref + n, sh, dt))
            ps4 = lambda n, sh, dt=F32: st4.enter_context(nc.psum_tensor(pref + n, sh, dt))
            P = Plan()
            wo = [sb4(f"wo{i}", [128, 16 * 512], BF16) for i in range(2)]
            xo3 = [sb4(f"xo3{i}", [128, 512]) for i in range(2)]
            xn3 = [sb4(f"xn3{i}", [128, 512]) for i in range(2)]
            py = [ps4(f"py{i}", [128, 512]) for i in range(2)]
            s_wo = [sem("swo0"), sem("swo1")]
            s_xo = [sem("sxoa"), sem("sxob")]
            s_sy = [sem("ssya"), sem("ssyb")]
            s_ty, s_vy = sem("sty"), sem("svy")
            c_wo = [0, 0]
            c_xo = [0, 0]
            c_sy = [0, 0]
            c_ty = c_vy = 0
            for hc in range(4):
                slw = hc % 2
                if hc >= 2:
                    P.wait("gpsimd", s_ty, (hc - 1) * 16)
                P.ins("gpsimd", lambda e, slw=slw, hc=hc: e.dma_start(
                    out=wo[slw][:, :], in_=d["wo_r"][hc, :, :]), (s_wo[slw], 16))
                c_wo[slw] += 16
                for ts in range(NTS):
                    g = hc * 16 + ts
                    psl = g % 2
                    slx = g % 2
                    if g >= 2:
                        P.wait("sync", s_vy, g - 1)
                    P.ins("sync", lambda e, slx=slx, ts=ts, hc=hc: e.dma_start(
                        out=xo3[slx][:, :], in_=X_in[ts * 128:(ts + 1) * 128, hc * 512:(hc + 1) * 512]),
                        (s_xo[slx], 16))
                    c_xo[slx] += 16
                    P.wait("tensor", s_wo[slw], c_wo[slw])
                    if g >= 2:
                        P.wait("tensor", s_vy, g - 1)
                    for dt in range(16):
                        P.ins("tensor", lambda e, psl=psl, slw=slw, dt=dt, ts=ts: e.matmul(
                            py[psl][:, :], QO[:, dt * 2048 + ts * 128: dt * 2048 + ts * 128 + 128],
                            wo[slw][:, dt * 512:(dt + 1) * 512],
                            start=(dt == 0), stop=(dt == 15)),
                            (s_ty, 1) if dt == 15 else None)
                    c_ty += 1
                    P.wait("vector", s_ty, c_ty)
                    P.wait("vector", s_xo[slx], c_xo[slx])
                    if g >= 2:
                        P.wait("vector", s_sy[slx], c_sy[slx])
                    P.ins("vector", lambda e, psl=psl, slx=slx: e.tensor_tensor(
                        xn3[slx][:, :], py[psl][:, :], xo3[slx][:, :], op=ALU.add), (s_vy, 1))
                    c_vy += 1
                    P.wait("sync", s_vy, c_vy)
                    P.ins("sync", lambda e, slx=slx, ts=ts, hc=hc: e.dma_start(
                        out=X_out[ts * 128:(ts + 1) * 128, hc * 512:(hc + 1) * 512], in_=xn3[slx][:, :]),
                        (s_sy[slx], 16))
                    c_sy[slx] += 16
            P.wait("sync", s_sy[0], c_sy[0])
            P.wait("sync", s_sy[1], c_sy[1])
            P.emit(nc)



def build_attn1(nc, d, sp, X_in, X_out, qkv_s, qT_s, kT_s):
    pref = "a1_"
    sem = lambda n: sp.enter_context(nc.semaphore(pref + n))

    with ExitStack() as stB:
        xnT = stB.enter_context(nc.sbuf_tensor(pref + "xnT", [128, 16 * 2048], BF16))
        with ExitStack() as st1:
            P = Plan()
            plan_ln(nc, P, st1, sp, d, X_in, xnT[:, :], pref + "ln_")
            P.emit(nc)
        with ExitStack() as st2:
            sb2 = lambda n, sh, dt=F32: st2.enter_context(nc.sbuf_tensor(pref + n, sh, dt))
            ps2 = lambda n, sh, dt=F32: st2.enter_context(nc.psum_tensor(pref + n, sh, dt))
            P = Plan()
            wqv = [sb2(f"wqv{i}", [128, 16 * 512], BF16) for i in range(2)]
            qb = [sb2(f"qb{i}", [128, 512], BF16) for i in range(2)]
            pqk = [ps2(f"pqk{i}", [128, 512]) for i in range(2)]
            s_w = [sem("sw0"), sem("sw1")]
            s_t, s_e = sem("sqt"), sem("sqe")
            s_st = [sem("sqsa"), sem("sqsb")]
            c_w = [0, 0]; c_st = [0, 0]; c_t = c_e = 0
            for jc in range(12):
                sl = jc % 2
                if jc >= 2:
                    P.wait("gpsimd", s_t, (jc - 1) * 16)
                P.ins("gpsimd", lambda e, sl=sl, jc=jc: e.dma_start(
                    out=wqv[sl][:, :], in_=d["wqkv_r"][jc, :, :]), (s_w[sl], 16))
                c_w[sl] += 16
                P.wait("tensor", s_w[sl], c_w[sl])
                for ts in range(NTS):
                    g = jc * 16 + ts
                    psl = g % 2
                    if g >= 2:
                        P.wait("tensor", s_e, g - 1)
                    for kh in range(KH):
                        P.ins("tensor", lambda e, psl=psl, kh=kh, sl=sl, ts=ts: e.matmul(
                            pqk[psl][:, :], xnT[:, kh * 2048 + ts * 128: kh * 2048 + ts * 128 + 128],
                            wqv[sl][:, kh * 512:(kh + 1) * 512],
                            start=(kh == 0), stop=(kh == KH - 1)),
                            (s_t, 1) if kh == KH - 1 else None)
                    c_t += 1
                    P.wait("scalar", s_t, c_t)
                    if g >= 2:
                        P.wait("scalar", s_st[g % 2], c_st[g % 2])
                    P.ins("scalar", lambda e, psl=psl, g=g: e.copy(
                        qb[g % 2][:, :], pqk[psl][:, :]), (s_e, 1))
                    c_e += 1
                    P.wait("sync", s_e, c_e)
                    P.ins("sync", lambda e, g=g, ts=ts, jc=jc: e.dma_start(
                        out=qkv_s[ts * 128:(ts + 1) * 128, jc * 512:(jc + 1) * 512],
                        in_=qb[g % 2][:, :]), (s_st[g % 2], 16))
                    c_st[g % 2] += 16
            P.wait("sync", s_st[0], c_st[0])
            P.wait("sync", s_st[1], c_st[1])
            P.emit(nc)

    with ExitStack() as st3:
        sb3 = lambda n, sh, dt=F32: st3.enter_context(nc.sbuf_tensor(pref + n, sh, dt))
        ps3 = lambda n, sh, dt=F32: st3.enter_context(nc.psum_tensor(pref + n, sh, dt))
        P = Plan()
        qkt = [sb3(f"qkt{i}", [128, 4096], BF16) for i in range(2)]
        qnw = sb3("qnw", [128, 2048], BF16)
        knw = sb3("knw", [128, 2048], BF16)
        ctm = sb3("ctm", [128, 16 * 128], BF16)
        stm = sb3("stm", [128, 16 * 128], BF16)
        sqd = sb3("q2sqd", [128, 2048])
        lnt = sb3("q2lnt", [128, 8])
        qn = [sb3(f"qn{i}", [128, 2048], BF16) for i in range(2)]
        rt1 = [sb3(f"q2rt1{i}", [128, 2048], BF16) for i in range(2)]
        rth = [sb3(f"q2rth{i}", [128, 2048], BF16) for i in range(2)]
        qrb = [sb3(f"qrb{i}", [128, 2048], BF16) for i in range(2)]
        qTs = [sb3(f"qTs{i}", [128, 2048], BF16) for i in range(2)]
        pst = [ps3(f"q2pst{i}", [128, 512], BF16) for i in range(8)]
        identb = d["identb_sb"]
        pending_ev = None
        s_l, s_v, s_s, s_t, s_e = (sem("r_l"), sem("r_v"), sem("r_s"),
                                   sem("r_t"), sem("r_e"))
        s_lq = [sem("r_lqa"), sem("r_lqb")]
        s_st = [sem("r_sta"), sem("r_stb")]
        c_lq = [0, 0]; c_stk = [0, 0]
        c_l = c_v = c_s = c_t = c_e = c_g = 0
        stt_v = {}
        s_done = {}
        P.ins("sync", lambda e: e.dma_start(out=qnw[:, :], in_=d["qnw"][:, :]), (s_l, 16)); c_l += 16
        P.ins("sync", lambda e: e.dma_start(out=knw[:, :], in_=d["knw"][:, :]), (s_l, 16)); c_l += 16
        for ts in range(16):
            P.ins("sync", lambda e, ts=ts: e.dma_start(
                out=ctm[:, ts * 128:(ts + 1) * 128], in_=d["cosTM_r"][ts, :, :]), (s_l, 16)); c_l += 16
            P.ins("sync", lambda e, ts=ts: e.dma_start(
                out=stm[:, ts * 128:(ts + 1) * 128], in_=d["sinTM_r"][ts, :, :]), (s_l, 16)); c_l += 16
        grp = 0
        done_l = {}
        for ts in range(NTS):
            sl = ts % 2
            if ts >= 2:
                P.wait("sync", s_v, done_l[ts - 2])
                P.wait("sync", s_s, s_done[ts - 2])
            P.ins("sync", lambda e, sl=sl, ts=ts: e.dma_start(
                out=qkt[sl][:, :], in_=qkv_s[ts * 128:(ts + 1) * 128, 0:4096]), (s_lq[sl], 16))
            c_lq[sl] += 16
            for kind in range(2):
                off = kind * 2048
                nw = qnw if kind == 0 else knw
                dstT = qT_s if kind == 0 else kT_s
                lcol = kind * 4
                P.wait("scalar", s_l, c_l)
                P.wait("scalar", s_lq[sl], c_lq[sl])
                if ts >= 1:
                    P.wait("scalar", s_v, stt_v[(ts - 1, kind)])
                P.ins("scalar", lambda e, sl=sl, off=off, lcol=lcol: e.activation(
                    sqd[:, :], qkt[sl][:, off:off + 2048], AF.Square,
                    accum_out=lnt[:, lcol:lcol + 1]), (s_s, 1)); c_s += 1
                P.wait("scalar", s_s, c_s)
                P.ins("scalar", lambda e, lcol=lcol: e.activation(
                    lnt[:, lcol + 2:lcol + 3], lnt[:, lcol:lcol + 1], AF.Ln,
                    bias=d["eps_rms_col"], scale=1.0 / 2048.0), (s_s, 1)); c_s += 1
                P.wait("scalar", s_s, c_s)
                P.ins("scalar", lambda e, lcol=lcol: e.activation(
                    lnt[:, lcol + 3:lcol + 4], lnt[:, lcol + 2:lcol + 3], AF.Exp,
                    scale=-0.5), (s_s, 1)); c_s += 1
                cosAP = bass.AP(ctm[:, :].tensor, ts * 128, [[16 * 128, 128], [0, 16], [1, 128]])
                sinHI = bass.AP(stm[:, :].tensor, ts * 128 + 64, [[16 * 128, 128], [0, 16], [1, 64]])
                sinLO = bass.AP(stm[:, :].tensor, ts * 128, [[16 * 128, 128], [0, 16], [1, 64]])
                qnLO = bass.AP(qn[kind][:, :].tensor, 0, [[2048, 128], [128, 16], [1, 64]])
                qnHI = bass.AP(qn[kind][:, :].tensor, 64, [[2048, 128], [128, 16], [1, 64]])
                rthLO = bass.AP(rth[kind][:, :].tensor, 0, [[2048, 128], [128, 16], [1, 64]])
                rthHI = bass.AP(rth[kind][:, :].tensor, 64, [[2048, 128], [128, 16], [1, 64]])
                P.wait("vector", s_s, c_s)
                P.ins("vector", lambda e, sl=sl, off=off, nw=nw, lcol=lcol, kind=kind: e.scalar_tensor_tensor(
                    qn[kind][:, :], qkt[sl][:, off:off + 2048], lnt[:, lcol + 3:lcol + 4], nw[:, :],
                    op0=ALU.mult, op1=ALU.mult), (s_v, 1)); c_v += 1
                stt_v[(ts, kind)] = c_v
                P.wait("vector", s_v, c_v)
                P.ins("vector", lambda e, cosAP=cosAP, kind=kind: e.tensor_tensor(
                    rt1[kind][:, :], qn[kind][:, :], cosAP, op=ALU.mult), (s_v, 1)); c_v += 1
                P.ins("vector", lambda e, rthLO=rthLO, qnHI=qnHI, sinHI=sinHI: e.tensor_tensor(
                    rthLO, qnHI, sinHI, op=ALU.mult), (s_v, 1)); c_v += 1
                P.ins("vector", lambda e, rthHI=rthHI, qnLO=qnLO, sinLO=sinLO: e.tensor_tensor(
                    rthHI, qnLO, sinLO, op=ALU.mult), (s_v, 1)); c_v += 1
                P.wait("vector", s_v, c_v)
                if grp >= 2:
                    P.wait("vector", s_t, 4 * (grp - 1))
                P.ins("vector", lambda e, kind=kind: e.tensor_tensor(
                    qrb[kind][:, :], rt1[kind][:, :], rth[kind][:, :], op=ALU.add), (s_v, 1)); c_v += 1
                P.wait("tensor", s_v, c_v)
                tsl = kind
                if grp >= 2:
                    P.wait("tensor", s_st[tsl], c_stk[tsl])
                for g4 in range(4):
                    psl = (grp * 4 + g4) % 8
                    if grp * 4 + g4 >= 8:
                        P.wait("tensor", s_e, grp * 4 + g4 - 7)
                    for j in range(4):
                        dt = 4 * g4 + j
                        P.ins("tensor", lambda e, psl=psl, dt=dt, j=j, kind=kind: e.transpose(
                            pst[psl][:, j * 128:(j + 1) * 128],
                            qrb[kind][:, dt * 128:(dt + 1) * 128], identb[:, :]),
                            (s_t, 1) if j == 3 else None)
                    c_t += 1
                this_ev = []
                for g4 in range(4):
                    psl = (grp * 4 + g4) % 8
                    tcnt = c_t - 3 + g4
                    this_ev.append((psl, g4, tsl, tcnt))
                this_store = (ts, tsl, dstT, grp)
                if pending_ev is not None:
                    pev, pstore = pending_ev
                    pgrp = pstore[3]
                    ptsl0 = pstore[1]
                    if pgrp >= 2:
                        P.wait("scalar", s_st[ptsl0], c_stk[ptsl0])
                    for (ppsl, pg4, ptsl, ptcnt) in pev:
                        P.wait("scalar", s_t, ptcnt)
                        P.ins("scalar", lambda e, ppsl=ppsl, pg4=pg4, ptsl=ptsl: e.copy(
                            qTs[ptsl][:, pg4 * 512:(pg4 + 1) * 512], pst[ppsl][:, :]), (s_e, 1))
                        c_e += 1
                    P.wait("sync", s_e, c_e)
                    pts, ptsl, pdstT = pstore[:3]
                    P.ins("sync", lambda e, pts=pts, ptsl=ptsl, pdstT=pdstT: e.dma_start(
                        out=pdstT[pts, :, :], in_=qTs[ptsl][:, :]), (s_st[ptsl], 16))
                    c_stk[ptsl] += 16
                pending_ev = (this_ev, this_store)
                grp += 1
            done_l[ts] = c_v
            s_done[ts] = c_s
        if pending_ev is not None:
            pev, pstore = pending_ev
            pgrp = pstore[3]
            ptsl0 = pstore[1]
            if pgrp >= 2:
                P.wait("scalar", s_st[ptsl0], c_stk[ptsl0])
            for (ppsl, pg4, ptsl, ptcnt) in pev:
                P.wait("scalar", s_t, ptcnt)
                P.ins("scalar", lambda e, ppsl=ppsl, pg4=pg4, ptsl=ptsl: e.copy(
                    qTs[ptsl][:, pg4 * 512:(pg4 + 1) * 512], pst[ppsl][:, :]), (s_e, 1))
                c_e += 1
            P.wait("sync", s_e, c_e)
            pts, ptsl, pdstT = pstore[:3]
            P.ins("sync", lambda e, pts=pts, ptsl=ptsl, pdstT=pdstT: e.dma_start(
                out=pdstT[pts, :, :], in_=qTs[ptsl][:, :]), (s_st[ptsl], 16))
            c_stk[ptsl] += 16
        P.wait("sync", s_st[0], c_stk[0])
        P.wait("sync", s_st[1], c_stk[1])
        P.emit(nc)

    with ExitStack() as stO:
        outQ = stO.enter_context(nc.sbuf_tensor(pref + "outQ", [128, 16 * 2048], BF16, side="right"))
        with ExitStack() as st4:
            sb4 = lambda n, sh, dt=F32: st4.enter_context(nc.sbuf_tensor(pref + n, sh, dt))
            ps4 = lambda n, sh, dt=F32: st4.enter_context(nc.psum_tensor(pref + n, sh, dt))
            P = Plan()
            KTr = sb4("KTr", [128, 12 * 2048], BF16)
            Vr = sb4("Vr", [128, 12 * 2048], BF16)
            qTq = sb4("qTq", [128, 4 * 2048], BF16)
            mk1 = sb4("mk1", [128, 12 * 512], BF16)
            expT = sb4("a1expT", [128, 3 * 512], BF16)
            eacc = sb4("a1eacc", [128, 512])
            rs = sb4("a1rs", [1, 512])
            bc = sb4("a1bc", [128, 512])
            pss = [ps4(f"a1pss{i}", [128, 512]) for i in range(3)]
            po = [ps4(f"a1po{i}", [128, 512]) for i in range(2)]
            psum_s = [ps4(f"a1ps{i}", [1, 512]) for i in range(2)]
            pbc = ps4("a1pbc", [128, 512])
            s_kl = [sem("w_kl0"), sem("w_kl1")]
            s_ql = [sem("w_ql0"), sem("w_ql1")]
            s_ts, s_se, s_vm, s_tp = sem("w_ts"), sem("w_se"), sem("w_vm"), sem("w_tp")
            s_va, s_sm = sem("w_va"), sem("w_sm")
            s_vr, s_tb, s_vb, s_qo = sem("w_vr"), sem("w_tb"), sem("w_vb"), sem("w_qo")
            c_kl = [0, 0]; c_ql = [0, 0]
            c_ts = c_se = c_vm = c_tp = c_vr = c_tb = c_vb = c_qo = 0
            c_va = c_sm = 0
            psum_s_free = None
            acc_read = [None] * 3
            for j in range(12):
                P.ins("sync", lambda e, j=j: e.dma_start(
                    out=mk1[:, j * 512:(j + 1) * 512], in_=d["masks1"][j, :, :]), (s_kl[0], 16))
                c_kl[0] += 16
            pss_free = [None, None, None]
            expT_free = [None, None, None]
            po_free = [None, None]
            pbc_free = None
            pending_norm = None
            nslot = 0
            tp_at_qc_end = {}

            def flush_norm():
                nonlocal c_vr, c_tb, c_vb, c_qo, pbc_free, pending_norm
                if pending_norm is None:
                    return
                ph, pqc, pslot, ptp = pending_norm
                pending_norm = None
                P.wait("vector", s_tp, ptp)
                P.ins("vector", lambda e, pslot=pslot: e.reciprocal(
                    rs[:, :], psum_s[pslot][:, :]), (s_vr, 1))
                c_vr += 1
                P.wait("tensor", s_vr, c_vr)
                if pbc_free is not None:
                    P.wait("tensor", *pbc_free)
                P.ins("tensor", lambda e: e.matmul(
                    pbc[:, :], d["ones_row_sb"], rs[:, :], start=True, stop=True), (s_tb, 1))
                c_tb += 1
                P.wait("vector", s_tb, c_tb)
                P.ins("vector", lambda e: e.tensor_copy(bc[:, :], pbc[:, :]), (s_vb, 1))
                c_vb += 1
                pbc_free = (s_vb, c_vb)
                P.wait("vector", s_vb, c_vb)
                P.ins("vector", lambda e, ph=ph, pqc=pqc, pslot=pslot: e.tensor_tensor(
                    outQ[:, ph * 2048 + pqc * 512: ph * 2048 + (pqc + 1) * 512],
                    po[pslot][:, :], bc[:, :], op=ALU.mult), (s_qo, 1))
                c_qo += 1
                po_free[pslot] = (s_qo, c_qo)
            for qc in range(4):
                qp = qc % 2
                for kt in range(4 * qc, 4 * qc + 4):
                    slot = kt % 12
                    if qc >= 3:
                        P.wait("sync", s_tp, tp_at_qc_end[qc - 1])
                    P.ins("sync", lambda e, kt=kt, slot=slot: e.dma_start(
                        out=KTr[:, slot * 2048:(slot + 1) * 2048], in_=kT_s[kt, :, :]), (s_kl[qp], 16))
                    c_kl[qp] += 16
                    P.ins("sync", lambda e, kt=kt, slot=slot: e.dma_start(
                        out=Vr[:, slot * 2048:(slot + 1) * 2048],
                        in_=qkv_s[kt * 128:(kt + 1) * 128, 4096:6144]), (s_kl[qp], 16))
                    c_kl[qp] += 16
                if qc >= 1:
                    P.wait("sync", s_ts, c_ts)
                for j in range(4):
                    P.ins("sync", lambda e, j=j, qc=qc: e.dma_start(
                        out=qTq[:, j * 2048:(j + 1) * 2048], in_=qT_s[4 * qc + j, :, :]), (s_ql[qp], 16))
                    c_ql[qp] += 16
                kt_lo = max(0, 4 * qc - 8)
                kts = list(range(kt_lo, 4 * qc + 4))
                for h in range(16):
                    rdy = {}
                    def emit_pv(k, h=h, qc=qc, kts=kts, nslot=None):
                        nonlocal c_tp
                        P.wait("tensor", *rdy[k])
                        if k == kts[0] and po_free[nslot] is not None:
                            P.wait("tensor", *po_free[nslot])
                        P.ins("tensor", lambda e, k=k, h=h, nslot=nslot: e.matmul(
                            po[nslot][:, :], Vr[:, (k % 12) * 2048 + h * 128: (k % 12) * 2048 + h * 128 + 128],
                            expT[:, (k % 3) * 512:(k % 3 + 1) * 512],
                            start=(k == kts[0]), stop=(k == kts[-1])))
                        P.ins("tensor", lambda e, k=k, nslot=nslot: e.matmul(
                            psum_s[nslot][:, :], d["ones_col_sb"],
                            expT[:, (k % 3) * 512:(k % 3 + 1) * 512],
                            start=(k == kts[0]), stop=(k == kts[-1])), (s_tp, 1))
                        c_tp += 1
                        expT_free[k % 3] = (s_tp, c_tp)
                    for ki, kt in enumerate(kts):
                        ktr = kt + 8 - 4 * qc
                        psl = kt % 3
                        if h == 0 and ki == 0:
                            P.wait("tensor", s_kl[0], c_kl[0])
                            P.wait("tensor", s_kl[1], c_kl[1])
                            P.wait("tensor", s_ql[qc % 2], c_ql[qc % 2])
                        if ki == 2:
                            flush_norm()
                        if pss_free[psl] is not None:
                            P.wait("tensor", *pss_free[psl])
                        rhsq = bass.AP(qTq[:, :].tensor, h * 128, [[4 * 2048, 128], [2048, 4], [1, 128]])
                        P.ins("tensor", lambda e, psl=psl, kt=kt, h=h, rhsq=rhsq: e.matmul(
                            pss[psl][:, :],
                            KTr[:, (kt % 12) * 2048 + h * 128: (kt % 12) * 2048 + h * 128 + 128],
                            rhsq, start=True, stop=True), (s_ts, 1))
                        c_ts += 1
                        P.wait("scalar", s_ts, c_ts)
                        if expT_free[kt % 3] is not None:
                            P.wait("scalar", *expT_free[kt % 3])
                        P.ins("scalar", lambda e, psl=psl, kt=kt: e.activation(
                            expT[:, (kt % 3) * 512:(kt % 3 + 1) * 512], pss[psl][:, :],
                            AF.Exp, scale=SC0), (s_se, 1))
                        c_se += 1
                        pss_free[psl] = (s_se, c_se)
                        rdy[kt] = (s_se, c_se)
                        if not (4 <= ktr <= 7):
                            P.wait("vector", s_se, c_se)
                            P.ins("vector", lambda e, kt=kt, ktr=ktr: e.tensor_tensor(
                                expT[:, (kt % 3) * 512:(kt % 3 + 1) * 512],
                                expT[:, (kt % 3) * 512:(kt % 3 + 1) * 512],
                                mk1[:, ktr * 512:(ktr + 1) * 512], op=ALU.mult), (s_vm, 1))
                            c_vm += 1
                            rdy[kt] = (s_vm, c_vm)
                        if ki >= 1:
                            emit_pv(kts[ki - 1], nslot=nslot)
                    emit_pv(kts[-1], nslot=nslot)
                    pending_norm = (h, qc, nslot, c_tp)
                    nslot = 1 - nslot
                tp_at_qc_end[qc] = c_tp
            flush_norm()
            P.emit(nc)

        with ExitStack() as st5:
            sb5 = lambda n, sh, dt=F32: st5.enter_context(nc.sbuf_tensor(pref + n, sh, dt))
            ps5 = lambda n, sh, dt=F32: st5.enter_context(nc.psum_tensor(pref + n, sh, dt))
            P = Plan()
            wo = [sb5(f"a1wo{i}", [128, 16 * 512], BF16) for i in range(2)]
            xo3 = [sb5(f"a1xo{i}", [128, 512]) for i in range(2)]
            xn3 = [sb5(f"a1xn{i}", [128, 512]) for i in range(2)]
            py = [ps5(f"a1py{i}", [128, 512]) for i in range(2)]
            s_wo = [sem("o_w0"), sem("o_w1")]
            s_xo = [sem("o_xoa"), sem("o_xob")]
            s_sy = [sem("o_sya"), sem("o_syb")]
            s_ty, s_vy = sem("o_ty"), sem("o_vy")
            c_wo = [0, 0]
            c_xo = [0, 0]
            c_sy = [0, 0]
            c_ty = c_vy = 0
            for hc in range(4):
                slw = hc % 2
                if hc >= 2:
                    P.wait("gpsimd", s_ty, (hc - 1) * 16)
                P.ins("gpsimd", lambda e, slw=slw, hc=hc: e.dma_start(
                    out=wo[slw][:, :], in_=d["wo1_r"][hc, :, :]), (s_wo[slw], 16))
                c_wo[slw] += 16
                for ts in range(NTS):
                    g = hc * 16 + ts
                    psl = g % 2
                    slx = g % 2
                    if g >= 2:
                        P.wait("sync", s_vy, g - 1)
                    P.ins("sync", lambda e, slx=slx, ts=ts, hc=hc: e.dma_start(
                        out=xo3[slx][:, :], in_=X_in[ts * 128:(ts + 1) * 128, hc * 512:(hc + 1) * 512]),
                        (s_xo[slx], 16))
                    c_xo[slx] += 16
                    P.wait("tensor", s_wo[slw], c_wo[slw])
                    if g >= 2:
                        P.wait("tensor", s_vy, g - 1)
                    for dt in range(16):
                        P.ins("tensor", lambda e, psl=psl, slw=slw, dt=dt, ts=ts: e.matmul(
                            py[psl][:, :], outQ[:, dt * 2048 + ts * 128: dt * 2048 + ts * 128 + 128],
                            wo[slw][:, dt * 512:(dt + 1) * 512],
                            start=(dt == 0), stop=(dt == 15)),
                            (s_ty, 1) if dt == 15 else None)
                    c_ty += 1
                    P.wait("vector", s_ty, c_ty)
                    P.wait("vector", s_xo[slx], c_xo[slx])
                    if g >= 2:
                        P.wait("vector", s_sy[slx], c_sy[slx])
                    P.ins("vector", lambda e, psl=psl, slx=slx: e.tensor_tensor(
                        xn3[slx][:, :], py[psl][:, :], xo3[slx][:, :], op=ALU.add), (s_vy, 1))
                    c_vy += 1
                    P.wait("sync", s_vy, c_vy)
                    P.ins("sync", lambda e, slx=slx, ts=ts, hc=hc: e.dma_start(
                        out=X_out[ts * 128:(ts + 1) * 128, hc * 512:(hc + 1) * 512], in_=xn3[slx][:, :]),
                        (s_sy[slx], 16))
                    c_sy[slx] += 16
            P.wait("sync", s_sy[0], c_sy[0])
            P.wait("sync", s_sy[1], c_sy[1])
            P.emit(nc)



def _tern_chunks(d):
    out = []
    for i in range(40):
        for c in range(4):
            out.append((0, i, c * 512, 512))
    for i in range(8):
        for c in range(20):
            out.append((1, i, c * 512, 512))
    return out


def plan_tern_absmean(nc, P, Pt, st, sp, d, thr_sb, c_col):
    sb = lambda n, sh, dt=F32: st.enter_context(nc.sbuf_tensor("tm_" + n, sh, dt))
    ps = lambda n, sh, dt=F32: st.enter_context(nc.psum_tensor("tm_" + n, sh, dt))
    sem = lambda n: sp.enter_context(nc.semaphore("tm_" + n))
    wt = [sb(f"wt{i}", [128, 512]) for i in range(2)]
    acc = sb("acc", [128, 320])
    red = sb("red", [128, 2])
    sml = sb("sml", [1, 8])
    pam = ps("pam", [1, 2])
    pbc2 = ps("pbc2", [128, 8])
    s_l, s_v, s_t, s_st = sem("sl"), sem("sv"), sem("st"), sem("sst")
    c_l = c_v = c_t = c_st = 0
    chunks = _tern_chunks(d)
    srcs = [d["wfc_r"], d["wpj_r"]]
    for ci, (m, i, c0, w) in enumerate(chunks):
        sl = ci % 2
        if ci >= 2:
            P.wait("gpsimd", s_v, ci - 1)
        P.ins("gpsimd", lambda e, sl=sl, m=m, i=i, c0=c0, w=w: e.dma_start(
            out=wt[sl][:, 0:w], in_=srcs[m][i, :, c0:c0 + w]), (s_l, 16)); c_l += 16
        P.wait("vector", s_l, c_l)
        P.ins("vector", lambda e, sl=sl, ci=ci, w=w: e.reduce_sum(
            acc[:, ci:ci + 1], wt[sl][:, 0:w], axis=AX.X,
            apply_absolute_value=True), (s_v, 1)); c_v += 1
    nfc_ch = 160
    P = Pt
    P.wait("vector", s_v, c_v)
    P.ins("vector", lambda e: e.reduce_sum(red[:, 0:1], acc[:, 0:nfc_ch], axis=AX.X), (s_v, 1)); c_v += 1
    P.ins("vector", lambda e: e.reduce_sum(red[:, 1:2], acc[:, nfc_ch:320], axis=AX.X), (s_v, 1)); c_v += 1
    P.wait("tensor", s_v, c_v)
    P.ins("tensor", lambda e: e.matmul(pam[:, 0:1], d["ones_col_f_sb"], red[:, 0:1],
                                       start=True, stop=True), (s_t, 1)); c_t += 1
    P.ins("tensor", lambda e: e.matmul(pam[:, 1:2], d["ones_col_f_sb"], red[:, 1:2],
                                       start=True, stop=True), (s_t, 1)); c_t += 1
    NEL = 5120 * 2048
    P.wait("vector", s_t, c_t)
    P.ins("vector", lambda e: e.tensor_scalar(
        sml[:, 0:1], pam[:, 0:1], 1.0 / NEL, 1e-5, op0=ALU.mult, op1=ALU.max), (s_v, 1)); c_v += 1
    P.ins("vector", lambda e: e.tensor_scalar(
        sml[:, 1:2], pam[:, 1:2], 1.0 / NEL, 1e-5, op0=ALU.mult, op1=ALU.max), (s_v, 1)); c_v += 1
    P.wait("vector", s_v, c_v)
    P.ins("vector", lambda e: e.tensor_scalar(
        sml[:, 2:3], sml[:, 0:1], 0.7, None, op0=ALU.mult), (s_v, 1)); c_v += 1
    P.ins("vector", lambda e: e.tensor_scalar(
        sml[:, 3:4], sml[:, 0:1], -0.7, None, op0=ALU.mult), (s_v, 1)); c_v += 1
    P.ins("vector", lambda e: e.tensor_scalar(
        sml[:, 4:5], sml[:, 1:2], 0.7, None, op0=ALU.mult), (s_v, 1)); c_v += 1
    P.ins("vector", lambda e: e.tensor_scalar(
        sml[:, 5:6], sml[:, 1:2], -0.7, None, op0=ALU.mult), (s_v, 1)); c_v += 1
    P.ins("vector", lambda e: e.tensor_tensor(
        sml[:, 6:7], sml[:, 0:1], sml[:, 0:1], op=ALU.mult), (s_v, 1)); c_v += 1
    P.wait("vector", s_v, c_v)
    P.ins("vector", lambda e: e.tensor_tensor(
        sml[:, 6:7], sml[:, 6:7], sml[:, 1:2], op=ALU.mult), (s_v, 1)); c_v += 1
    P.wait("vector", s_v, c_v)
    P.ins("vector", lambda e: e.tensor_scalar(
        sml[:, 6:7], sml[:, 6:7], 0.5, None, op0=ALU.mult), (s_v, 1)); c_v += 1
    P.wait("tensor", s_v, c_v)
    P.ins("tensor", lambda e: e.matmul(pbc2[:, :], d["ones_row_sb"], sml[:, :],
                                       start=True, stop=True), (s_t, 1)); c_t += 1
    P.wait("vector", s_t, c_t)
    P.ins("vector", lambda e: e.tensor_copy(thr_sb[:, :], pbc2[:, :]), (s_v, 1)); c_v += 1
    P.wait("sync", s_v, c_v)
    P.ins("sync", lambda e: e.dma_start(out=c_col[:, :], in_=thr_sb[:, 6:7],
                                        allow_slow_non_contiguous=True), (s_st, 16)); c_st += 16
    P.wait("sync", s_st, c_st)


def plan_tern_quant(nc, P, st, sp, d, thr_sb, qfc_c, qproj_c):
    sb = lambda n, sh, dt=F32: st.enter_context(nc.sbuf_tensor("tq_" + n, sh, dt))
    sem = lambda n: sp.enter_context(nc.semaphore("tq_" + n))
    wt = [sb(f"wt{i}", [128, 512]) for i in range(2)]
    tb = [sb(f"tb{i}", [128, 512]) for i in range(2)]
    qt = [sb(f"qt{i}", [128, 512], BF16) for i in range(2)]
    s_l, s_g, s_v, s_st = sem("sl"), sem("sg"), sem("sv"), sem("sst")
    c_l = c_g = c_v = c_st = 0
    chunks = _tern_chunks(d)
    srcs = [d["wfc_r"], d["wpj_r"]]
    dsts = [qfc_c, qproj_c]
    for ci, (m, i, c0, w) in enumerate(chunks):
        sl = ci % 2
        tcol, ncol = (2, 3) if m == 0 else (4, 5)
        if ci >= 2:
            P.wait("sync", s_v, ci - 1)
        P.ins("sync", lambda e, sl=sl, m=m, i=i, c0=c0, w=w: e.dma_start(
            out=wt[sl][:, :], in_=srcs[m][i, :, c0:c0 + w]), (s_l, 16)); c_l += 16
        P.wait("vector", s_l, c_l)
        if ci >= 2:
            P.wait("vector", s_st, 16 * (ci - 1))
        P.ins("vector", lambda e, sl=sl, ncol=ncol: e.tensor_scalar(
            tb[sl][:, :], wt[sl][:, :], thr_sb[:, ncol:ncol + 1], None,
            op0=ALU.is_lt), (s_g, 1)); c_g += 1
        P.wait("vector", s_g, c_g)
        P.ins("vector", lambda e, sl=sl, tcol=tcol: e.scalar_tensor_tensor(
            qt[sl][:, :], wt[sl][:, :], thr_sb[:, tcol:tcol + 1], tb[sl][:, :],
            op0=ALU.is_gt, op1=ALU.subtract), (s_v, 1)); c_v += 1
        P.wait("sync", s_v, c_v)
        P.ins("sync", lambda e, sl=sl, m=m, i=i, c0=c0, w=w: e.dma_start(
            out=dsts[m][i, :, c0:c0 + w], in_=qt[sl][:, :]), (s_st, 16)); c_st += 16
    P.wait("sync", s_st, c_st)



def build_tern(nc, d, sp, qfc_c, qproj_c, c_col):
    pref = "tn_"
    sem = lambda n: sp.enter_context(nc.semaphore(pref + n))
    NEL = 5120 * 2048
    fc_ch = [(0, i, 0) for i in range(40)]
    pj_ch = [(1, i, c * 2048) for i in range(8) for c in range(5)]
    zipped = []
    for i in range(40):
        zipped.append(("f", i))
        zipped.append(("p", i))

    with ExitStack() as stA:
        thr = stA.enter_context(nc.sbuf_tensor(pref + "thr", [128, 8], F32, side="right"))
        with ExitStack() as st1:
            sb1 = lambda n, sh, dt=F32: st1.enter_context(nc.sbuf_tensor(pref + n, sh, dt))
            ps1 = lambda n, sh, dt=F32: st1.enter_context(nc.psum_tensor(pref + n, sh, dt))
            P = Plan()
            NB = 4
            wtf = [sb1(f"wtf{i}", [128, 2048]) for i in range(NB)]
            wtp = [sb1(f"wtp{i}", [128, 2048]) for i in range(NB)]
            accf = sb1("accf", [128, 40])
            accp = sb1("accp", [128, 40])
            red = sb1("red", [128, 2])
            sml = sb1("sml", [1, 8])
            tb = sb1("tb", [128, 2048])
            qtf = [sb1(f"qtf{i}", [128, 2048], BF16) for i in range(NB)]
            qtp = [sb1(f"qtp{i}", [128, 2048], BF16) for i in range(NB)]
            pam = ps1("pam", [1, 2])
            pbc2 = ps1("pbc2", [128, 8])
            s_lf = [sem(f"slf{i}") for i in range(NB)]
            s_lp = [sem(f"slp{i}") for i in range(NB)]
            s_vf, s_vp, s_t, s_st = sem("svf"), sem("svp"), sem("stt"), sem("sst")
            s_sf = [sem(f"ssf{i}") for i in range(NB)]
            s_sp = [sem(f"ssp{i}") for i in range(NB)]
            s_v = sem("sv")
            c_lf = [0] * NB; c_lp = [0] * NB; c_sf = [0] * NB; c_sp = [0] * NB
            c_vf = c_vp = c_t = c_st = c_v = 0
            vf_done = {}
            vp_done = {}
            for kind, i in zipped:
                if kind == "f":
                    sl = i % NB
                    m, ti, c0 = fc_ch[i]
                    if i >= NB:
                        P.wait("sync", s_vf, vf_done[i - NB])
                    P.ins("sync", lambda e, sl=sl, ti=ti, c0=c0: e.dma_start(
                        out=wtf[sl][:, :], in_=d["wfc_r"][ti, :, c0:c0 + 2048]), (s_lf[sl], 16)); c_lf[sl] += 16
                    P.wait("vector", s_lf[sl], c_lf[sl])
                    P.ins("vector", lambda e, sl=sl, i=i: e.reduce_sum(
                        accf[:, i:i + 1], wtf[sl][:, :], axis=AX.X,
                        apply_absolute_value=True), (s_vf, 1)); c_vf += 1
                    vf_done[i] = c_vf
                else:
                    sl = i % NB
                    m, ti, c0 = pj_ch[i]
                    if i >= NB:
                        P.wait("gpsimd", s_vp, vp_done[i - NB])
                    P.ins("gpsimd", lambda e, sl=sl, ti=ti, c0=c0: e.dma_start(
                        out=wtp[sl][:, :], in_=d["wpj_r"][ti, :, c0:c0 + 2048]), (s_lp[sl], 16)); c_lp[sl] += 16
                    P.wait("vector", s_lp[sl], c_lp[sl])
                    P.ins("vector", lambda e, sl=sl, i=i: e.reduce_sum(
                        accp[:, i:i + 1], wtp[sl][:, :], axis=AX.X,
                        apply_absolute_value=True), (s_vp, 1)); c_vp += 1
                    vp_done[i] = c_vp
            P.wait("vector", s_vf, c_vf)
            P.wait("vector", s_vp, c_vp)
            P.ins("vector", lambda e: e.reduce_sum(red[:, 0:1], accf[:, :], axis=AX.X), (s_v, 1)); c_v += 1
            P.ins("vector", lambda e: e.reduce_sum(red[:, 1:2], accp[:, :], axis=AX.X), (s_v, 1)); c_v += 1
            P.wait("tensor", s_v, c_v)
            P.ins("tensor", lambda e: e.matmul(pam[:, 0:1], d["ones_col_f_sb"], red[:, 0:1],
                                               start=True, stop=True), (s_t, 1)); c_t += 1
            P.ins("tensor", lambda e: e.matmul(pam[:, 1:2], d["ones_col_f_sb"], red[:, 1:2],
                                               start=True, stop=True), (s_t, 1)); c_t += 1
            P.wait("vector", s_t, c_t)
            P.ins("vector", lambda e: e.tensor_scalar(
                sml[:, 0:1], pam[:, 0:1], 1.0 / NEL, 1e-5, op0=ALU.mult, op1=ALU.max), (s_v, 1)); c_v += 1
            P.ins("vector", lambda e: e.tensor_scalar(
                sml[:, 1:2], pam[:, 1:2], 1.0 / NEL, 1e-5, op0=ALU.mult, op1=ALU.max), (s_v, 1)); c_v += 1
            P.wait("vector", s_v, c_v)
            P.ins("vector", lambda e: e.tensor_scalar(
                sml[:, 2:3], sml[:, 0:1], 0.7, None, op0=ALU.mult), (s_v, 1)); c_v += 1
            P.ins("vector", lambda e: e.tensor_scalar(
                sml[:, 3:4], sml[:, 0:1], -0.7, None, op0=ALU.mult), (s_v, 1)); c_v += 1
            P.ins("vector", lambda e: e.tensor_scalar(
                sml[:, 4:5], sml[:, 1:2], 0.7, None, op0=ALU.mult), (s_v, 1)); c_v += 1
            P.ins("vector", lambda e: e.tensor_scalar(
                sml[:, 5:6], sml[:, 1:2], -0.7, None, op0=ALU.mult), (s_v, 1)); c_v += 1
            P.ins("vector", lambda e: e.tensor_tensor(
                sml[:, 6:7], sml[:, 0:1], sml[:, 0:1], op=ALU.mult), (s_v, 1)); c_v += 1
            P.wait("vector", s_v, c_v)
            P.ins("vector", lambda e: e.tensor_tensor(
                sml[:, 6:7], sml[:, 6:7], sml[:, 1:2], op=ALU.mult), (s_v, 1)); c_v += 1
            P.wait("vector", s_v, c_v)
            P.ins("vector", lambda e: e.tensor_scalar(
                sml[:, 6:7], sml[:, 6:7], 0.5, None, op0=ALU.mult), (s_v, 1)); c_v += 1
            P.wait("tensor", s_v, c_v)
            P.ins("tensor", lambda e: e.matmul(pbc2[:, :], d["ones_row_sb"], sml[:, :],
                                               start=True, stop=True), (s_t, 1)); c_t += 1
            P.wait("vector", s_t, c_t)
            P.ins("vector", lambda e: e.tensor_copy(thr[:, :], pbc2[:, :]), (s_v, 1)); c_v += 1
            P.wait("sync", s_v, c_v)
            P.ins("sync", lambda e: e.dma_start(out=c_col[:, :], in_=thr[:, 6:7],
                                                allow_slow_non_contiguous=True), (s_st, 16)); c_st += 16
            s_lf2 = [sem(f"lf2{i}") for i in range(NB)]
            s_lp2 = [sem(f"lp2{i}") for i in range(NB)]
            c_lf2 = [0] * NB; c_lp2 = [0] * NB
            s_v2f, s_v2p = sem("v2f"), sem("v2p")
            c_v2f = c_v2p = 0
            v2f_done = {}
            v2p_done = {}
            first_q = True
            for kind, i in zipped:
                if kind == "f":
                    sl = i % NB
                    m, ti, c0 = fc_ch[i]
                    if i >= NB:
                        P.wait("sync", s_v2f, v2f_done[i - NB])
                        P.wait("sync", s_sf[sl], c_sf[sl])
                    P.ins("sync", lambda e, sl=sl, ti=ti, c0=c0: e.dma_start(
                        out=wtf[sl][:, :], in_=d["wfc_r"][ti, :, c0:c0 + 2048]), (s_lf2[sl], 16)); c_lf2[sl] += 16
                    P.wait("vector", s_lf2[sl], c_lf2[sl])
                    if first_q:
                        P.wait("vector", s_v, c_v)
                        first_q = False
                    P.ins("vector", lambda e, sl=sl: e.tensor_scalar(
                        tb[:, :], wtf[sl][:, :], thr[:, 3:4], None, op0=ALU.is_lt), (s_v2f, 1)); c_v2f += 1
                    P.wait("vector", s_v2f, c_v2f)
                    P.ins("vector", lambda e, sl=sl: e.scalar_tensor_tensor(
                        qtf[sl][:, :], wtf[sl][:, :], thr[:, 2:3], tb[:, :],
                        op0=ALU.is_gt, op1=ALU.subtract), (s_v2f, 1)); c_v2f += 1
                    v2f_done[i] = c_v2f
                    P.wait("sync", s_v2f, c_v2f)
                    P.ins("sync", lambda e, sl=sl, ti=ti, c0=c0: e.dma_start(
                        out=qfc_c[ti, :, c0:c0 + 2048], in_=qtf[sl][:, :]), (s_sf[sl], 16)); c_sf[sl] += 16
                else:
                    sl = i % NB
                    m, ti, c0 = pj_ch[i]
                    if i >= NB:
                        P.wait("gpsimd", s_v2p, v2p_done[i - NB])
                        P.wait("gpsimd", s_sp[sl], c_sp[sl])
                    P.ins("gpsimd", lambda e, sl=sl, ti=ti, c0=c0: e.dma_start(
                        out=wtp[sl][:, :], in_=d["wpj_r"][ti, :, c0:c0 + 2048]), (s_lp2[sl], 16)); c_lp2[sl] += 16
                    P.wait("vector", s_lp2[sl], c_lp2[sl])
                    if first_q:
                        P.wait("vector", s_v, c_v)
                        first_q = False
                    P.ins("vector", lambda e, sl=sl: e.tensor_scalar(
                        tb[:, :], wtp[sl][:, :], thr[:, 5:6], None, op0=ALU.is_lt), (s_v2p, 1)); c_v2p += 1
                    P.wait("vector", s_v2p, c_v2p)
                    P.ins("vector", lambda e, sl=sl: e.scalar_tensor_tensor(
                        qtp[sl][:, :], wtp[sl][:, :], thr[:, 4:5], tb[:, :],
                        op0=ALU.is_gt, op1=ALU.subtract), (s_v2p, 1)); c_v2p += 1
                    v2p_done[i] = c_v2p
                    P.wait("gpsimd", s_v2p, c_v2p)
                    P.ins("gpsimd", lambda e, sl=sl, ti=ti, c0=c0: e.dma_start(
                        out=qproj_c[ti, :, c0:c0 + 2048], in_=qtp[sl][:, :]), (s_sp[sl], 16)); c_sp[sl] += 16
            for i in range(NB):
                P.wait("sync", s_sf[i], c_sf[i])
                P.wait("gpsimd", s_sp[i], c_sp[i])
            P.wait("sync", s_st, c_st)
            P.emit(nc)



def _bf(x):
    return np.ascontiguousarray(x.astype(BF))


def prep_host(inputs):
    d = {}
    f32 = lambda x: np.ascontiguousarray(np.asarray(x, dtype=np.float32))

    gw = f32(inputs["L0_ffn_gate_w"])
    uw = f32(inputs["L0_ffn_up_w"])
    dw = f32(inputs["L0_ffn_down_w"])
    d["gw_r"] = np.ascontiguousarray(gw.reshape(44, 128, 16, 128).transpose(0, 3, 2, 1).reshape(44, 128, 2048))
    d["uw_r"] = np.ascontiguousarray(uw.reshape(44, 128, 16, 128).transpose(0, 3, 2, 1).reshape(44, 128, 2048))
    d["dw_r"] = np.ascontiguousarray(dw.T.reshape(44, 128, 8, 256).transpose(2, 1, 0, 3).reshape(8, 128, 44 * 256))
    d["gs128"] = np.ascontiguousarray(f32(inputs["L0_ffn_gate_s"]).reshape(44, 128).T)
    d["us128"] = np.ascontiguousarray(f32(inputs["L0_ffn_up_s"]).reshape(44, 128).T)
    d["dsT"] = _bf(np.tile(f32(inputs["L0_ffn_down_s"])[None, :], (128, 1)))

    wq = f32(inputs["L0_attn_Wq"])
    d["wq_r"] = np.ascontiguousarray(wq.reshape(16, 128, 16, 128).transpose(0, 3, 2, 1).reshape(16, 128, 2048))
    wkd = f32(inputs["L0_attn_Wk_down"])
    d["wkd_r"] = np.ascontiguousarray(wkd.T.reshape(16, 128, 256).transpose(1, 0, 2).reshape(128, 16 * 256))
    wku = f32(inputs["L0_attn_Wk_up"])
    d["wku_r"] = np.ascontiguousarray(wku.T.reshape(2, 128, 512).transpose(1, 0, 2).reshape(128, 1024))
    wvd = f32(inputs["L0_attn_Wv_down"])
    d["wvd_r"] = np.ascontiguousarray(wvd.T.reshape(16, 128, 384).transpose(1, 0, 2).reshape(128, 16 * 384))
    wvu = f32(inputs["L0_attn_Wv_up"])
    d["wvu_r"] = np.ascontiguousarray(wvu.T.reshape(3, 128, 512).transpose(1, 0, 2).reshape(128, 3 * 512))
    wo = f32(inputs["L0_attn_Wo"])
    d["wo_r"] = np.ascontiguousarray(wo.T.reshape(16, 128, 4, 512).transpose(2, 1, 0, 3).reshape(4, 128, 16 * 512))
    d["kanc"] = np.ascontiguousarray(f32(inputs["L0_attn_k_anchor"]).reshape(4, 128).T)
    d["vanc"] = np.ascontiguousarray(np.tile(f32(inputs["L0_attn_v_anchor"])[None, :], (128, 1)))

    inv = 1.0 / (10000.0 ** (np.arange(0, 64, dtype=np.float64) * 2.0 / 128.0))
    pos = np.arange(S, dtype=np.float64)
    ang = pos[None, :] * inv[:, None]
    cosT = np.concatenate([np.cos(ang), np.cos(ang)], axis=0).astype(np.float32)
    sinT = np.concatenate([np.sin(ang), -np.sin(ang)], axis=0).astype(np.float32)
    d["cosT"] = _bf(cosT)
    d["sinT"] = _bf(sinT)
    d["cosTM_r"] = np.ascontiguousarray(_bf(cosT.T).reshape(16, 128, 128))
    d["sinTM_r"] = np.ascontiguousarray(_bf(sinT.T).reshape(16, 128, 128))

    tri = (np.arange(128)[None, :] >= np.arange(128)[:, None])
    masks = np.zeros((4, 128, 512), dtype=np.float32)
    for j in range(4):
        masks[j, :, 128 * j:128 * (j + 1)] = tri
        masks[j, :, 128 * (j + 1):] = 1.0
    d["masks0"] = _bf(masks.reshape(4, 128, 512))

    m1 = np.zeros((12, 128, 512), dtype=np.float32)
    edge = (np.arange(128)[None, :] < np.arange(128)[:, None])
    for ktr in range(12):
        for jp in range(4):
            m = jp - ktr + 8
            if m == 0:
                m1[ktr, :, 128 * jp:128 * (jp + 1)] = tri
            elif m == 8:
                m1[ktr, :, 128 * jp:128 * (jp + 1)] = edge
            elif 0 < m < 8:
                m1[ktr, :, 128 * jp:128 * (jp + 1)] = 1.0
    d["masks1"] = _bf(m1)

    d["wqkv_r"] = np.ascontiguousarray(
        f32(inputs["L1_attn_Wqkv"]).reshape(16, 128, 12, 512).transpose(2, 1, 0, 3).reshape(12, 128, 16 * 512))
    d["wo1_r"] = np.ascontiguousarray(
        f32(inputs["L1_attn_Wo"]).reshape(16, 128, 4, 512).transpose(2, 1, 0, 3).reshape(4, 128, 16 * 512))
    d["qnw"] = _bf(np.tile(f32(inputs["L1_attn_q_norm_w"])[None, :], (128, 1)))
    d["knw"] = _bf(np.tile(f32(inputs["L1_attn_k_norm_w"])[None, :], (128, 1)))

    wfc = f32(inputs["L1_ffn_w_fc"])
    wpj = f32(inputs["L1_ffn_w_proj"])
    d["wfc_r"] = np.ascontiguousarray(wfc.reshape(40, 128, 16, 128).transpose(0, 3, 2, 1).reshape(40, 128, 2048))
    d["wpj_r"] = np.ascontiguousarray(wpj.T.reshape(40, 128, 8, 256).transpose(2, 1, 0, 3).reshape(8, 128, 40 * 256))

    d["identb"] = np.eye(128, dtype=BF)
    d["onescol"] = np.ones((128, 1), dtype=BF)
    d["onesrow"] = np.ones((1, 128), dtype=np.float32)
    return d



def _declare_common(nc, d, names):
    aps = {}
    for n in names:
        arr = d[n]
        dt = BF16 if arr.dtype == BF else F32
        aps[n] = nc.dram_tensor(n, list(arr.shape), dt, kind="ExternalInput").ap()
    return aps


_FUSED_CACHE = {}

ALL_W_NAMES = ["gw_r", "uw_r", "dw_r", "gs128", "us128", "dsT",
               "wq_r", "wkd_r", "wku_r", "wvd_r", "wvu_r", "wo_r",
               "kanc", "vanc", "cosT", "sinT", "masks0",
               "wfc_r", "wpj_r",
               "wqkv_r", "wo1_r", "qnw", "knw", "cosTM_r", "sinTM_r", "masks1",
               "identb"]


def _build_fused(d):
    nc = _mk_nc()
    aps = _declare_common(nc, d, ALL_W_NAMES)
    x_in = nc.dram_tensor("Xin", [S, HD], F32, kind="ExternalInput").ap()
    x_out = nc.dram_tensor("Xout", [S, HD], F32, kind="ExternalOutput").ap()
    XA = nc.dram_tensor("XA", [S, HD], F32).ap()
    XB = nc.dram_tensor("XB", [S, HD], F32).ap()
    dd = dict(aps)
    dd["gw_c"] = nc.dram_tensor("gw_c", [44, 128, 2048], BF16).ap()
    dd["uw_c"] = nc.dram_tensor("uw_c", [44, 128, 2048], BF16).ap()
    dd["dw_c"] = nc.dram_tensor("dw_c", [8, 128, 44 * 256], BF16).ap()
    dd["qfc_c"] = nc.dram_tensor("qfc_c", [40, 128, 2048], BF16).ap()
    dd["qproj_c"] = nc.dram_tensor("qproj_c", [8, 128, 40 * 256], BF16).ap()
    dd["c_col"] = nc.dram_tensor("c_col", [128, 1], F32).ap()
    qkv_s = nc.dram_tensor("qkv_s", [S, 6144], BF16).ap()
    qT_s = nc.dram_tensor("qT_s", [16, 128, 2048], BF16).ap()
    kT_s = nc.dram_tensor("kT_s", [16, 128, 2048], BF16).ap()
    _const_block(nc, d, aps, dd)

    def rst():
        nc.reset(
            previous_sbuf_base=nc.sbuf_base, previous_sbuf_top=nc.sbuf_top,
            previous_psum_base=nc.psum_base, previous_psum_top=nc.psum_top,
            previous_local_dram_base=nc.local_dram_base,
            previous_shared_dram_base=nc.shared_dram_base)

    build_ffn(nc, dd, ExitStack(), 0, 0, x_in, XA, have_cache=False)
    rst()
    build_attn0(nc, dd, ExitStack(), XA, XB)
    rst()
    build_ffn(nc, dd, ExitStack(), 0, 1, XB, XA, have_cache=True)
    rst()
    build_tern(nc, dd, ExitStack(), dd["qfc_c"], dd["qproj_c"], dd["c_col"])
    rst()
    build_ffn(nc, dd, ExitStack(), 1, 0, XA, XB, have_cache=True)
    rst()
    build_attn1(nc, dd, ExitStack(), XB, XA, qkv_s, qT_s, kT_s)
    rst()
    build_ffn(nc, dd, ExitStack(), 1, 1, XA, x_out, have_cache=True)
    return nc


def kernel(**inputs):
    d = prep_host(inputs)
    x_full = np.ascontiguousarray(np.asarray(inputs["L0_attn_hidden_states"], dtype=np.float32))
    B = x_full.shape[0]

    if "nc" not in _FUSED_CACHE:
        _FUSED_CACHE["nc"] = _build_fused(d)
    nc = _FUSED_CACHE["nc"]
    shared = {n: d[n] for n in ALL_W_NAMES}
    in_maps = []
    for b in range(B):
        m = dict(shared)
        m["Xin"] = np.ascontiguousarray(x_full[b])
        in_maps.append(m)
    res = run_bass_kernel_spmd(nc, in_maps, core_ids=list(range(B)))
    return np.stack([res.results[b]["Xout"] for b in range(B)], axis=0)


def _mk_nc():
    return bass.Bass("TRN2", target_bir_lowering=False)


def _const_block(nc, d, aps, dd):
    h = nc.sbuf_tensor("identb_sb", [128, 128], BF16).__enter__()
    cst = nc.sbuf_tensor("cst_sb", [128, 4], F32).__enter__()
    orow = nc.sbuf_tensor("onesrow_sb", [1, 128], F32).__enter__()
    ocol = nc.sbuf_tensor("onescol_sb", [128, 1], BF16).__enter__()
    ocolf = nc.sbuf_tensor("onescolf_sb", [128, 1], F32).__enter__()
    with nc.semaphore("cst_s") as s0, nc.Block() as block:
        @block.sync
        def _(e):
            e.dma_start(out=h[:, :], in_=aps["identb"][:, :]).then_inc(s0, 16)
        @block.vector
        def _(e):
            e.memset(cst[:, 0:1], EPS_LN)
            e.memset(cst[:, 1:2], EPS_RMS)
            e.memset(orow[:, :], 1.0)
            e.memset(ocol[:, :], 1.0)
            e.memset(ocolf[:, :], 1.0).then_inc(s0, 1)
            e.wait_ge(s0, 17)
    dd["identb_sb"] = h
    dd["eps_ln_col"] = cst[:, 0:1]
    dd["eps_rms_col"] = cst[:, 1:2]
    dd["ones_row_sb"] = orow[:, :]
    dd["ones_col_sb"] = ocol[:, :]
    dd["ones_col_f_sb"] = ocolf[:, :]


def _run_chain(d, xs, apps=("f0", "a0", "f0", "tern", "f1", "a1", "f1")):
    B = len(xs)
    cores = list(range(B))
    progs = {}

    def make_ffn0():
        nc = _mk_nc()
        names = ["gw_r", "uw_r", "dw_r", "gs128", "us128", "dsT", "identb"]
        aps = _declare_common(nc, d, names)
        X_in = nc.dram_tensor("Xin", [S, HD], F32, kind="ExternalInput").ap()
        X_out = nc.dram_tensor("Xout", [S, HD], F32, kind="ExternalOutput").ap()
        dd = dict(aps)
        dd["gw_c"] = nc.dram_tensor("gw_c", [44, 128, 2048], BF16).ap()
        dd["uw_c"] = nc.dram_tensor("uw_c", [44, 128, 2048], BF16).ap()
        dd["dw_c"] = nc.dram_tensor("dw_c", [8, 128, 44 * 256], BF16).ap()
        _const_block(nc, d, aps, dd)
        sp = ExitStack()
        build_ffn(nc, dd, sp, layer=0, app_idx=0, X_in=X_in, X_out=X_out, have_cache=False)
        return nc, names

    def make_attn0():
        nc = _mk_nc()
        names = ["wq_r", "wkd_r", "wku_r", "wvd_r", "wvu_r", "wo_r",
                 "kanc", "vanc", "cosT", "sinT", "masks0", "identb"]
        aps = _declare_common(nc, d, names)
        X_in = nc.dram_tensor("Xin", [S, HD], F32, kind="ExternalInput").ap()
        X_out = nc.dram_tensor("Xout", [S, HD], F32, kind="ExternalOutput").ap()
        dd = dict(aps)
        _const_block(nc, d, aps, dd)
        sp = ExitStack()
        build_attn0(nc, dd, sp, X_in, X_out)
        return nc, names

    def make_tern():
        nc = _mk_nc()
        names = ["wfc_r", "wpj_r", "identb"]
        aps = _declare_common(nc, d, names)
        qfc_c = nc.dram_tensor("qfc_c", [40, 128, 2048], BF16, kind="ExternalOutput").ap()
        qproj_c = nc.dram_tensor("qproj_c", [8, 128, 40 * 256], BF16, kind="ExternalOutput").ap()
        c_col = nc.dram_tensor("c_col", [128, 1], F32, kind="ExternalOutput").ap()
        dd = dict(aps)
        _const_block(nc, d, aps, dd)
        sp = ExitStack()
        build_tern(nc, dd, sp, qfc_c, qproj_c, c_col)
        return nc, names

    def make_ffn1():
        nc = _mk_nc()
        names = ["identb"]
        aps = _declare_common(nc, d, names)
        X_in = nc.dram_tensor("Xin", [S, HD], F32, kind="ExternalInput").ap()
        X_out = nc.dram_tensor("Xout", [S, HD], F32, kind="ExternalOutput").ap()
        dd = dict(aps)
        dd["qfc_c"] = nc.dram_tensor("qfc_c", [40, 128, 2048], BF16, kind="ExternalInput").ap()
        dd["qproj_c"] = nc.dram_tensor("qproj_c", [8, 128, 40 * 256], BF16, kind="ExternalInput").ap()
        dd["c_col"] = nc.dram_tensor("c_col", [128, 1], F32, kind="ExternalInput").ap()
        _const_block(nc, d, aps, dd)
        sp = ExitStack()
        build_ffn(nc, dd, sp, layer=1, app_idx=0, X_in=X_in, X_out=X_out, have_cache=True)
        return nc, names

    def make_attn1():
        nc = _mk_nc()
        names = ["wqkv_r", "wo1_r", "qnw", "knw", "cosTM_r", "sinTM_r", "masks1", "identb"]
        aps = _declare_common(nc, d, names)
        X_in = nc.dram_tensor("Xin", [S, HD], F32, kind="ExternalInput").ap()
        X_out = nc.dram_tensor("Xout", [S, HD], F32, kind="ExternalOutput").ap()
        dd = dict(aps)
        qkv_s = nc.dram_tensor("qkv_s", [S, 6144], BF16).ap()
        qT_s = nc.dram_tensor("qT_s", [16, 128, 2048], BF16).ap()
        kT_s = nc.dram_tensor("kT_s", [16, 128, 2048], BF16).ap()
        _const_block(nc, d, aps, dd)
        sp = ExitStack()
        build_attn1(nc, dd, sp, X_in, X_out, qkv_s, qT_s, kT_s)
        return nc, names

    MAKERS = {"f0": make_ffn0, "a0": make_attn0, "tern": make_tern,
              "f1": make_ffn1, "a1": make_attn1}
    tern_out = {}

    def run_app(key):
        nonlocal xs
        if key not in progs:
            progs[key] = MAKERS[key]()
        nc, names = progs[key]
        shared = {n: d[n] for n in names}
        if key == "tern":
            in_maps = [dict(shared) for _ in range(B)]
            res = run_bass_kernel_spmd(nc, in_maps, core_ids=cores)
            for k in ("qfc_c", "qproj_c", "c_col"):
                tern_out[k] = res.results[0][k]
            return
        if key == "f1":
            shared.update(tern_out)
        in_maps = []
        for b in range(B):
            m = dict(shared)
            m["Xin"] = np.ascontiguousarray(xs[b])
            in_maps.append(m)
        res = run_bass_kernel_spmd(nc, in_maps, core_ids=cores)
        xs = [res.results[b]["Xout"] for b in range(B)]

    for a in apps:
        if a in MAKERS:
            run_app(a)
    return xs


# version 26
# speedup vs baseline: 1.1289x; 1.0131x over previous
import numpy as np
import ml_dtypes
from contextlib import ExitStack

import concourse.bass as bass
import concourse.mybir as mybir
from concourse.bass_utils import run_bass_kernel_spmd

F32 = mybir.dt.float32
BF16 = mybir.dt.bfloat16
AF = mybir.ActivationFunctionType
ALU = mybir.AluOpType
AX = mybir.AxisListType
BF = ml_dtypes.bfloat16

S = 2048
HD = 2048
NTS = 16
KH = 16
NBLK = 4
EPS_LN = 1e-5
EPS_RMS = 1e-6


class Plan:
    ENGINES = ("sync", "gpsimd", "tensor", "vector", "scalar")

    def __init__(self):
        self.items = {e: [] for e in self.ENGINES}

    def add(self, eng, fn):
        self.items[eng].append(fn)

    def wait(self, eng, sem, val):
        if val > 0:
            self.items[eng].append(lambda e, s=sem, v=val: e.wait_ge(s, v))

    def ins(self, eng, fn, inc=None):
        def g(e, fn=fn, inc=inc):
            i = fn(e)
            if inc is not None:
                i.then_inc(inc[0], inc[1])
        self.items[eng].append(g)

    def splice(self, other, spread=(), start_frac=0.0):
        for eng in self.ENGINES:
            oi = other.items[eng]
            if not oi:
                continue
            if eng in spread:
                mine = self.items[eng]
                k0 = int(len(mine) * start_frac)
                tail = mine[k0:]
                merged = []
                n_t, n_o = len(tail), len(oi)
                ti = oi_i = 0
                for step in range(n_t + n_o):
                    if ti * n_o <= oi_i * n_t and ti < n_t:
                        merged.append(tail[ti]); ti += 1
                    elif oi_i < n_o:
                        merged.append(oi[oi_i]); oi_i += 1
                    else:
                        merged.append(tail[ti]); ti += 1
                self.items[eng] = mine[:k0] + merged
            else:
                self.items[eng].extend(oi)

    def emit(self, nc):
        with nc.Block() as block:
            for eng in self.ENGINES:
                fns = self.items[eng]
                if not fns:
                    continue
                def body(e, fns=fns):
                    for f in fns:
                        f(e)
                getattr(block, eng)(body)



class LNEmitter:

    def __init__(self, nc, P, st, sp, d, X_ap, xnT, pref):
        sb = lambda n, sh, dt=F32: st.enter_context(nc.sbuf_tensor(pref + n, sh, dt))
        ps = lambda n, sh, dt=F32: st.enter_context(nc.psum_tensor(pref + n, sh, dt))
        sem = lambda n: sp.enter_context(nc.semaphore(pref + n))
        self.nc, self.P, self.d = nc, P, d
        self.X_ap, self.xnT = X_ap, xnT
        self.xo = [sb(f"xo{i}", [128, 2048]) for i in range(2)]
        self.stats = sb("stats", [128, 24])
        self.mv = sb("mv", [128, 2])
        self.lnt = sb("lnt", [128, 2])
        self.xnb = [sb("xnb0", [128, 2048], BF16)] * 2
        self.pst = [ps(f"pst{i}", [128, 512], BF16) for i in range(2)]
        self.s_x = [sem("sx0"), sem("sx1")]
        self.s_v, self.s_s, self.s_t, self.s_e = (
            sem("sv"), sem("ss"), sem("st"), sem("se"))
        self.cx = [0, 0]
        self.cv = self.cs = self.ct = self.ce = 0
        self.v_ts_done = {}
        self.grp = 0

    def e_count(self, ts):
        return 4 * (ts + 1)

    def emit(self, ts):
        P, d = self.P, self.d
        xo, stats, mv, lnt, xnb, pst = (self.xo, self.stats, self.mv,
                                        self.lnt, self.xnb, self.pst)
        s_x, s_v, s_s, s_t, s_e = self.s_x, self.s_v, self.s_s, self.s_t, self.s_e
        X_ap, xnT = self.X_ap, self.xnT
        identb = d["identb_sb"]
        sl = ts % 2
        if ts >= 2:
            P.wait("sync", s_v, self.v_ts_done[ts - 2])
        P.ins("sync", lambda e, sl=sl, ts=ts: e.dma_start(
            out=xo[sl][:, :], in_=X_ap[ts * 128:(ts + 1) * 128, :]), (s_x[sl], 16))
        self.cx[sl] += 16
        P.wait("vector", s_x[sl], self.cx[sl])
        for j in range(4):
            P.ins("vector", lambda e, sl=sl, j=j: e.bn_stats(
                stats[:, 6 * j:6 * j + 6], xo[sl][:, 512 * j:512 * (j + 1)]), (s_v, 1))
            self.cv += 1
        P.wait("vector", s_v, self.cv)
        P.ins("vector", lambda e: e.bn_aggr(mv[:, :], stats[:, :]), (s_v, 1))
        self.cv += 1
        P.wait("scalar", s_v, self.cv)
        P.ins("scalar", lambda e: e.activation(
            lnt[:, 0:1], mv[:, 1:2], AF.Ln, bias=d["eps_ln_col"]), (s_s, 1))
        self.cs += 1
        P.wait("scalar", s_s, self.cs)
        P.ins("scalar", lambda e: e.activation(
            lnt[:, 1:2], lnt[:, 0:1], AF.Exp, scale=-0.5), (s_s, 1))
        self.cs += 1
        P.wait("vector", s_s, self.cs)
        if ts >= 1:
            P.wait("vector", s_t, 4 * ts)
        P.ins("vector", lambda e, sl=sl: e.tensor_scalar(
            xnb[sl][:, :], xo[sl][:, :], mv[:, 0:1], lnt[:, 1:2],
            op0=ALU.subtract, op1=ALU.mult), (s_v, 1))
        self.cv += 1
        self.v_ts_done[ts] = self.cv
        P.wait("tensor", s_v, self.cv)
        for g in range(4):
            psl = self.grp % 2
            if self.grp >= 2:
                P.wait("tensor", s_e, self.grp - 1)
            for j in range(4):
                dt = 4 * g + j
                P.ins("tensor", lambda e, sl=sl, psl=psl, dt=dt, j=j: e.transpose(
                    pst[psl][:, j * 128:(j + 1) * 128],
                    xnb[sl][:, dt * 128:(dt + 1) * 128], identb[:, :]),
                    (s_t, 1) if j == 3 else None)
            self.ct += 1
            P.wait("scalar", s_t, self.ct)
            dst = bass.AP(xnT.tensor if hasattr(xnT, "tensor") else xnT,
                          (4 * g) * 2048 + ts * 128,
                          [[16 * 2048, 128], [2048, 4], [1, 128]])
            P.ins("scalar", lambda e, psl=psl, dst=dst: e.copy(dst, pst[psl][:, :]), (s_e, 1))
            self.ce += 1
            self.grp += 1



def plan_ln(nc, P, st, sp, d, X_ap, xnT, pref):
    sb = lambda n, sh, dt=F32: st.enter_context(nc.sbuf_tensor(pref + n, sh, dt))
    ps = lambda n, sh, dt=F32: st.enter_context(nc.psum_tensor(pref + n, sh, dt))
    sem = lambda n: sp.enter_context(nc.semaphore(pref + n))

    xo = [sb(f"xo{i}", [128, 2048]) for i in range(2)]
    stats = sb("stats", [128, 24])
    mv = sb("mv", [128, 2])
    lnt = sb("lnt", [128, 2])
    xnb = [sb(f"xnb{i}", [128, 2048], BF16) for i in range(2)]
    pst = [ps(f"pst{i}", [128, 512], BF16) for i in range(2)]
    identb = d["identb_sb"]

    s_x = [sem("sx0"), sem("sx1")]
    s_v, s_s, s_t, s_e = sem("sv"), sem("ss"), sem("st"), sem("se")
    cx = [0, 0]
    cv = cs = ct = ce = 0
    v_ts_done = {}
    grp = 0

    for ts in range(NTS):
        sl = ts % 2
        if ts >= 2:
            P.wait("sync", s_v, v_ts_done[ts - 2])
        P.ins("sync", lambda e, sl=sl, ts=ts: e.dma_start(
            out=xo[sl][:, :], in_=X_ap[ts * 128:(ts + 1) * 128, :]), (s_x[sl], 16))
        cx[sl] += 16
        P.wait("vector", s_x[sl], cx[sl])
        for j in range(4):
            P.ins("vector", lambda e, sl=sl, j=j: e.bn_stats(
                stats[:, 6 * j:6 * j + 6], xo[sl][:, 512 * j:512 * (j + 1)]), (s_v, 1))
            cv += 1
        P.wait("vector", s_v, cv)
        P.ins("vector", lambda e: e.bn_aggr(mv[:, :], stats[:, :]), (s_v, 1))
        cv += 1
        P.wait("scalar", s_v, cv)
        P.ins("scalar", lambda e: e.activation(
            lnt[:, 0:1], mv[:, 1:2], AF.Ln, bias=d["eps_ln_col"]), (s_s, 1))
        cs += 1
        P.wait("scalar", s_s, cs)
        P.ins("scalar", lambda e: e.activation(
            lnt[:, 1:2], lnt[:, 0:1], AF.Exp, scale=-0.5), (s_s, 1))
        cs += 1
        P.wait("vector", s_s, cs)
        if ts >= 2:
            P.wait("vector", s_t, 4 * (ts - 1))
        P.ins("vector", lambda e, sl=sl: e.tensor_scalar(
            xnb[sl][:, :], xo[sl][:, :], mv[:, 0:1], lnt[:, 1:2],
            op0=ALU.subtract, op1=ALU.mult), (s_v, 1))
        cv += 1
        v_ts_done[ts] = cv
        P.wait("tensor", s_v, cv)
        for g in range(4):
            psl = grp % 2
            if grp >= 2:
                P.wait("tensor", s_e, grp - 1)
            for j in range(4):
                dt = 4 * g + j
                last = (j == 3)
                P.ins("tensor", lambda e, sl=sl, psl=psl, dt=dt, j=j: e.transpose(
                    pst[psl][:, j * 128:(j + 1) * 128],
                    xnb[sl][:, dt * 128:(dt + 1) * 128], identb[:, :]),
                    (s_t, 1) if last else None)
            ct += 1
            P.wait("scalar", s_t, ct)
            dst = bass.AP(xnT.tensor if hasattr(xnT, "tensor") else xnT,
                          (4 * g) * 2048 + ts * 128,
                          [[16 * 2048, 128], [2048, 4], [1, 128]])
            P.ins("scalar", lambda e, psl=psl, dst=dst: e.copy(dst, pst[psl][:, :]), (s_e, 1))
            ce += 1
            grp += 1



def build_ffn(nc, d, sp, layer, app_idx, X_in, X_out, have_cache, extra=None, x_eng="gpsimd"):
    pref = f"f{layer}_{app_idx}_"
    NI = 44 if layer == 0 else 40
    with ExitStack() as st:
        xnT = st.enter_context(nc.sbuf_tensor(pref + "xnT", [128, 16 * 2048], BF16))

        with ExitStack() as st2:
            sb2 = lambda n, sh, dt=F32: st2.enter_context(nc.sbuf_tensor(pref + n, sh, dt))
            ps2 = lambda n, sh, dt=F32: st2.enter_context(nc.psum_tensor(pref + n, sh, dt))
            sem2 = lambda n: sp.enter_context(nc.semaphore(pref + n))
            P = Plan()
            ln = LNEmitter(nc, P, st2, sp, d, X_in, xnT[:, :], pref + "ln_")

            gwb = [sb2(f"gwb{i}", [128, 16 * 128], BF16) for i in range(2)]
            if layer == 0:
                uwb = [sb2(f"uwb{i}", [128, 16 * 128], BF16) for i in range(2)]
                gbuf = [sb2(f"gbuf{i}", [128, 512]) for i in range(2)]
                gsil = gbuf
                ubuf = [sb2(f"ubuf{i}", [128, 512]) for i in range(2)]
                gs_sb = sb2("gs", [128, NI])
                us_sb = sb2("us", [128, NI])
                ds_sb = sb2("ds", [128, 2048], BF16)
                t1b = [sb2(f"t1b{i}", [128, 256]) for i in range(2)]
            else:
                rbuf = [sb2(f"rbuf{i}", [128, 512]) for i in range(2)]
                cc_sb = sb2("cc", [128, 1])
            hdn = sb2("hdn", [128, NI * 512], BF16)
            dwb = [sb2(f"dwb{i}", [128, NI * 256], BF16) for i in range(2)]
            xo2 = [sb2(f"xo2{i}", [128, 256]) for i in range(2)]
            xnew = [sb2(f"xnew{i}", [128, 256]) for i in range(2)]

            pg = [ps2(f"pg{i}", [128, 512]) for i in range(2)]
            pu = [ps2(f"pu{i}", [128, 512]) for i in range(2)]
            s_wg = [sem2(f"swg{i}") for i in range(2)]
            s_wu = [sem2(f"swu{i}") for i in range(2)]
            s_wd = [sem2(f"swd{i}") for i in range(2)]
            s_wgf = [sem2(f"swgf{i}") for i in range(2)]
            s_wuf = [sem2(f"swuf{i}") for i in range(2)]
            s_wdf = [sem2(f"swdf{i}") for i in range(2)]
            c_wgf = [0, 0]; c_wuf = [0, 0]; c_wdf = [0, 0]
            wg_rdy = {}; wu_rdy = {}; wd_rdy = {}
            s_mm = sem2("smm")
            s_ev = sem2("sev")
            s_hd = sem2("shd")
            s_sc = sem2("ssc")
            s_v2 = sem2("sv2")
            s_md = sem2("smd")
            s_x2 = sem2("sx2")
            s_xo2 = [sem2("sxo2a"), sem2("sxo2b")]
            s_xn = sem2("sxn")
            s_st = [sem2("ssta"), sem2("sstb")]
            s_cw = [sem2("scwa"), sem2("scwb")]
            s_cwd = [sem2("scwda"), sem2("scwdb")]

            c_wg = [0, 0]; c_wu = [0, 0]; c_wd = [0, 0]
            c_xo2 = [0, 0]; c_st = [0, 0]
            c_cw = [0, 0]; c_cwd = [0, 0]
            c_mm = c_ev = c_hd = c_sc = c_v2 = c_md = c_x2 = c_xn = 0
            mm_done_at_it = {}
            ev_done_at_it = {}
            md_total = 0
            xn_total = 0

            if layer == 0:
                P.ins("sync", lambda e: e.dma_start(out=gs_sb[:, :], in_=d["gs128"][:, :]), (s_x2, 16)); c_x2 += 16
                P.ins("sync", lambda e: e.dma_start(out=us_sb[:, :], in_=d["us128"][:, :]), (s_x2, 16)); c_x2 += 16
                P.ins("sync", lambda e: e.dma_start(out=ds_sb[:, :], in_=d["dsT"][:, :]), (s_x2, 16)); c_x2 += 16
                P.wait("vector", s_x2, c_x2)
                gW, uW, dW = d["gw_r"], d["uw_r"], d["dw_r"]
                gC, uC, dC = d["gw_c"], d["uw_c"], d["dw_c"]
            else:
                P.ins("sync", lambda e: e.dma_start(out=cc_sb[:, :], in_=d["c_col"][:, :]), (s_x2, 16)); c_x2 += 16
                P.wait("vector", s_x2, c_x2)
                gC, dC = d["qfc_c"], d["qproj_c"]

            for blk in range(NBLK):
                use_f32 = (layer == 0) and (not have_cache) and (blk == 0)
                for ts_ln in range(4 * blk, 4 * blk + 4):
                    ln.emit(ts_ln)
                for it in range(NI):
                    sl = it % 2
                    git = blk * NI + it
                    wload = "gpsimd" if use_f32 else "sync"
                    if git >= 2:
                        P.wait(wload, s_mm, mm_done_at_it[git - 2])
                    if use_f32:
                        if git >= 2:
                            P.wait(wload, s_cw[sl], c_cw[sl])
                        P.ins(wload, lambda e, sl=sl, it=it: e.dma_start(
                            out=gwb[sl][:, :], in_=gW[it, :, :]), (s_wgf[sl], 16))
                        c_wgf[sl] += 16
                        wg_rdy[git] = (s_wgf[sl], c_wgf[sl])
                        P.ins(wload, lambda e, sl=sl, it=it: e.dma_start(
                            out=uwb[sl][:, :], in_=uW[it, :, :]), (s_wuf[sl], 16))
                        c_wuf[sl] += 16
                        wu_rdy[git] = (s_wuf[sl], c_wuf[sl])
                        P.wait("sync", s_wgf[sl], c_wgf[sl])
                        P.ins("sync", lambda e, sl=sl, it=it: e.dma_start(
                            out=gC[it, :, :], in_=gwb[sl][:, :]), (s_cw[sl], 16))
                        c_cw[sl] += 16
                        P.wait("sync", s_wuf[sl], c_wuf[sl])
                        P.ins("sync", lambda e, sl=sl, it=it: e.dma_start(
                            out=uC[it, :, :], in_=uwb[sl][:, :]), (s_cw[sl], 16))
                        c_cw[sl] += 16
                    else:
                        if (layer == 0) and (not have_cache) and it == 0:
                            P.wait(wload, s_cw[0], c_cw[0])
                            P.wait(wload, s_cw[1], c_cw[1])
                        P.ins(wload, lambda e, sl=sl, it=it: e.dma_start(
                            out=gwb[sl][:, :], in_=gC[it, :, :]), (s_wg[sl], 16))
                        c_wg[sl] += 16
                        wg_rdy[git] = (s_wg[sl], c_wg[sl])
                        if layer == 0:
                            P.ins(wload, lambda e, sl=sl, it=it: e.dma_start(
                                out=uwb[sl][:, :], in_=uC[it, :, :]), (s_wu[sl], 16))
                            c_wu[sl] += 16
                            wu_rdy[git] = (s_wu[sl], c_wu[sl])
                    P.wait("tensor", *wg_rdy[git])
                    if layer == 0:
                        P.wait("tensor", *wu_rdy[git])
                    if it == 0:
                        P.wait("tensor", ln.s_e, ln.e_count(4 * blk + 3))
                    if git >= 2:
                        P.wait("tensor", s_ev, ev_done_at_it[git - 2])
                    if blk > 0 and it < 2:
                        P.wait("tensor", s_xn, xn_total)
                    for kh in range(KH):
                        P.ins("tensor", lambda e, sl=sl, kh=kh, blk=blk: e.matmul(
                            pg[sl][:, :], gwb[sl][:, kh * 128:(kh + 1) * 128],
                            xnT[:, kh * 2048 + blk * 512: kh * 2048 + (blk + 1) * 512],
                            start=(kh == 0), stop=(kh == KH - 1)),
                            (s_mm, 1) if (kh == KH - 1 and layer != 0) else None)
                    if layer == 0:
                        for kh in range(KH):
                            P.ins("tensor", lambda e, sl=sl, kh=kh, blk=blk: e.matmul(
                                pu[sl][:, :], uwb[sl][:, kh * 128:(kh + 1) * 128],
                                xnT[:, kh * 2048 + blk * 512: kh * 2048 + (blk + 1) * 512],
                                start=(kh == 0), stop=(kh == KH - 1)),
                                (s_mm, 1) if kh == KH - 1 else None)
                    c_mm += 1
                    mm_done_at_it[git] = c_mm
                    if layer == 0:
                        P.wait("vector", s_mm, c_mm)
                        if git >= 2:
                            P.wait("vector", s_sc, git - 1)
                        P.ins("vector", lambda e, sl=sl, it=it: e.tensor_scalar(
                            gbuf[sl][:, :], pg[sl][:, :], gs_sb[:, it:it + 1], None,
                            op0=ALU.mult), (s_ev, 1))
                        c_ev += 1; ev_gbuf = c_ev
                        P.ins("vector", lambda e, sl=sl, it=it: e.tensor_scalar(
                            ubuf[sl][:, :], pu[sl][:, :], us_sb[:, it:it + 1], None,
                            op0=ALU.mult), (s_ev, 1))
                        c_ev += 1
                        ev_done_at_it[git] = c_ev
                        P.wait("scalar", s_ev, ev_gbuf)
                        if git >= 2:
                            P.wait("scalar", s_hd, git - 1)
                        P.ins("scalar", lambda e, sl=sl: e.activation(
                            gsil[sl][:, :], gbuf[sl][:, :], AF.Silu), (s_sc, 1))
                        c_sc += 1
                        P.wait("vector", s_sc, c_sc)
                        P.wait("vector", s_ev, c_ev)
                        if blk > 0 and it == 0:
                            P.wait("vector", s_md, md_total)
                        P.ins("vector", lambda e, sl=sl, it=it: e.tensor_tensor(
                            hdn[:, it * 512:(it + 1) * 512], gsil[sl][:, :],
                            ubuf[sl][:, :], op=ALU.mult), (s_hd, 1))
                        c_hd += 1
                    else:
                        P.wait("scalar", s_mm, c_mm)
                        if git >= 2:
                            P.wait("scalar", s_hd, git - 1)
                        P.ins("scalar", lambda e, sl=sl: e.activation(
                            rbuf[sl][:, :], pg[sl][:, :], AF.Relu), (s_ev, 1))
                        c_ev += 1
                        ev_done_at_it[git] = c_ev
                        P.wait("vector", s_ev, c_ev)
                        if blk > 0 and it == 0:
                            P.wait("vector", s_md, md_total)
                        P.ins("vector", lambda e, sl=sl, it=it: e.tensor_tensor(
                            hdn[:, it * 512:(it + 1) * 512], rbuf[sl][:, :],
                            rbuf[sl][:, :], op=ALU.mult), (s_hd, 1))
                        c_hd += 1

                use_f32_d = (layer == 0) and (not have_cache) and (blk == 0)
                for hc2 in range(8):
                    dload = "gpsimd" if use_f32_d else "sync"
                    sld = hc2 % 2
                    ghc = blk * 8 + hc2
                    if ghc >= 2:
                        P.wait(dload, s_md, (ghc - 1) * 4)
                    if use_f32_d:
                        if ghc >= 2:
                            P.wait(dload, s_cwd[sld], c_cwd[sld])
                        P.ins(dload, lambda e, sld=sld, hc2=hc2: e.dma_start(
                            out=dwb[sld][:, :], in_=dW[hc2, :, :]), (s_wdf[sld], 16))
                        c_wdf[sld] += 16
                        wd_rdy[ghc] = (s_wdf[sld], c_wdf[sld])
                        P.wait("sync", s_wdf[sld], c_wdf[sld])
                        P.ins("sync", lambda e, sld=sld, hc2=hc2: e.dma_start(
                            out=dC[hc2, :, :], in_=dwb[sld][:, :]), (s_cwd[sld], 16))
                        c_cwd[sld] += 16
                    else:
                        if (layer == 0) and (not have_cache) and hc2 == 0:
                            P.wait(dload, s_cwd[0], c_cwd[0])
                            P.wait(dload, s_cwd[1], c_cwd[1])
                        P.ins(dload, lambda e, sld=sld, hc2=hc2: e.dma_start(
                            out=dwb[sld][:, :], in_=dC[hc2, :, :]), (s_wd[sld], 16))
                        c_wd[sld] += 16
                        wd_rdy[ghc] = (s_wd[sld], c_wd[sld])
                    for ts in range(4):
                        gmd = blk * 32 + hc2 * 4 + ts
                        slp = gmd % 2
                        slx = gmd % 2
                        row0 = blk * 512 + ts * 128
                        col0 = hc2 * 256
                        if gmd >= 2:
                            P.wait(x_eng, s_xn, gmd - 1)
                        P.ins(x_eng, lambda e, slx=slx, row0=row0, col0=col0: e.dma_start(
                            out=xo2[slx][:, :], in_=X_in[row0:row0 + 128, col0:col0 + 256]),
                            (s_xo2[slx], 16))
                        c_xo2[slx] += 16
                        P.wait("tensor", *wd_rdy[ghc])
                        if hc2 == 0 and ts == 0:
                            P.wait("tensor", s_hd, c_hd)
                            P.wait("tensor", s_ev, c_ev)
                        if gmd >= 2:
                            P.wait("tensor", s_xn, gmd - 1)
                        for it in range(NI):
                            P.ins("tensor", lambda e, slp=slp, sld=sld, it=it, ts=ts: e.matmul(
                                pg[slp][:, 0:256],
                                hdn[:, it * 512 + ts * 128: it * 512 + (ts + 1) * 128],
                                dwb[sld][:, it * 256:(it + 1) * 256],
                                start=(it == 0), stop=(it == NI - 1)),
                                (s_md, 1) if it == NI - 1 else None)
                        c_md += 1
                        P.wait("vector", s_md, c_md)
                        P.wait("vector", s_xo2[slx], c_xo2[slx])
                        if gmd >= 2:
                            P.wait("vector", s_st[slx], c_st[slx])
                        if layer == 0:
                            P.ins("vector", lambda e, slp=slp, slx=slx, col0=col0: e.tensor_tensor(
                                t1b[slx][:, :], pg[slp][:, 0:256],
                                ds_sb[:, col0:col0 + 256], op=ALU.mult), (s_v2, 1))
                            c_v2 += 1
                            P.wait("vector", s_v2, c_v2)
                            P.ins("vector", lambda e, slx=slx: e.scalar_tensor_tensor(
                                xnew[slx][:, :], t1b[slx][:, :], 0.5, xo2[slx][:, :],
                                op0=ALU.mult, op1=ALU.add), (s_xn, 1))
                        else:
                            P.ins("vector", lambda e, slp=slp, slx=slx: e.scalar_tensor_tensor(
                                xnew[slx][:, :], pg[slp][:, 0:256], cc_sb[:, 0:1],
                                xo2[slx][:, :], op0=ALU.mult, op1=ALU.add), (s_xn, 1))
                        c_xn += 1
                        P.wait(x_eng, s_xn, c_xn)
                        P.ins(x_eng, lambda e, slx=slx, row0=row0, col0=col0: e.dma_start(
                            out=X_out[row0:row0 + 128, col0:col0 + 256], in_=xnew[slx][:, :]),
                            (s_st[slx], 16))
                        c_st[slx] += 16
                md_total = c_md
                xn_total = c_xn

            if extra is not None:
                extra(nc, P, st2, sp)
            P.wait(x_eng, s_st[0], c_st[0])
            P.wait(x_eng, s_st[1], c_st[1])
            for i in range(2):
                if c_cw[i]:
                    P.wait("sync", s_cw[i], c_cw[i])
                if c_cwd[i]:
                    P.wait("sync", s_cwd[i], c_cwd[i])
            P.emit(nc)



SC0 = 0.08838834764831845

def build_attn0(nc, d, sp, X_in, X_out, extra=None):
    pref = "a0_"
    sem = lambda n: sp.enter_context(nc.semaphore(pref + n))
    with ExitStack() as stA:
        sbA = lambda n, sh, dt=F32: stA.enter_context(nc.sbuf_tensor(pref + n, sh, dt, side="right"))
        KT = sbA("KT", [128, 4 * 2048], BF16)
        V = sbA("V", [128, 16 * 512], BF16)
        cosb = sbA("cosb", [128, 2048], BF16)
        sinb = sbA("sinb", [128, 2048], BF16)
        mk0 = sbA("mk0", [128, 4 * 512], BF16)

        with ExitStack() as stB:
            sbB = lambda n, sh, dt=F32: stB.enter_context(nc.sbuf_tensor(pref + n, sh, dt))
            xnT = sbB("xnT", [128, 16 * 2048], BF16)

            with ExitStack() as st1:
                P = Plan()
                plan_ln(nc, P, st1, sp, d, X_in, xnT[:, :], pref + "ln_")
                P.emit(nc)

            with ExitStack() as st2:
                sb2 = lambda n, sh, dt=F32: st2.enter_context(nc.sbuf_tensor(pref + n, sh, dt))
                ps2 = lambda n, sh, dt=F32: st2.enter_context(nc.psum_tensor(pref + n, sh, dt))
                P = Plan()
                wkd = sb2("wkd", [128, 16 * 256], BF16)
                wku = sb2("wku", [128, 2 * 512], BF16)
                wvd = sb2("wvd", [128, 16 * 384], BF16)
                wvu = sb2("wvu", [128, 3 * 512], BF16)
                kdT = sb2("kdT", [128, 2 * 2048], BF16)
                vdT = sb2("vdT", [128, 3 * 2048], BF16)
                kanc = sb2("kanc", [128, 4])
                vanc = sb2("vanc", [128, 512])
                kf = [sb2(f"kf{i}", [128, 512]) for i in range(2)]
                rt1 = sb2("rt1", [128, 512])
                rt2 = sb2("rt2", [128, 512])
                pk = [ps2(f"pk{i}", [128, 512]) for i in range(2)]

                s_wk, s_ck = sem("swk"), sem("sck")
                s_t1, s_e1, s_v1 = sem("st1"), sem("se1"), sem("sv1")
                c_wk = c_ck = c_t1 = c_e1 = c_v1 = 0
                pk_free = [None, None]

                P.ins("gpsimd", lambda e: e.dma_start(out=wkd[:, :], in_=d["wkd_r"][:, :]), (s_wk, 16)); c_wk += 16
                P.ins("gpsimd", lambda e: e.dma_start(out=wku[:, :], in_=d["wku_r"][:, :]), (s_wk, 16)); c_wk += 16
                P.ins("gpsimd", lambda e: e.dma_start(out=wvd[:, :], in_=d["wvd_r"][:, :]), (s_wk, 16)); c_wk += 16
                P.ins("gpsimd", lambda e: e.dma_start(out=wvu[:, :], in_=d["wvu_r"][:, :]), (s_wk, 16)); c_wk += 16
                P.ins("sync", lambda e: e.dma_start(out=kanc[:, :], in_=d["kanc"][:, :]), (s_ck, 16)); c_ck += 16
                P.ins("sync", lambda e: e.dma_start(out=vanc[:, :], in_=d["vanc"][:, :]), (s_ck, 16)); c_ck += 16
                P.ins("sync", lambda e: e.dma_start(out=cosb[:, :], in_=d["cosT"][:, :]), (s_ck, 16)); c_ck += 16
                P.ins("sync", lambda e: e.dma_start(out=sinb[:, :], in_=d["sinT"][:, :]), (s_ck, 16)); c_ck += 16
                for j in range(4):
                    P.ins("sync", lambda e, j=j: e.dma_start(
                        out=mk0[:, j * 512:(j + 1) * 512], in_=d["masks0"][j, :, :]), (s_ck, 16)); c_ck += 16

                g = 0
                P.wait("tensor", s_wk, c_wk)
                for rt in range(2):
                    for tc in range(4):
                        psl = g % 2
                        if pk_free[psl] is not None:
                            P.wait("tensor", *pk_free[psl])
                        for kh in range(KH):
                            P.ins("tensor", lambda e, psl=psl, kh=kh, rt=rt, tc=tc: e.matmul(
                                pk[psl][:, :], wkd[:, kh * 256 + rt * 128: kh * 256 + rt * 128 + 128],
                                xnT[:, kh * 2048 + tc * 512: kh * 2048 + (tc + 1) * 512],
                                start=(kh == 0), stop=(kh == KH - 1)),
                                (s_t1, 1) if kh == KH - 1 else None)
                        c_t1 += 1
                        P.wait("scalar", s_t1, c_t1)
                        P.ins("scalar", lambda e, psl=psl, rt=rt, tc=tc: e.copy(
                            kdT[:, rt * 2048 + tc * 512: rt * 2048 + (tc + 1) * 512],
                            pk[psl][:, :]), (s_e1, 1))
                        c_e1 += 1
                        pk_free[psl] = (s_e1, c_e1)
                        g += 1
                P.wait("tensor", s_wk, c_wk)
                for rt in range(3):
                    for tc in range(4):
                        psl = g % 2
                        P.wait("tensor", *pk_free[psl])
                        for kh in range(KH):
                            P.ins("tensor", lambda e, psl=psl, kh=kh, rt=rt, tc=tc: e.matmul(
                                pk[psl][:, :], wvd[:, kh * 384 + rt * 128: kh * 384 + rt * 128 + 128],
                                xnT[:, kh * 2048 + tc * 512: kh * 2048 + (tc + 1) * 512],
                                start=(kh == 0), stop=(kh == KH - 1)),
                                (s_t1, 1) if kh == KH - 1 else None)
                        c_t1 += 1
                        P.wait("scalar", s_t1, c_t1)
                        P.ins("scalar", lambda e, psl=psl, rt=rt, tc=tc: e.copy(
                            vdT[:, rt * 2048 + tc * 512: rt * 2048 + (tc + 1) * 512],
                            pk[psl][:, :]), (s_e1, 1))
                        c_e1 += 1
                        pk_free[psl] = (s_e1, c_e1)
                        g += 1
                P.wait("tensor", s_wk, c_wk)
                P.wait("vector", s_ck, c_ck)
                for dk in range(4):
                    for tc in range(4):
                        psl = g % 2
                        kfs = g % 2
                        P.wait("tensor", *pk_free[psl])
                        for rt in range(2):
                            P.ins("tensor", lambda e, psl=psl, rt=rt, dk=dk, tc=tc: e.matmul(
                                pk[psl][:, :], wku[:, rt * 512 + dk * 128: rt * 512 + dk * 128 + 128],
                                kdT[:, rt * 2048 + tc * 512: rt * 2048 + (tc + 1) * 512],
                                start=(rt == 0), stop=(rt == 1)),
                                (s_t1, 1) if rt == 1 else None)
                        c_t1 += 1
                        P.wait("vector", s_t1, c_t1)
                        P.ins("vector", lambda e, psl=psl, kfs=kfs, dk=dk: e.tensor_scalar(
                            kf[kfs][:, :], pk[psl][:, :], kanc[:, dk:dk + 1], None,
                            op0=ALU.add), (s_v1, 1))
                        c_v1 += 1
                        pk_free[psl] = (s_v1, c_v1)
                        P.wait("vector", s_v1, c_v1)
                        cs_ = lambda t: slice(t * 512, (t + 1) * 512)
                        P.ins("vector", lambda e, kfs=kfs, tc=tc: e.tensor_tensor(
                            rt1[:, :], kf[kfs][:, :], cosb[:, tc * 512:(tc + 1) * 512],
                            op=ALU.mult), (s_v1, 1)); c_v1 += 1
                        P.ins("vector", lambda e, kfs=kfs, tc=tc: e.tensor_tensor(
                            rt2[0:64, :], kf[kfs][64:128, :], sinb[64:128, tc * 512:(tc + 1) * 512],
                            op=ALU.mult), (s_v1, 1)); c_v1 += 1
                        P.ins("vector", lambda e, kfs=kfs, tc=tc: e.tensor_tensor(
                            rt2[64:128, :], kf[kfs][0:64, :], sinb[0:64, tc * 512:(tc + 1) * 512],
                            op=ALU.mult), (s_v1, 1)); c_v1 += 1
                        P.wait("vector", s_v1, c_v1)
                        P.ins("vector", lambda e, dk=dk, tc=tc: e.tensor_tensor(
                            KT[:, dk * 2048 + tc * 512: dk * 2048 + (tc + 1) * 512],
                            rt1[:, :], rt2[:, :], op=ALU.add), (s_v1, 1)); c_v1 += 1
                        g += 1
                for ts in range(NTS):
                    psl = g % 2
                    P.wait("tensor", *pk_free[psl])
                    for rt in range(3):
                        P.ins("tensor", lambda e, psl=psl, rt=rt, ts=ts: e.matmul(
                            pk[psl][:, :], vdT[:, rt * 2048 + ts * 128: rt * 2048 + ts * 128 + 128],
                            wvu[:, rt * 512:(rt + 1) * 512],
                            start=(rt == 0), stop=(rt == 2)),
                            (s_t1, 1) if rt == 2 else None)
                    c_t1 += 1
                    P.wait("vector", s_t1, c_t1)
                    P.ins("vector", lambda e, psl=psl, ts=ts: e.tensor_tensor(
                        V[:, ts * 512:(ts + 1) * 512], pk[psl][:, :], vanc[:, :],
                        op=ALU.add), (s_v1, 1))
                    c_v1 += 1
                    pk_free[psl] = (s_v1, c_v1)
                    g += 1
                P.emit(nc)

            QO = stA.enter_context(nc.sbuf_tensor(pref + "QO", [128, 16 * 2048], BF16, side="right"))
            with ExitStack() as st3:
                sb3 = lambda n, sh, dt=F32: st3.enter_context(nc.sbuf_tensor(pref + n, sh, dt))
                ps3 = lambda n, sh, dt=F32: st3.enter_context(nc.psum_tensor(pref + n, sh, dt))
                P = Plan()
                wq = [sb3(f"wq{i}", [128, 2048], BF16) for i in range(2)]
                expT = sb3("expT", [128, 4 * 512], BF16)
                rt1 = sb3("rt1b", [128, 512])
                rt2 = sb3("rt2b", [128, 512])
                rs = sb3("rs", [1, 512])
                bc = sb3("bc", [128, 512])
                pq = ps3("pq", [128, 512])
                pss = [ps3(f"pss{i}", [128, 512]) for i in range(2)]
                po = [ps3(f"po{i}", [128, 512]) for i in range(2)]
                psum_s = [ps3(f"psum_s{i}", [1, 512]) for i in range(2)]
                pbc = ps3("pbc", [128, 512])

                s_wq = [sem("swq0"), sem("swq1")]
                s_tq, s_vq = sem("stq"), sem("svq")
                s_ts, s_se, s_vm, s_tp = sem("sts"), sem("sse"), sem("svm"), sem("stp")
                s_vr, s_tb, s_vb, s_qo = sem("svr"), sem("stb"), sem("svb"), sem("sqo")
                c_wq = [0, 0]
                c_tq = c_vq = c_ts = c_se = c_vm = c_tp = c_vr = c_tb = c_vb = c_qo = 0
                tq_done = {}
                vq_done = {}
                pq_free = None
                pss_free = [None, None]
                expT_free = [None] * 4
                po_free = [None, None]
                pbc_free = None
                pending_norm = None
                nslot = 0

                def flush_norm():
                    nonlocal c_vr, c_tb, c_vb, c_qo, pbc_free, pending_norm
                    if pending_norm is None:
                        return
                    ph, pqc, pslot, ptp = pending_norm
                    pending_norm = None
                    P.wait("vector", s_tp, ptp)
                    P.ins("vector", lambda e, pslot=pslot: e.reciprocal(
                        rs[:, :], psum_s[pslot][:, :]), (s_vr, 1))
                    c_vr += 1
                    P.wait("tensor", s_vr, c_vr)
                    if pbc_free is not None:
                        P.wait("tensor", *pbc_free)
                    P.ins("tensor", lambda e: e.matmul(
                        pbc[:, :], d["ones_row_sb"], rs[:, :], start=True, stop=True), (s_tb, 1))
                    c_tb += 1
                    P.wait("vector", s_tb, c_tb)
                    P.ins("vector", lambda e: e.tensor_copy(bc[:, :], pbc[:, :]), (s_vb, 1))
                    c_vb += 1
                    pbc_free = (s_vb, c_vb)
                    P.wait("vector", s_vb, c_vb)
                    P.ins("vector", lambda e, ph=ph, pqc=pqc, pslot=pslot: e.tensor_tensor(
                        QO[:, ph * 2048 + pqc * 512: ph * 2048 + (pqc + 1) * 512],
                        po[pslot][:, :], bc[:, :], op=ALU.mult), (s_qo, 1))
                    c_qo += 1
                    po_free[pslot] = (s_qo, c_qo)

                for h in range(17):
                    if h < 16:
                        slw = h % 2
                        if h >= 2:
                            P.wait("gpsimd", s_tq, tq_done[h - 2])
                        P.ins("gpsimd", lambda e, slw=slw, h=h: e.dma_start(
                            out=wq[slw][:, :], in_=d["wq_r"][h, :, :]), (s_wq[slw], 16))
                        c_wq[slw] += 16
                    for step in range(4):
                        if h < 16:
                            slw = h % 2
                            tc = step
                            if step == 0:
                                P.wait("tensor", s_wq[slw], c_wq[slw])
                            if pq_free is not None:
                                P.wait("tensor", *pq_free)
                            for kh in range(KH):
                                P.ins("tensor", lambda e, kh=kh, slw=slw, tc=tc: e.matmul(
                                    pq[:, :], wq[slw][:, kh * 128:(kh + 1) * 128],
                                    xnT[:, kh * 2048 + tc * 512: kh * 2048 + (tc + 1) * 512],
                                    start=(kh == 0), stop=(kh == KH - 1)),
                                    (s_tq, 1) if kh == KH - 1 else None)
                            c_tq += 1
                            P.wait("vector", s_tq, c_tq)
                            P.ins("vector", lambda e, tc=tc: e.tensor_tensor(
                                rt1[:, :], pq[:, :], cosb[:, tc * 512:(tc + 1) * 512],
                                op=ALU.mult), (s_vq, 1)); c_vq += 1
                            P.ins("vector", lambda e, tc=tc: e.tensor_tensor(
                                rt2[0:64, :], pq[64:128, :], sinb[64:128, tc * 512:(tc + 1) * 512],
                                op=ALU.mult), (s_vq, 1)); c_vq += 1
                            P.ins("vector", lambda e, tc=tc: e.tensor_tensor(
                                rt2[64:128, :], pq[0:64, :], sinb[0:64, tc * 512:(tc + 1) * 512],
                                op=ALU.mult), (s_vq, 1)); c_vq += 1
                            P.wait("vector", s_vq, c_vq)
                            P.ins("vector", lambda e, h=h, tc=tc: e.tensor_tensor(
                                QO[:, h * 2048 + tc * 512: h * 2048 + (tc + 1) * 512],
                                rt1[:, :], rt2[:, :], op=ALU.add), (s_vq, 1)); c_vq += 1
                            pq_free = (s_vq, c_vq)
                            vq_done[(h, tc)] = c_vq
                            if step == 3:
                                tq_done[h] = c_tq
                        if h >= 1:
                            ah = h - 1
                            kv = ah // 4
                            qc = step
                            nkt = 4 * qc + 4
                            rdy = {}
                            def emit_pv(k, ah=ah, qc=qc, nkt=nkt, nslot=None):
                                nonlocal c_tp
                                kv_ = ah // 4
                                P.wait("tensor", *rdy[k])
                                if k == 0 and po_free[nslot] is not None:
                                    P.wait("tensor", *po_free[nslot])
                                P.ins("tensor", lambda e, k=k, kv_=kv_, nslot=nslot, nkt=nkt: e.matmul(
                                    po[nslot][:, :], V[:, k * 512 + kv_ * 128: k * 512 + kv_ * 128 + 128],
                                    expT[:, (k % 4) * 512:(k % 4 + 1) * 512],
                                    start=(k == 0), stop=(k == nkt - 1)))
                                P.ins("tensor", lambda e, k=k, nslot=nslot, nkt=nkt: e.matmul(
                                    psum_s[nslot][:, :], d["ones_col_sb"],
                                    expT[:, (k % 4) * 512:(k % 4 + 1) * 512],
                                    start=(k == 0), stop=(k == nkt - 1)), (s_tp, 1))
                                c_tp += 1
                                expT_free[k % 4] = (s_tp, c_tp)
                            for kt in range(nkt):
                                psl = kt % 2
                                if kt == 0:
                                    P.wait("tensor", s_vq, vq_done[(ah, qc)])
                                if kt == 2:
                                    flush_norm()
                                if pss_free[psl] is not None:
                                    P.wait("tensor", *pss_free[psl])
                                P.ins("tensor", lambda e, psl=psl, kv=kv, kt=kt, ah=ah, qc=qc: e.matmul(
                                    pss[psl][:, :], KT[:, kv * 2048 + kt * 128: kv * 2048 + kt * 128 + 128],
                                    QO[:, ah * 2048 + qc * 512: ah * 2048 + (qc + 1) * 512],
                                    start=True, stop=True), (s_ts, 1))
                                c_ts += 1
                                P.wait("scalar", s_ts, c_ts)
                                if expT_free[kt % 4] is not None:
                                    P.wait("scalar", *expT_free[kt % 4])
                                P.ins("scalar", lambda e, psl=psl, kt=kt: e.activation(
                                    expT[:, (kt % 4) * 512:(kt % 4 + 1) * 512], pss[psl][:, :],
                                    AF.Exp, scale=SC0), (s_se, 1))
                                c_se += 1
                                pss_free[psl] = (s_se, c_se)
                                rdy[kt] = (s_se, c_se)
                                if kt >= 4 * qc:
                                    j = kt - 4 * qc
                                    P.wait("vector", s_se, c_se)
                                    P.ins("vector", lambda e, kt=kt, j=j: e.tensor_tensor(
                                        expT[:, (kt % 4) * 512:(kt % 4 + 1) * 512],
                                        expT[:, (kt % 4) * 512:(kt % 4 + 1) * 512],
                                        mk0[:, j * 512:(j + 1) * 512], op=ALU.mult), (s_vm, 1))
                                    c_vm += 1
                                    rdy[kt] = (s_vm, c_vm)
                                if kt >= 1:
                                    emit_pv(kt - 1, nslot=nslot)
                            emit_pv(nkt - 1, nslot=nslot)
                            pending_norm = (ah, qc, nslot, c_tp)
                            nslot = 1 - nslot
                flush_norm()
                if extra is not None:
                    extra(nc, P, st3, sp)
                P.emit(nc)

        with ExitStack() as st4:
            sb4 = lambda n, sh, dt=F32: st4.enter_context(nc.sbuf_tensor(pref + n, sh, dt))
            ps4 = lambda n, sh, dt=F32: st4.enter_context(nc.psum_tensor(pref + n, sh, dt))
            P = Plan()
            wo = [sb4(f"wo{i}", [128, 16 * 512], BF16) for i in range(2)]
            xo3 = [sb4(f"xo3{i}", [128, 512]) for i in range(2)]
            xn3 = [sb4(f"xn3{i}", [128, 512]) for i in range(2)]
            py = [ps4(f"py{i}", [128, 512]) for i in range(2)]
            s_wo = [sem("swo0"), sem("swo1")]
            s_xo = [sem("sxoa"), sem("sxob")]
            s_sy = [sem("ssya"), sem("ssyb")]
            s_ty, s_vy = sem("sty"), sem("svy")
            c_wo = [0, 0]
            c_xo = [0, 0]
            c_sy = [0, 0]
            c_ty = c_vy = 0
            for hc in range(4):
                slw = hc % 2
                if hc >= 2:
                    P.wait("gpsimd", s_ty, (hc - 1) * 16)
                P.ins("gpsimd", lambda e, slw=slw, hc=hc: e.dma_start(
                    out=wo[slw][:, :], in_=d["wo_r"][hc, :, :]), (s_wo[slw], 16))
                c_wo[slw] += 16
                for ts in range(NTS):
                    g = hc * 16 + ts
                    psl = g % 2
                    slx = g % 2
                    if g >= 2:
                        P.wait("sync", s_vy, g - 1)
                    P.ins("sync", lambda e, slx=slx, ts=ts, hc=hc: e.dma_start(
                        out=xo3[slx][:, :], in_=X_in[ts * 128:(ts + 1) * 128, hc * 512:(hc + 1) * 512]),
                        (s_xo[slx], 16))
                    c_xo[slx] += 16
                    P.wait("tensor", s_wo[slw], c_wo[slw])
                    if g >= 2:
                        P.wait("tensor", s_vy, g - 1)
                    for dt in range(16):
                        P.ins("tensor", lambda e, psl=psl, slw=slw, dt=dt, ts=ts: e.matmul(
                            py[psl][:, :], QO[:, dt * 2048 + ts * 128: dt * 2048 + ts * 128 + 128],
                            wo[slw][:, dt * 512:(dt + 1) * 512],
                            start=(dt == 0), stop=(dt == 15)),
                            (s_ty, 1) if dt == 15 else None)
                    c_ty += 1
                    P.wait("vector", s_ty, c_ty)
                    P.wait("vector", s_xo[slx], c_xo[slx])
                    if g >= 2:
                        P.wait("vector", s_sy[slx], c_sy[slx])
                    P.ins("vector", lambda e, psl=psl, slx=slx: e.tensor_tensor(
                        xn3[slx][:, :], py[psl][:, :], xo3[slx][:, :], op=ALU.add), (s_vy, 1))
                    c_vy += 1
                    P.wait("sync", s_vy, c_vy)
                    P.ins("sync", lambda e, slx=slx, ts=ts, hc=hc: e.dma_start(
                        out=X_out[ts * 128:(ts + 1) * 128, hc * 512:(hc + 1) * 512], in_=xn3[slx][:, :]),
                        (s_sy[slx], 16))
                    c_sy[slx] += 16
            P.wait("sync", s_sy[0], c_sy[0])
            P.wait("sync", s_sy[1], c_sy[1])
            P.emit(nc)



def build_attn1(nc, d, sp, X_in, X_out, qkv_s, qT_s, kT_s):
    pref = "a1_"
    sem = lambda n: sp.enter_context(nc.semaphore(pref + n))

    with ExitStack() as stB:
        xnT = stB.enter_context(nc.sbuf_tensor(pref + "xnT", [128, 16 * 2048], BF16))
        with ExitStack() as st1:
            P = Plan()
            plan_ln(nc, P, st1, sp, d, X_in, xnT[:, :], pref + "ln_")
            P.emit(nc)
        with ExitStack() as st2:
            sb2 = lambda n, sh, dt=F32: st2.enter_context(nc.sbuf_tensor(pref + n, sh, dt))
            ps2 = lambda n, sh, dt=F32: st2.enter_context(nc.psum_tensor(pref + n, sh, dt))
            P = Plan()
            wqv = [sb2(f"wqv{i}", [128, 16 * 512], BF16) for i in range(2)]
            qb = [sb2(f"qb{i}", [128, 512], BF16) for i in range(2)]
            pqk = [ps2(f"pqk{i}", [128, 512]) for i in range(2)]
            s_w = [sem("sw0"), sem("sw1")]
            s_t, s_e = sem("sqt"), sem("sqe")
            s_st = [sem("sqsa"), sem("sqsb")]
            c_w = [0, 0]; c_st = [0, 0]; c_t = c_e = 0
            for jc in range(12):
                sl = jc % 2
                if jc >= 2:
                    P.wait("gpsimd", s_t, (jc - 1) * 16)
                P.ins("gpsimd", lambda e, sl=sl, jc=jc: e.dma_start(
                    out=wqv[sl][:, :], in_=d["wqkv_r"][jc, :, :]), (s_w[sl], 16))
                c_w[sl] += 16
                P.wait("tensor", s_w[sl], c_w[sl])
                for ts in range(NTS):
                    g = jc * 16 + ts
                    psl = g % 2
                    if g >= 2:
                        P.wait("tensor", s_e, g - 1)
                    for kh in range(KH):
                        P.ins("tensor", lambda e, psl=psl, kh=kh, sl=sl, ts=ts: e.matmul(
                            pqk[psl][:, :], xnT[:, kh * 2048 + ts * 128: kh * 2048 + ts * 128 + 128],
                            wqv[sl][:, kh * 512:(kh + 1) * 512],
                            start=(kh == 0), stop=(kh == KH - 1)),
                            (s_t, 1) if kh == KH - 1 else None)
                    c_t += 1
                    P.wait("scalar", s_t, c_t)
                    if g >= 2:
                        P.wait("scalar", s_st[g % 2], c_st[g % 2])
                    P.ins("scalar", lambda e, psl=psl, g=g: e.copy(
                        qb[g % 2][:, :], pqk[psl][:, :]), (s_e, 1))
                    c_e += 1
                    P.wait("sync", s_e, c_e)
                    P.ins("sync", lambda e, g=g, ts=ts, jc=jc: e.dma_start(
                        out=qkv_s[ts * 128:(ts + 1) * 128, jc * 512:(jc + 1) * 512],
                        in_=qb[g % 2][:, :]), (s_st[g % 2], 16))
                    c_st[g % 2] += 16
            P.wait("sync", s_st[0], c_st[0])
            P.wait("sync", s_st[1], c_st[1])
            P.emit(nc)

    with ExitStack() as st3:
        sb3 = lambda n, sh, dt=F32: st3.enter_context(nc.sbuf_tensor(pref + n, sh, dt))
        ps3 = lambda n, sh, dt=F32: st3.enter_context(nc.psum_tensor(pref + n, sh, dt))
        P = Plan()
        qkt = [sb3(f"qkt{i}", [128, 4096], BF16) for i in range(2)]
        qnw = sb3("qnw", [128, 2048], BF16)
        knw = sb3("knw", [128, 2048], BF16)
        ctm = sb3("ctm", [128, 16 * 128], BF16)
        stm = sb3("stm", [128, 16 * 128], BF16)
        sqd = sb3("q2sqd", [128, 2048])
        lnt = sb3("q2lnt", [128, 8])
        qn = [sb3(f"qn{i}", [128, 2048], BF16) for i in range(2)]
        rt1 = [sb3(f"q2rt1{i}", [128, 2048], BF16) for i in range(2)]
        rth = [sb3(f"q2rth{i}", [128, 2048], BF16) for i in range(2)]
        qrb = [sb3(f"qrb{i}", [128, 2048], BF16) for i in range(2)]
        qTs = [sb3(f"qTs{i}", [128, 2048], BF16) for i in range(2)]
        pst = [ps3(f"q2pst{i}", [128, 512], BF16) for i in range(8)]
        identb = d["identb_sb"]
        pending_ev = None
        s_l, s_v, s_s, s_t, s_e = (sem("r_l"), sem("r_v"), sem("r_s"),
                                   sem("r_t"), sem("r_e"))
        s_lq = [sem("r_lqa"), sem("r_lqb")]
        s_st = [sem("r_sta"), sem("r_stb")]
        c_lq = [0, 0]; c_stk = [0, 0]
        c_l = c_v = c_s = c_t = c_e = c_g = 0
        stt_v = {}
        s_done = {}
        P.ins("sync", lambda e: e.dma_start(out=qnw[:, :], in_=d["qnw"][:, :]), (s_l, 16)); c_l += 16
        P.ins("sync", lambda e: e.dma_start(out=knw[:, :], in_=d["knw"][:, :]), (s_l, 16)); c_l += 16
        for ts in range(16):
            P.ins("sync", lambda e, ts=ts: e.dma_start(
                out=ctm[:, ts * 128:(ts + 1) * 128], in_=d["cosTM_r"][ts, :, :]), (s_l, 16)); c_l += 16
            P.ins("sync", lambda e, ts=ts: e.dma_start(
                out=stm[:, ts * 128:(ts + 1) * 128], in_=d["sinTM_r"][ts, :, :]), (s_l, 16)); c_l += 16
        grp = 0
        done_l = {}
        for ts in range(NTS):
            sl = ts % 2
            if ts >= 2:
                P.wait("sync", s_v, done_l[ts - 2])
                P.wait("sync", s_s, s_done[ts - 2])
            P.ins("sync", lambda e, sl=sl, ts=ts: e.dma_start(
                out=qkt[sl][:, :], in_=qkv_s[ts * 128:(ts + 1) * 128, 0:4096]), (s_lq[sl], 16))
            c_lq[sl] += 16
            for kind in range(2):
                off = kind * 2048
                nw = qnw if kind == 0 else knw
                dstT = qT_s if kind == 0 else kT_s
                lcol = kind * 4
                P.wait("scalar", s_l, c_l)
                P.wait("scalar", s_lq[sl], c_lq[sl])
                if ts >= 1:
                    P.wait("scalar", s_v, stt_v[(ts - 1, kind)])
                P.ins("scalar", lambda e, sl=sl, off=off, lcol=lcol: e.activation(
                    sqd[:, :], qkt[sl][:, off:off + 2048], AF.Square,
                    accum_out=lnt[:, lcol:lcol + 1]), (s_s, 1)); c_s += 1
                P.wait("scalar", s_s, c_s)
                P.ins("scalar", lambda e, lcol=lcol: e.activation(
                    lnt[:, lcol + 2:lcol + 3], lnt[:, lcol:lcol + 1], AF.Ln,
                    bias=d["eps_rms_col"], scale=1.0 / 2048.0), (s_s, 1)); c_s += 1
                P.wait("scalar", s_s, c_s)
                P.ins("scalar", lambda e, lcol=lcol: e.activation(
                    lnt[:, lcol + 3:lcol + 4], lnt[:, lcol + 2:lcol + 3], AF.Exp,
                    scale=-0.5), (s_s, 1)); c_s += 1
                cosAP = bass.AP(ctm[:, :].tensor, ts * 128, [[16 * 128, 128], [0, 16], [1, 128]])
                sinHI = bass.AP(stm[:, :].tensor, ts * 128 + 64, [[16 * 128, 128], [0, 16], [1, 64]])
                sinLO = bass.AP(stm[:, :].tensor, ts * 128, [[16 * 128, 128], [0, 16], [1, 64]])
                qnLO = bass.AP(qn[kind][:, :].tensor, 0, [[2048, 128], [128, 16], [1, 64]])
                qnHI = bass.AP(qn[kind][:, :].tensor, 64, [[2048, 128], [128, 16], [1, 64]])
                rthLO = bass.AP(rth[kind][:, :].tensor, 0, [[2048, 128], [128, 16], [1, 64]])
                rthHI = bass.AP(rth[kind][:, :].tensor, 64, [[2048, 128], [128, 16], [1, 64]])
                P.wait("vector", s_s, c_s)
                P.ins("vector", lambda e, sl=sl, off=off, nw=nw, lcol=lcol, kind=kind: e.scalar_tensor_tensor(
                    qn[kind][:, :], qkt[sl][:, off:off + 2048], lnt[:, lcol + 3:lcol + 4], nw[:, :],
                    op0=ALU.mult, op1=ALU.mult), (s_v, 1)); c_v += 1
                stt_v[(ts, kind)] = c_v
                P.wait("vector", s_v, c_v)
                P.ins("vector", lambda e, cosAP=cosAP, kind=kind: e.tensor_tensor(
                    rt1[kind][:, :], qn[kind][:, :], cosAP, op=ALU.mult), (s_v, 1)); c_v += 1
                P.ins("vector", lambda e, rthLO=rthLO, qnHI=qnHI, sinHI=sinHI: e.tensor_tensor(
                    rthLO, qnHI, sinHI, op=ALU.mult), (s_v, 1)); c_v += 1
                P.ins("vector", lambda e, rthHI=rthHI, qnLO=qnLO, sinLO=sinLO: e.tensor_tensor(
                    rthHI, qnLO, sinLO, op=ALU.mult), (s_v, 1)); c_v += 1
                P.wait("vector", s_v, c_v)
                if grp >= 2:
                    P.wait("vector", s_t, 4 * (grp - 1))
                P.ins("vector", lambda e, kind=kind: e.tensor_tensor(
                    qrb[kind][:, :], rt1[kind][:, :], rth[kind][:, :], op=ALU.add), (s_v, 1)); c_v += 1
                P.wait("tensor", s_v, c_v)
                tsl = kind
                if grp >= 2:
                    P.wait("tensor", s_st[tsl], c_stk[tsl])
                for g4 in range(4):
                    psl = (grp * 4 + g4) % 8
                    if grp * 4 + g4 >= 8:
                        P.wait("tensor", s_e, grp * 4 + g4 - 7)
                    for j in range(4):
                        dt = 4 * g4 + j
                        P.ins("tensor", lambda e, psl=psl, dt=dt, j=j, kind=kind: e.transpose(
                            pst[psl][:, j * 128:(j + 1) * 128],
                            qrb[kind][:, dt * 128:(dt + 1) * 128], identb[:, :]),
                            (s_t, 1) if j == 3 else None)
                    c_t += 1
                this_ev = []
                for g4 in range(4):
                    psl = (grp * 4 + g4) % 8
                    tcnt = c_t - 3 + g4
                    this_ev.append((psl, g4, tsl, tcnt))
                this_store = (ts, tsl, dstT, grp)
                if pending_ev is not None:
                    pev, pstore = pending_ev
                    pgrp = pstore[3]
                    ptsl0 = pstore[1]
                    if pgrp >= 2:
                        P.wait("scalar", s_st[ptsl0], c_stk[ptsl0])
                    for (ppsl, pg4, ptsl, ptcnt) in pev:
                        P.wait("scalar", s_t, ptcnt)
                        P.ins("scalar", lambda e, ppsl=ppsl, pg4=pg4, ptsl=ptsl: e.copy(
                            qTs[ptsl][:, pg4 * 512:(pg4 + 1) * 512], pst[ppsl][:, :]), (s_e, 1))
                        c_e += 1
                    P.wait("sync", s_e, c_e)
                    pts, ptsl, pdstT = pstore[:3]
                    P.ins("sync", lambda e, pts=pts, ptsl=ptsl, pdstT=pdstT: e.dma_start(
                        out=pdstT[pts, :, :], in_=qTs[ptsl][:, :]), (s_st[ptsl], 16))
                    c_stk[ptsl] += 16
                pending_ev = (this_ev, this_store)
                grp += 1
            done_l[ts] = c_v
            s_done[ts] = c_s
        if pending_ev is not None:
            pev, pstore = pending_ev
            pgrp = pstore[3]
            ptsl0 = pstore[1]
            if pgrp >= 2:
                P.wait("scalar", s_st[ptsl0], c_stk[ptsl0])
            for (ppsl, pg4, ptsl, ptcnt) in pev:
                P.wait("scalar", s_t, ptcnt)
                P.ins("scalar", lambda e, ppsl=ppsl, pg4=pg4, ptsl=ptsl: e.copy(
                    qTs[ptsl][:, pg4 * 512:(pg4 + 1) * 512], pst[ppsl][:, :]), (s_e, 1))
                c_e += 1
            P.wait("sync", s_e, c_e)
            pts, ptsl, pdstT = pstore[:3]
            P.ins("sync", lambda e, pts=pts, ptsl=ptsl, pdstT=pdstT: e.dma_start(
                out=pdstT[pts, :, :], in_=qTs[ptsl][:, :]), (s_st[ptsl], 16))
            c_stk[ptsl] += 16
        P.wait("sync", s_st[0], c_stk[0])
        P.wait("sync", s_st[1], c_stk[1])
        P.emit(nc)

    with ExitStack() as stO:
        outQ = stO.enter_context(nc.sbuf_tensor(pref + "outQ", [128, 16 * 2048], BF16, side="right"))
        with ExitStack() as st4:
            sb4 = lambda n, sh, dt=F32: st4.enter_context(nc.sbuf_tensor(pref + n, sh, dt))
            ps4 = lambda n, sh, dt=F32: st4.enter_context(nc.psum_tensor(pref + n, sh, dt))
            P = Plan()
            KTr = sb4("KTr", [128, 12 * 2048], BF16)
            Vr = sb4("Vr", [128, 12 * 2048], BF16)
            qTq = sb4("qTq", [128, 4 * 2048], BF16)
            mk1 = sb4("mk1", [128, 12 * 512], BF16)
            expT = sb4("a1expT", [128, 3 * 512], BF16)
            eacc = sb4("a1eacc", [128, 512])
            rs = sb4("a1rs", [1, 512])
            bc = sb4("a1bc", [128, 512])
            pss = [ps4(f"a1pss{i}", [128, 512]) for i in range(3)]
            po = [ps4(f"a1po{i}", [128, 512]) for i in range(2)]
            psum_s = [ps4(f"a1ps{i}", [1, 512]) for i in range(2)]
            pbc = ps4("a1pbc", [128, 512])
            s_kl = [sem("w_kl0"), sem("w_kl1")]
            s_ql = [sem("w_ql0"), sem("w_ql1")]
            s_ts, s_se, s_vm, s_tp = sem("w_ts"), sem("w_se"), sem("w_vm"), sem("w_tp")
            s_va, s_sm = sem("w_va"), sem("w_sm")
            s_vr, s_tb, s_vb, s_qo = sem("w_vr"), sem("w_tb"), sem("w_vb"), sem("w_qo")
            c_kl = [0, 0]; c_ql = [0, 0]
            c_ts = c_se = c_vm = c_tp = c_vr = c_tb = c_vb = c_qo = 0
            c_va = c_sm = 0
            psum_s_free = None
            acc_read = [None] * 3
            for j in range(12):
                P.ins("sync", lambda e, j=j: e.dma_start(
                    out=mk1[:, j * 512:(j + 1) * 512], in_=d["masks1"][j, :, :]), (s_kl[0], 16))
                c_kl[0] += 16
            pss_free = [None, None, None]
            expT_free = [None, None, None]
            po_free = [None, None]
            pbc_free = None
            pending_norm = None
            nslot = 0
            tp_at_qc_end = {}

            def flush_norm():
                nonlocal c_vr, c_tb, c_vb, c_qo, pbc_free, pending_norm
                if pending_norm is None:
                    return
                ph, pqc, pslot, ptp = pending_norm
                pending_norm = None
                P.wait("vector", s_tp, ptp)
                P.ins("vector", lambda e, pslot=pslot: e.reciprocal(
                    rs[:, :], psum_s[pslot][:, :]), (s_vr, 1))
                c_vr += 1
                P.wait("tensor", s_vr, c_vr)
                if pbc_free is not None:
                    P.wait("tensor", *pbc_free)
                P.ins("tensor", lambda e: e.matmul(
                    pbc[:, :], d["ones_row_sb"], rs[:, :], start=True, stop=True), (s_tb, 1))
                c_tb += 1
                P.wait("vector", s_tb, c_tb)
                P.ins("vector", lambda e: e.tensor_copy(bc[:, :], pbc[:, :]), (s_vb, 1))
                c_vb += 1
                pbc_free = (s_vb, c_vb)
                P.wait("vector", s_vb, c_vb)
                P.ins("vector", lambda e, ph=ph, pqc=pqc, pslot=pslot: e.tensor_tensor(
                    outQ[:, ph * 2048 + pqc * 512: ph * 2048 + (pqc + 1) * 512],
                    po[pslot][:, :], bc[:, :], op=ALU.mult), (s_qo, 1))
                c_qo += 1
                po_free[pslot] = (s_qo, c_qo)
            for qc in range(4):
                qp = qc % 2
                for kt in range(4 * qc, 4 * qc + 4):
                    slot = kt % 12
                    if qc >= 3:
                        P.wait("sync", s_tp, tp_at_qc_end[qc - 1])
                    P.ins("sync", lambda e, kt=kt, slot=slot: e.dma_start(
                        out=KTr[:, slot * 2048:(slot + 1) * 2048], in_=kT_s[kt, :, :]), (s_kl[qp], 16))
                    c_kl[qp] += 16
                    P.ins("sync", lambda e, kt=kt, slot=slot: e.dma_start(
                        out=Vr[:, slot * 2048:(slot + 1) * 2048],
                        in_=qkv_s[kt * 128:(kt + 1) * 128, 4096:6144]), (s_kl[qp], 16))
                    c_kl[qp] += 16
                if qc >= 1:
                    P.wait("sync", s_ts, c_ts)
                for j in range(4):
                    P.ins("sync", lambda e, j=j, qc=qc: e.dma_start(
                        out=qTq[:, j * 2048:(j + 1) * 2048], in_=qT_s[4 * qc + j, :, :]), (s_ql[qp], 16))
                    c_ql[qp] += 16
                kt_lo = max(0, 4 * qc - 8)
                kts = list(range(kt_lo, 4 * qc + 4))
                for h in range(16):
                    rdy = {}
                    def emit_pv(k, h=h, qc=qc, kts=kts, nslot=None):
                        nonlocal c_tp
                        P.wait("tensor", *rdy[k])
                        if k == kts[0] and po_free[nslot] is not None:
                            P.wait("tensor", *po_free[nslot])
                        P.ins("tensor", lambda e, k=k, h=h, nslot=nslot: e.matmul(
                            po[nslot][:, :], Vr[:, (k % 12) * 2048 + h * 128: (k % 12) * 2048 + h * 128 + 128],
                            expT[:, (k % 3) * 512:(k % 3 + 1) * 512],
                            start=(k == kts[0]), stop=(k == kts[-1])))
                        P.ins("tensor", lambda e, k=k, nslot=nslot: e.matmul(
                            psum_s[nslot][:, :], d["ones_col_sb"],
                            expT[:, (k % 3) * 512:(k % 3 + 1) * 512],
                            start=(k == kts[0]), stop=(k == kts[-1])), (s_tp, 1))
                        c_tp += 1
                        expT_free[k % 3] = (s_tp, c_tp)
                    for ki, kt in enumerate(kts):
                        ktr = kt + 8 - 4 * qc
                        psl = kt % 3
                        if h == 0 and ki == 0:
                            P.wait("tensor", s_kl[0], c_kl[0])
                            P.wait("tensor", s_kl[1], c_kl[1])
                            P.wait("tensor", s_ql[qc % 2], c_ql[qc % 2])
                        if ki == 2:
                            flush_norm()
                        if pss_free[psl] is not None:
                            P.wait("tensor", *pss_free[psl])
                        rhsq = bass.AP(qTq[:, :].tensor, h * 128, [[4 * 2048, 128], [2048, 4], [1, 128]])
                        P.ins("tensor", lambda e, psl=psl, kt=kt, h=h, rhsq=rhsq: e.matmul(
                            pss[psl][:, :],
                            KTr[:, (kt % 12) * 2048 + h * 128: (kt % 12) * 2048 + h * 128 + 128],
                            rhsq, start=True, stop=True), (s_ts, 1))
                        c_ts += 1
                        P.wait("scalar", s_ts, c_ts)
                        if expT_free[kt % 3] is not None:
                            P.wait("scalar", *expT_free[kt % 3])
                        P.ins("scalar", lambda e, psl=psl, kt=kt: e.activation(
                            expT[:, (kt % 3) * 512:(kt % 3 + 1) * 512], pss[psl][:, :],
                            AF.Exp, scale=SC0), (s_se, 1))
                        c_se += 1
                        pss_free[psl] = (s_se, c_se)
                        rdy[kt] = (s_se, c_se)
                        if not (4 <= ktr <= 7):
                            P.wait("vector", s_se, c_se)
                            P.ins("vector", lambda e, kt=kt, ktr=ktr: e.tensor_tensor(
                                expT[:, (kt % 3) * 512:(kt % 3 + 1) * 512],
                                expT[:, (kt % 3) * 512:(kt % 3 + 1) * 512],
                                mk1[:, ktr * 512:(ktr + 1) * 512], op=ALU.mult), (s_vm, 1))
                            c_vm += 1
                            rdy[kt] = (s_vm, c_vm)
                        if ki >= 1:
                            emit_pv(kts[ki - 1], nslot=nslot)
                    emit_pv(kts[-1], nslot=nslot)
                    pending_norm = (h, qc, nslot, c_tp)
                    nslot = 1 - nslot
                tp_at_qc_end[qc] = c_tp
            flush_norm()
            P.emit(nc)

        with ExitStack() as st5:
            sb5 = lambda n, sh, dt=F32: st5.enter_context(nc.sbuf_tensor(pref + n, sh, dt))
            ps5 = lambda n, sh, dt=F32: st5.enter_context(nc.psum_tensor(pref + n, sh, dt))
            P = Plan()
            wo = [sb5(f"a1wo{i}", [128, 16 * 512], BF16) for i in range(2)]
            xo3 = [sb5(f"a1xo{i}", [128, 512]) for i in range(2)]
            xn3 = [sb5(f"a1xn{i}", [128, 512]) for i in range(2)]
            py = [ps5(f"a1py{i}", [128, 512]) for i in range(2)]
            s_wo = [sem("o_w0"), sem("o_w1")]
            s_xo = [sem("o_xoa"), sem("o_xob")]
            s_sy = [sem("o_sya"), sem("o_syb")]
            s_ty, s_vy = sem("o_ty"), sem("o_vy")
            c_wo = [0, 0]
            c_xo = [0, 0]
            c_sy = [0, 0]
            c_ty = c_vy = 0
            for hc in range(4):
                slw = hc % 2
                if hc >= 2:
                    P.wait("gpsimd", s_ty, (hc - 1) * 16)
                P.ins("gpsimd", lambda e, slw=slw, hc=hc: e.dma_start(
                    out=wo[slw][:, :], in_=d["wo1_r"][hc, :, :]), (s_wo[slw], 16))
                c_wo[slw] += 16
                for ts in range(NTS):
                    g = hc * 16 + ts
                    psl = g % 2
                    slx = g % 2
                    if g >= 2:
                        P.wait("sync", s_vy, g - 1)
                    P.ins("sync", lambda e, slx=slx, ts=ts, hc=hc: e.dma_start(
                        out=xo3[slx][:, :], in_=X_in[ts * 128:(ts + 1) * 128, hc * 512:(hc + 1) * 512]),
                        (s_xo[slx], 16))
                    c_xo[slx] += 16
                    P.wait("tensor", s_wo[slw], c_wo[slw])
                    if g >= 2:
                        P.wait("tensor", s_vy, g - 1)
                    for dt in range(16):
                        P.ins("tensor", lambda e, psl=psl, slw=slw, dt=dt, ts=ts: e.matmul(
                            py[psl][:, :], outQ[:, dt * 2048 + ts * 128: dt * 2048 + ts * 128 + 128],
                            wo[slw][:, dt * 512:(dt + 1) * 512],
                            start=(dt == 0), stop=(dt == 15)),
                            (s_ty, 1) if dt == 15 else None)
                    c_ty += 1
                    P.wait("vector", s_ty, c_ty)
                    P.wait("vector", s_xo[slx], c_xo[slx])
                    if g >= 2:
                        P.wait("vector", s_sy[slx], c_sy[slx])
                    P.ins("vector", lambda e, psl=psl, slx=slx: e.tensor_tensor(
                        xn3[slx][:, :], py[psl][:, :], xo3[slx][:, :], op=ALU.add), (s_vy, 1))
                    c_vy += 1
                    P.wait("sync", s_vy, c_vy)
                    P.ins("sync", lambda e, slx=slx, ts=ts, hc=hc: e.dma_start(
                        out=X_out[ts * 128:(ts + 1) * 128, hc * 512:(hc + 1) * 512], in_=xn3[slx][:, :]),
                        (s_sy[slx], 16))
                    c_sy[slx] += 16
            P.wait("sync", s_sy[0], c_sy[0])
            P.wait("sync", s_sy[1], c_sy[1])
            P.emit(nc)



def _tern_chunks(d):
    out = []
    for i in range(40):
        for c in range(4):
            out.append((0, i, c * 512, 512))
    for i in range(8):
        for c in range(20):
            out.append((1, i, c * 512, 512))
    return out


def plan_tern_absmean(nc, P, Pt, st, sp, d, thr_sb, c_col):
    sb = lambda n, sh, dt=F32: st.enter_context(nc.sbuf_tensor("tm_" + n, sh, dt))
    ps = lambda n, sh, dt=F32: st.enter_context(nc.psum_tensor("tm_" + n, sh, dt))
    sem = lambda n: sp.enter_context(nc.semaphore("tm_" + n))
    wt = [sb(f"wt{i}", [128, 512]) for i in range(2)]
    acc = sb("acc", [128, 320])
    red = sb("red", [128, 2])
    sml = sb("sml", [1, 8])
    pam = ps("pam", [1, 2])
    pbc2 = ps("pbc2", [128, 8])
    s_l, s_v, s_t, s_st = sem("sl"), sem("sv"), sem("st"), sem("sst")
    c_l = c_v = c_t = c_st = 0
    chunks = _tern_chunks(d)
    srcs = [d["wfc_r"], d["wpj_r"]]
    for ci, (m, i, c0, w) in enumerate(chunks):
        sl = ci % 2
        if ci >= 2:
            P.wait("gpsimd", s_v, ci - 1)
        P.ins("gpsimd", lambda e, sl=sl, m=m, i=i, c0=c0, w=w: e.dma_start(
            out=wt[sl][:, 0:w], in_=srcs[m][i, :, c0:c0 + w]), (s_l, 16)); c_l += 16
        P.wait("vector", s_l, c_l)
        P.ins("vector", lambda e, sl=sl, ci=ci, w=w: e.reduce_sum(
            acc[:, ci:ci + 1], wt[sl][:, 0:w], axis=AX.X,
            apply_absolute_value=True), (s_v, 1)); c_v += 1
    nfc_ch = 160
    P = Pt
    P.wait("vector", s_v, c_v)
    P.ins("vector", lambda e: e.reduce_sum(red[:, 0:1], acc[:, 0:nfc_ch], axis=AX.X), (s_v, 1)); c_v += 1
    P.ins("vector", lambda e: e.reduce_sum(red[:, 1:2], acc[:, nfc_ch:320], axis=AX.X), (s_v, 1)); c_v += 1
    P.wait("tensor", s_v, c_v)
    P.ins("tensor", lambda e: e.matmul(pam[:, 0:1], d["ones_col_f_sb"], red[:, 0:1],
                                       start=True, stop=True), (s_t, 1)); c_t += 1
    P.ins("tensor", lambda e: e.matmul(pam[:, 1:2], d["ones_col_f_sb"], red[:, 1:2],
                                       start=True, stop=True), (s_t, 1)); c_t += 1
    NEL = 5120 * 2048
    P.wait("vector", s_t, c_t)
    P.ins("vector", lambda e: e.tensor_scalar(
        sml[:, 0:1], pam[:, 0:1], 1.0 / NEL, 1e-5, op0=ALU.mult, op1=ALU.max), (s_v, 1)); c_v += 1
    P.ins("vector", lambda e: e.tensor_scalar(
        sml[:, 1:2], pam[:, 1:2], 1.0 / NEL, 1e-5, op0=ALU.mult, op1=ALU.max), (s_v, 1)); c_v += 1
    P.wait("vector", s_v, c_v)
    P.ins("vector", lambda e: e.tensor_scalar(
        sml[:, 2:3], sml[:, 0:1], 0.7, None, op0=ALU.mult), (s_v, 1)); c_v += 1
    P.ins("vector", lambda e: e.tensor_scalar(
        sml[:, 3:4], sml[:, 0:1], -0.7, None, op0=ALU.mult), (s_v, 1)); c_v += 1
    P.ins("vector", lambda e: e.tensor_scalar(
        sml[:, 4:5], sml[:, 1:2], 0.7, None, op0=ALU.mult), (s_v, 1)); c_v += 1
    P.ins("vector", lambda e: e.tensor_scalar(
        sml[:, 5:6], sml[:, 1:2], -0.7, None, op0=ALU.mult), (s_v, 1)); c_v += 1
    P.ins("vector", lambda e: e.tensor_tensor(
        sml[:, 6:7], sml[:, 0:1], sml[:, 0:1], op=ALU.mult), (s_v, 1)); c_v += 1
    P.wait("vector", s_v, c_v)
    P.ins("vector", lambda e: e.tensor_tensor(
        sml[:, 6:7], sml[:, 6:7], sml[:, 1:2], op=ALU.mult), (s_v, 1)); c_v += 1
    P.wait("vector", s_v, c_v)
    P.ins("vector", lambda e: e.tensor_scalar(
        sml[:, 6:7], sml[:, 6:7], 0.5, None, op0=ALU.mult), (s_v, 1)); c_v += 1
    P.wait("tensor", s_v, c_v)
    P.ins("tensor", lambda e: e.matmul(pbc2[:, :], d["ones_row_sb"], sml[:, :],
                                       start=True, stop=True), (s_t, 1)); c_t += 1
    P.wait("vector", s_t, c_t)
    P.ins("vector", lambda e: e.tensor_copy(thr_sb[:, :], pbc2[:, :]), (s_v, 1)); c_v += 1
    P.wait("sync", s_v, c_v)
    P.ins("sync", lambda e: e.dma_start(out=c_col[:, :], in_=thr_sb[:, 6:7],
                                        allow_slow_non_contiguous=True), (s_st, 16)); c_st += 16
    P.wait("sync", s_st, c_st)


def plan_tern_quant(nc, P, st, sp, d, thr_sb, qfc_c, qproj_c):
    sb = lambda n, sh, dt=F32: st.enter_context(nc.sbuf_tensor("tq_" + n, sh, dt))
    sem = lambda n: sp.enter_context(nc.semaphore("tq_" + n))
    wt = [sb(f"wt{i}", [128, 512]) for i in range(2)]
    tb = [sb(f"tb{i}", [128, 512]) for i in range(2)]
    qt = [sb(f"qt{i}", [128, 512], BF16) for i in range(2)]
    s_l, s_g, s_v, s_st = sem("sl"), sem("sg"), sem("sv"), sem("sst")
    c_l = c_g = c_v = c_st = 0
    chunks = _tern_chunks(d)
    srcs = [d["wfc_r"], d["wpj_r"]]
    dsts = [qfc_c, qproj_c]
    for ci, (m, i, c0, w) in enumerate(chunks):
        sl = ci % 2
        tcol, ncol = (2, 3) if m == 0 else (4, 5)
        if ci >= 2:
            P.wait("sync", s_v, ci - 1)
        P.ins("sync", lambda e, sl=sl, m=m, i=i, c0=c0, w=w: e.dma_start(
            out=wt[sl][:, :], in_=srcs[m][i, :, c0:c0 + w]), (s_l, 16)); c_l += 16
        P.wait("vector", s_l, c_l)
        if ci >= 2:
            P.wait("vector", s_st, 16 * (ci - 1))
        P.ins("vector", lambda e, sl=sl, ncol=ncol: e.tensor_scalar(
            tb[sl][:, :], wt[sl][:, :], thr_sb[:, ncol:ncol + 1], None,
            op0=ALU.is_lt), (s_g, 1)); c_g += 1
        P.wait("vector", s_g, c_g)
        P.ins("vector", lambda e, sl=sl, tcol=tcol: e.scalar_tensor_tensor(
            qt[sl][:, :], wt[sl][:, :], thr_sb[:, tcol:tcol + 1], tb[sl][:, :],
            op0=ALU.is_gt, op1=ALU.subtract), (s_v, 1)); c_v += 1
        P.wait("sync", s_v, c_v)
        P.ins("sync", lambda e, sl=sl, m=m, i=i, c0=c0, w=w: e.dma_start(
            out=dsts[m][i, :, c0:c0 + w], in_=qt[sl][:, :]), (s_st, 16)); c_st += 16
    P.wait("sync", s_st, c_st)



def build_tern(nc, d, sp, qfc_c, qproj_c, c_col):
    pref = "tn_"
    sem = lambda n: sp.enter_context(nc.semaphore(pref + n))
    NEL = 5120 * 2048
    fc_ch = [(0, i, 0) for i in range(40)]
    pj_ch = [(1, i, c * 2048) for i in range(8) for c in range(5)]
    zipped = []
    for i in range(40):
        zipped.append(("f", i))
        zipped.append(("p", i))

    with ExitStack() as stA:
        thr = stA.enter_context(nc.sbuf_tensor(pref + "thr", [128, 8], F32, side="right"))
        with ExitStack() as st1:
            sb1 = lambda n, sh, dt=F32: st1.enter_context(nc.sbuf_tensor(pref + n, sh, dt))
            ps1 = lambda n, sh, dt=F32: st1.enter_context(nc.psum_tensor(pref + n, sh, dt))
            P = Plan()
            NB = 4
            wtf = [sb1(f"wtf{i}", [128, 2048]) for i in range(NB)]
            wtp = [sb1(f"wtp{i}", [128, 2048]) for i in range(NB)]
            accf = sb1("accf", [128, 40])
            accp = sb1("accp", [128, 40])
            red = sb1("red", [128, 2])
            sml = sb1("sml", [1, 8])
            tb = sb1("tb", [128, 2048])
            qtf = [sb1(f"qtf{i}", [128, 2048], BF16) for i in range(NB)]
            qtp = [sb1(f"qtp{i}", [128, 2048], BF16) for i in range(NB)]
            pam = ps1("pam", [1, 2])
            pbc2 = ps1("pbc2", [128, 8])
            s_lf = [sem(f"slf{i}") for i in range(NB)]
            s_lp = [sem(f"slp{i}") for i in range(NB)]
            s_vf, s_vp, s_t, s_st = sem("svf"), sem("svp"), sem("stt"), sem("sst")
            s_sf = [sem(f"ssf{i}") for i in range(NB)]
            s_sp = [sem(f"ssp{i}") for i in range(NB)]
            s_v = sem("sv")
            c_lf = [0] * NB; c_lp = [0] * NB; c_sf = [0] * NB; c_sp = [0] * NB
            c_vf = c_vp = c_t = c_st = c_v = 0
            vf_done = {}
            vp_done = {}
            for kind, i in zipped:
                if kind == "f":
                    sl = i % NB
                    m, ti, c0 = fc_ch[i]
                    if i >= NB:
                        P.wait("sync", s_vf, vf_done[i - NB])
                    P.ins("sync", lambda e, sl=sl, ti=ti, c0=c0: e.dma_start(
                        out=wtf[sl][:, :], in_=d["wfc_r"][ti, :, c0:c0 + 2048]), (s_lf[sl], 16)); c_lf[sl] += 16
                    P.wait("vector", s_lf[sl], c_lf[sl])
                    P.ins("vector", lambda e, sl=sl, i=i: e.reduce_sum(
                        accf[:, i:i + 1], wtf[sl][:, :], axis=AX.X,
                        apply_absolute_value=True), (s_vf, 1)); c_vf += 1
                    vf_done[i] = c_vf
                else:
                    sl = i % NB
                    m, ti, c0 = pj_ch[i]
                    if i >= NB:
                        P.wait("gpsimd", s_vp, vp_done[i - NB])
                    P.ins("gpsimd", lambda e, sl=sl, ti=ti, c0=c0: e.dma_start(
                        out=wtp[sl][:, :], in_=d["wpj_r"][ti, :, c0:c0 + 2048]), (s_lp[sl], 16)); c_lp[sl] += 16
                    P.wait("vector", s_lp[sl], c_lp[sl])
                    P.ins("vector", lambda e, sl=sl, i=i: e.reduce_sum(
                        accp[:, i:i + 1], wtp[sl][:, :], axis=AX.X,
                        apply_absolute_value=True), (s_vp, 1)); c_vp += 1
                    vp_done[i] = c_vp
            P.wait("vector", s_vf, c_vf)
            P.wait("vector", s_vp, c_vp)
            P.ins("vector", lambda e: e.reduce_sum(red[:, 0:1], accf[:, :], axis=AX.X), (s_v, 1)); c_v += 1
            P.ins("vector", lambda e: e.reduce_sum(red[:, 1:2], accp[:, :], axis=AX.X), (s_v, 1)); c_v += 1
            P.wait("tensor", s_v, c_v)
            P.ins("tensor", lambda e: e.matmul(pam[:, 0:1], d["ones_col_f_sb"], red[:, 0:1],
                                               start=True, stop=True), (s_t, 1)); c_t += 1
            P.ins("tensor", lambda e: e.matmul(pam[:, 1:2], d["ones_col_f_sb"], red[:, 1:2],
                                               start=True, stop=True), (s_t, 1)); c_t += 1
            P.wait("vector", s_t, c_t)
            P.ins("vector", lambda e: e.tensor_scalar(
                sml[:, 0:1], pam[:, 0:1], 1.0 / NEL, 1e-5, op0=ALU.mult, op1=ALU.max), (s_v, 1)); c_v += 1
            P.ins("vector", lambda e: e.tensor_scalar(
                sml[:, 1:2], pam[:, 1:2], 1.0 / NEL, 1e-5, op0=ALU.mult, op1=ALU.max), (s_v, 1)); c_v += 1
            P.wait("vector", s_v, c_v)
            P.ins("vector", lambda e: e.tensor_scalar(
                sml[:, 2:3], sml[:, 0:1], 0.7, None, op0=ALU.mult), (s_v, 1)); c_v += 1
            P.ins("vector", lambda e: e.tensor_scalar(
                sml[:, 3:4], sml[:, 0:1], -0.7, None, op0=ALU.mult), (s_v, 1)); c_v += 1
            P.ins("vector", lambda e: e.tensor_scalar(
                sml[:, 4:5], sml[:, 1:2], 0.7, None, op0=ALU.mult), (s_v, 1)); c_v += 1
            P.ins("vector", lambda e: e.tensor_scalar(
                sml[:, 5:6], sml[:, 1:2], -0.7, None, op0=ALU.mult), (s_v, 1)); c_v += 1
            P.ins("vector", lambda e: e.tensor_tensor(
                sml[:, 6:7], sml[:, 0:1], sml[:, 0:1], op=ALU.mult), (s_v, 1)); c_v += 1
            P.wait("vector", s_v, c_v)
            P.ins("vector", lambda e: e.tensor_tensor(
                sml[:, 6:7], sml[:, 6:7], sml[:, 1:2], op=ALU.mult), (s_v, 1)); c_v += 1
            P.wait("vector", s_v, c_v)
            P.ins("vector", lambda e: e.tensor_scalar(
                sml[:, 6:7], sml[:, 6:7], 0.5, None, op0=ALU.mult), (s_v, 1)); c_v += 1
            P.wait("tensor", s_v, c_v)
            P.ins("tensor", lambda e: e.matmul(pbc2[:, :], d["ones_row_sb"], sml[:, :],
                                               start=True, stop=True), (s_t, 1)); c_t += 1
            P.wait("vector", s_t, c_t)
            P.ins("vector", lambda e: e.tensor_copy(thr[:, :], pbc2[:, :]), (s_v, 1)); c_v += 1
            P.wait("sync", s_v, c_v)
            P.ins("sync", lambda e: e.dma_start(out=c_col[:, :], in_=thr[:, 6:7],
                                                allow_slow_non_contiguous=True), (s_st, 16)); c_st += 16
            s_lf2 = [sem(f"lf2{i}") for i in range(NB)]
            s_lp2 = [sem(f"lp2{i}") for i in range(NB)]
            c_lf2 = [0] * NB; c_lp2 = [0] * NB
            s_v2f, s_v2p = sem("v2f"), sem("v2p")
            c_v2f = c_v2p = 0
            v2f_done = {}
            v2p_done = {}
            first_q = True
            for kind, i in zipped:
                if kind == "f":
                    sl = i % NB
                    m, ti, c0 = fc_ch[i]
                    if i >= NB:
                        P.wait("sync", s_v2f, v2f_done[i - NB])
                        P.wait("sync", s_sf[sl], c_sf[sl])
                    else:
                        P.wait("sync", s_vf, c_vf)
                    P.ins("sync", lambda e, sl=sl, ti=ti, c0=c0: e.dma_start(
                        out=wtf[sl][:, :], in_=d["wfc_r"][ti, :, c0:c0 + 2048]), (s_lf2[sl], 16)); c_lf2[sl] += 16
                    P.wait("vector", s_lf2[sl], c_lf2[sl])
                    if first_q:
                        P.wait("vector", s_v, c_v)
                        first_q = False
                    P.ins("vector", lambda e, sl=sl: e.tensor_scalar(
                        tb[:, :], wtf[sl][:, :], thr[:, 3:4], None, op0=ALU.is_lt), (s_v2f, 1)); c_v2f += 1
                    P.wait("vector", s_v2f, c_v2f)
                    P.ins("vector", lambda e, sl=sl: e.scalar_tensor_tensor(
                        qtf[sl][:, :], wtf[sl][:, :], thr[:, 2:3], tb[:, :],
                        op0=ALU.is_gt, op1=ALU.subtract), (s_v2f, 1)); c_v2f += 1
                    v2f_done[i] = c_v2f
                    P.wait("sync", s_v2f, c_v2f)
                    P.ins("sync", lambda e, sl=sl, ti=ti, c0=c0: e.dma_start(
                        out=qfc_c[ti, :, c0:c0 + 2048], in_=qtf[sl][:, :]), (s_sf[sl], 16)); c_sf[sl] += 16
                else:
                    sl = i % NB
                    m, ti, c0 = pj_ch[i]
                    if i >= NB:
                        P.wait("gpsimd", s_v2p, v2p_done[i - NB])
                        P.wait("gpsimd", s_sp[sl], c_sp[sl])
                    else:
                        P.wait("gpsimd", s_vp, c_vp)
                    P.ins("gpsimd", lambda e, sl=sl, ti=ti, c0=c0: e.dma_start(
                        out=wtp[sl][:, :], in_=d["wpj_r"][ti, :, c0:c0 + 2048]), (s_lp2[sl], 16)); c_lp2[sl] += 16
                    P.wait("vector", s_lp2[sl], c_lp2[sl])
                    if first_q:
                        P.wait("vector", s_v, c_v)
                        first_q = False
                    P.ins("vector", lambda e, sl=sl: e.tensor_scalar(
                        tb[:, :], wtp[sl][:, :], thr[:, 5:6], None, op0=ALU.is_lt), (s_v2p, 1)); c_v2p += 1
                    P.wait("vector", s_v2p, c_v2p)
                    P.ins("vector", lambda e, sl=sl: e.scalar_tensor_tensor(
                        qtp[sl][:, :], wtp[sl][:, :], thr[:, 4:5], tb[:, :],
                        op0=ALU.is_gt, op1=ALU.subtract), (s_v2p, 1)); c_v2p += 1
                    v2p_done[i] = c_v2p
                    P.wait("gpsimd", s_v2p, c_v2p)
                    P.ins("gpsimd", lambda e, sl=sl, ti=ti, c0=c0: e.dma_start(
                        out=qproj_c[ti, :, c0:c0 + 2048], in_=qtp[sl][:, :]), (s_sp[sl], 16)); c_sp[sl] += 16
            for i in range(NB):
                P.wait("sync", s_sf[i], c_sf[i])
                P.wait("gpsimd", s_sp[i], c_sp[i])
            P.wait("sync", s_st, c_st)
            P.emit(nc)



def _bf(x):
    return np.ascontiguousarray(x.astype(BF))


def prep_host(inputs):
    d = {}
    f32 = lambda x: np.ascontiguousarray(np.asarray(x, dtype=np.float32))

    gw = f32(inputs["L0_ffn_gate_w"])
    uw = f32(inputs["L0_ffn_up_w"])
    dw = f32(inputs["L0_ffn_down_w"])
    d["gw_r"] = np.ascontiguousarray(gw.reshape(44, 128, 16, 128).transpose(0, 3, 2, 1).reshape(44, 128, 2048))
    d["uw_r"] = np.ascontiguousarray(uw.reshape(44, 128, 16, 128).transpose(0, 3, 2, 1).reshape(44, 128, 2048))
    d["dw_r"] = np.ascontiguousarray(dw.T.reshape(44, 128, 8, 256).transpose(2, 1, 0, 3).reshape(8, 128, 44 * 256))
    d["gs128"] = np.ascontiguousarray(f32(inputs["L0_ffn_gate_s"]).reshape(44, 128).T)
    d["us128"] = np.ascontiguousarray(f32(inputs["L0_ffn_up_s"]).reshape(44, 128).T)
    d["dsT"] = _bf(np.tile(f32(inputs["L0_ffn_down_s"])[None, :], (128, 1)))

    wq = f32(inputs["L0_attn_Wq"])
    d["wq_r"] = np.ascontiguousarray(wq.reshape(16, 128, 16, 128).transpose(0, 3, 2, 1).reshape(16, 128, 2048))
    wkd = f32(inputs["L0_attn_Wk_down"])
    d["wkd_r"] = np.ascontiguousarray(wkd.T.reshape(16, 128, 256).transpose(1, 0, 2).reshape(128, 16 * 256))
    wku = f32(inputs["L0_attn_Wk_up"])
    d["wku_r"] = np.ascontiguousarray(wku.T.reshape(2, 128, 512).transpose(1, 0, 2).reshape(128, 1024))
    wvd = f32(inputs["L0_attn_Wv_down"])
    d["wvd_r"] = np.ascontiguousarray(wvd.T.reshape(16, 128, 384).transpose(1, 0, 2).reshape(128, 16 * 384))
    wvu = f32(inputs["L0_attn_Wv_up"])
    d["wvu_r"] = np.ascontiguousarray(wvu.T.reshape(3, 128, 512).transpose(1, 0, 2).reshape(128, 3 * 512))
    wo = f32(inputs["L0_attn_Wo"])
    d["wo_r"] = np.ascontiguousarray(wo.T.reshape(16, 128, 4, 512).transpose(2, 1, 0, 3).reshape(4, 128, 16 * 512))
    d["kanc"] = np.ascontiguousarray(f32(inputs["L0_attn_k_anchor"]).reshape(4, 128).T)
    d["vanc"] = np.ascontiguousarray(np.tile(f32(inputs["L0_attn_v_anchor"])[None, :], (128, 1)))

    inv = 1.0 / (10000.0 ** (np.arange(0, 64, dtype=np.float64) * 2.0 / 128.0))
    pos = np.arange(S, dtype=np.float64)
    ang = pos[None, :] * inv[:, None]
    cosT = np.concatenate([np.cos(ang), np.cos(ang)], axis=0).astype(np.float32)
    sinT = np.concatenate([np.sin(ang), -np.sin(ang)], axis=0).astype(np.float32)
    d["cosT"] = _bf(cosT)
    d["sinT"] = _bf(sinT)
    d["cosTM_r"] = np.ascontiguousarray(_bf(cosT.T).reshape(16, 128, 128))
    d["sinTM_r"] = np.ascontiguousarray(_bf(sinT.T).reshape(16, 128, 128))

    tri = (np.arange(128)[None, :] >= np.arange(128)[:, None])
    masks = np.zeros((4, 128, 512), dtype=np.float32)
    for j in range(4):
        masks[j, :, 128 * j:128 * (j + 1)] = tri
        masks[j, :, 128 * (j + 1):] = 1.0
    d["masks0"] = _bf(masks.reshape(4, 128, 512))

    m1 = np.zeros((12, 128, 512), dtype=np.float32)
    edge = (np.arange(128)[None, :] < np.arange(128)[:, None])
    for ktr in range(12):
        for jp in range(4):
            m = jp - ktr + 8
            if m == 0:
                m1[ktr, :, 128 * jp:128 * (jp + 1)] = tri
            elif m == 8:
                m1[ktr, :, 128 * jp:128 * (jp + 1)] = edge
            elif 0 < m < 8:
                m1[ktr, :, 128 * jp:128 * (jp + 1)] = 1.0
    d["masks1"] = _bf(m1)

    d["wqkv_r"] = np.ascontiguousarray(
        f32(inputs["L1_attn_Wqkv"]).reshape(16, 128, 12, 512).transpose(2, 1, 0, 3).reshape(12, 128, 16 * 512))
    d["wo1_r"] = np.ascontiguousarray(
        f32(inputs["L1_attn_Wo"]).reshape(16, 128, 4, 512).transpose(2, 1, 0, 3).reshape(4, 128, 16 * 512))
    d["qnw"] = _bf(np.tile(f32(inputs["L1_attn_q_norm_w"])[None, :], (128, 1)))
    d["knw"] = _bf(np.tile(f32(inputs["L1_attn_k_norm_w"])[None, :], (128, 1)))

    wfc = f32(inputs["L1_ffn_w_fc"])
    wpj = f32(inputs["L1_ffn_w_proj"])
    d["wfc_r"] = np.ascontiguousarray(wfc.reshape(40, 128, 16, 128).transpose(0, 3, 2, 1).reshape(40, 128, 2048))
    d["wpj_r"] = np.ascontiguousarray(wpj.T.reshape(40, 128, 8, 256).transpose(2, 1, 0, 3).reshape(8, 128, 40 * 256))

    d["identb"] = np.eye(128, dtype=BF)
    d["onescol"] = np.ones((128, 1), dtype=BF)
    d["onesrow"] = np.ones((1, 128), dtype=np.float32)
    return d



def _declare_common(nc, d, names):
    aps = {}
    for n in names:
        arr = d[n]
        dt = BF16 if arr.dtype == BF else F32
        aps[n] = nc.dram_tensor(n, list(arr.shape), dt, kind="ExternalInput").ap()
    return aps


_FUSED_CACHE = {}

ALL_W_NAMES = ["gw_r", "uw_r", "dw_r", "gs128", "us128", "dsT",
               "wq_r", "wkd_r", "wku_r", "wvd_r", "wvu_r", "wo_r",
               "kanc", "vanc", "cosT", "sinT", "masks0",
               "wfc_r", "wpj_r",
               "wqkv_r", "wo1_r", "qnw", "knw", "cosTM_r", "sinTM_r", "masks1",
               "identb"]


def _build_fused(d):
    nc = _mk_nc()
    aps = _declare_common(nc, d, ALL_W_NAMES)
    x_in = nc.dram_tensor("Xin", [S, HD], F32, kind="ExternalInput").ap()
    x_out = nc.dram_tensor("Xout", [S, HD], F32, kind="ExternalOutput").ap()
    XA = nc.dram_tensor("XA", [S, HD], F32).ap()
    XB = nc.dram_tensor("XB", [S, HD], F32).ap()
    dd = dict(aps)
    dd["gw_c"] = nc.dram_tensor("gw_c", [44, 128, 2048], BF16).ap()
    dd["uw_c"] = nc.dram_tensor("uw_c", [44, 128, 2048], BF16).ap()
    dd["dw_c"] = nc.dram_tensor("dw_c", [8, 128, 44 * 256], BF16).ap()
    dd["qfc_c"] = nc.dram_tensor("qfc_c", [40, 128, 2048], BF16).ap()
    dd["qproj_c"] = nc.dram_tensor("qproj_c", [8, 128, 40 * 256], BF16).ap()
    dd["c_col"] = nc.dram_tensor("c_col", [128, 1], F32).ap()
    qkv_s = nc.dram_tensor("qkv_s", [S, 6144], BF16).ap()
    qT_s = nc.dram_tensor("qT_s", [16, 128, 2048], BF16).ap()
    kT_s = nc.dram_tensor("kT_s", [16, 128, 2048], BF16).ap()
    _const_block(nc, d, aps, dd)

    def rst():
        nc.reset(
            previous_sbuf_base=nc.sbuf_base, previous_sbuf_top=nc.sbuf_top,
            previous_psum_base=nc.psum_base, previous_psum_top=nc.psum_top,
            previous_local_dram_base=nc.local_dram_base,
            previous_shared_dram_base=nc.shared_dram_base)

    build_ffn(nc, dd, ExitStack(), 0, 0, x_in, XA, have_cache=False)
    rst()
    build_attn0(nc, dd, ExitStack(), XA, XB)
    rst()
    build_ffn(nc, dd, ExitStack(), 0, 1, XB, XA, have_cache=True)
    rst()
    build_tern(nc, dd, ExitStack(), dd["qfc_c"], dd["qproj_c"], dd["c_col"])
    rst()
    build_ffn(nc, dd, ExitStack(), 1, 0, XA, XB, have_cache=True)
    rst()
    build_attn1(nc, dd, ExitStack(), XB, XA, qkv_s, qT_s, kT_s)
    rst()
    build_ffn(nc, dd, ExitStack(), 1, 1, XA, x_out, have_cache=True)
    return nc


def kernel(**inputs):
    d = prep_host(inputs)
    x_full = np.ascontiguousarray(np.asarray(inputs["L0_attn_hidden_states"], dtype=np.float32))
    B = x_full.shape[0]

    if "nc" not in _FUSED_CACHE:
        _FUSED_CACHE["nc"] = _build_fused(d)
    nc = _FUSED_CACHE["nc"]
    shared = {n: d[n] for n in ALL_W_NAMES}
    in_maps = []
    for b in range(B):
        m = dict(shared)
        m["Xin"] = np.ascontiguousarray(x_full[b])
        in_maps.append(m)
    res = run_bass_kernel_spmd(nc, in_maps, core_ids=list(range(B)))
    return np.stack([res.results[b]["Xout"] for b in range(B)], axis=0)


def _mk_nc():
    return bass.Bass("TRN2", target_bir_lowering=False)


def _const_block(nc, d, aps, dd):
    h = nc.sbuf_tensor("identb_sb", [128, 128], BF16).__enter__()
    cst = nc.sbuf_tensor("cst_sb", [128, 4], F32).__enter__()
    orow = nc.sbuf_tensor("onesrow_sb", [1, 128], F32).__enter__()
    ocol = nc.sbuf_tensor("onescol_sb", [128, 1], BF16).__enter__()
    ocolf = nc.sbuf_tensor("onescolf_sb", [128, 1], F32).__enter__()
    with nc.semaphore("cst_s") as s0, nc.Block() as block:
        @block.sync
        def _(e):
            e.dma_start(out=h[:, :], in_=aps["identb"][:, :]).then_inc(s0, 16)
        @block.vector
        def _(e):
            e.memset(cst[:, 0:1], EPS_LN)
            e.memset(cst[:, 1:2], EPS_RMS)
            e.memset(orow[:, :], 1.0)
            e.memset(ocol[:, :], 1.0)
            e.memset(ocolf[:, :], 1.0).then_inc(s0, 1)
            e.wait_ge(s0, 17)
    dd["identb_sb"] = h
    dd["eps_ln_col"] = cst[:, 0:1]
    dd["eps_rms_col"] = cst[:, 1:2]
    dd["ones_row_sb"] = orow[:, :]
    dd["ones_col_sb"] = ocol[:, :]
    dd["ones_col_f_sb"] = ocolf[:, :]


def _run_chain(d, xs, apps=("f0", "a0", "f0", "tern", "f1", "a1", "f1")):
    B = len(xs)
    cores = list(range(B))
    progs = {}

    def make_ffn0():
        nc = _mk_nc()
        names = ["gw_r", "uw_r", "dw_r", "gs128", "us128", "dsT", "identb"]
        aps = _declare_common(nc, d, names)
        X_in = nc.dram_tensor("Xin", [S, HD], F32, kind="ExternalInput").ap()
        X_out = nc.dram_tensor("Xout", [S, HD], F32, kind="ExternalOutput").ap()
        dd = dict(aps)
        dd["gw_c"] = nc.dram_tensor("gw_c", [44, 128, 2048], BF16).ap()
        dd["uw_c"] = nc.dram_tensor("uw_c", [44, 128, 2048], BF16).ap()
        dd["dw_c"] = nc.dram_tensor("dw_c", [8, 128, 44 * 256], BF16).ap()
        _const_block(nc, d, aps, dd)
        sp = ExitStack()
        build_ffn(nc, dd, sp, layer=0, app_idx=0, X_in=X_in, X_out=X_out, have_cache=False)
        return nc, names

    def make_attn0():
        nc = _mk_nc()
        names = ["wq_r", "wkd_r", "wku_r", "wvd_r", "wvu_r", "wo_r",
                 "kanc", "vanc", "cosT", "sinT", "masks0", "identb"]
        aps = _declare_common(nc, d, names)
        X_in = nc.dram_tensor("Xin", [S, HD], F32, kind="ExternalInput").ap()
        X_out = nc.dram_tensor("Xout", [S, HD], F32, kind="ExternalOutput").ap()
        dd = dict(aps)
        _const_block(nc, d, aps, dd)
        sp = ExitStack()
        build_attn0(nc, dd, sp, X_in, X_out)
        return nc, names

    def make_tern():
        nc = _mk_nc()
        names = ["wfc_r", "wpj_r", "identb"]
        aps = _declare_common(nc, d, names)
        qfc_c = nc.dram_tensor("qfc_c", [40, 128, 2048], BF16, kind="ExternalOutput").ap()
        qproj_c = nc.dram_tensor("qproj_c", [8, 128, 40 * 256], BF16, kind="ExternalOutput").ap()
        c_col = nc.dram_tensor("c_col", [128, 1], F32, kind="ExternalOutput").ap()
        dd = dict(aps)
        _const_block(nc, d, aps, dd)
        sp = ExitStack()
        build_tern(nc, dd, sp, qfc_c, qproj_c, c_col)
        return nc, names

    def make_ffn1():
        nc = _mk_nc()
        names = ["identb"]
        aps = _declare_common(nc, d, names)
        X_in = nc.dram_tensor("Xin", [S, HD], F32, kind="ExternalInput").ap()
        X_out = nc.dram_tensor("Xout", [S, HD], F32, kind="ExternalOutput").ap()
        dd = dict(aps)
        dd["qfc_c"] = nc.dram_tensor("qfc_c", [40, 128, 2048], BF16, kind="ExternalInput").ap()
        dd["qproj_c"] = nc.dram_tensor("qproj_c", [8, 128, 40 * 256], BF16, kind="ExternalInput").ap()
        dd["c_col"] = nc.dram_tensor("c_col", [128, 1], F32, kind="ExternalInput").ap()
        _const_block(nc, d, aps, dd)
        sp = ExitStack()
        build_ffn(nc, dd, sp, layer=1, app_idx=0, X_in=X_in, X_out=X_out, have_cache=True)
        return nc, names

    def make_attn1():
        nc = _mk_nc()
        names = ["wqkv_r", "wo1_r", "qnw", "knw", "cosTM_r", "sinTM_r", "masks1", "identb"]
        aps = _declare_common(nc, d, names)
        X_in = nc.dram_tensor("Xin", [S, HD], F32, kind="ExternalInput").ap()
        X_out = nc.dram_tensor("Xout", [S, HD], F32, kind="ExternalOutput").ap()
        dd = dict(aps)
        qkv_s = nc.dram_tensor("qkv_s", [S, 6144], BF16).ap()
        qT_s = nc.dram_tensor("qT_s", [16, 128, 2048], BF16).ap()
        kT_s = nc.dram_tensor("kT_s", [16, 128, 2048], BF16).ap()
        _const_block(nc, d, aps, dd)
        sp = ExitStack()
        build_attn1(nc, dd, sp, X_in, X_out, qkv_s, qT_s, kT_s)
        return nc, names

    MAKERS = {"f0": make_ffn0, "a0": make_attn0, "tern": make_tern,
              "f1": make_ffn1, "a1": make_attn1}
    tern_out = {}

    def run_app(key):
        nonlocal xs
        if key not in progs:
            progs[key] = MAKERS[key]()
        nc, names = progs[key]
        shared = {n: d[n] for n in names}
        if key == "tern":
            in_maps = [dict(shared) for _ in range(B)]
            res = run_bass_kernel_spmd(nc, in_maps, core_ids=cores)
            for k in ("qfc_c", "qproj_c", "c_col"):
                tern_out[k] = res.results[0][k]
            return
        if key == "f1":
            shared.update(tern_out)
        in_maps = []
        for b in range(B):
            m = dict(shared)
            m["Xin"] = np.ascontiguousarray(xs[b])
            in_maps.append(m)
        res = run_bass_kernel_spmd(nc, in_maps, core_ids=cores)
        xs = [res.results[b]["Xout"] for b in range(B)]

    for a in apps:
        if a in MAKERS:
            run_app(a)
    return xs
